# version 02
# baseline (speedup 1.0000x reference)
import contextlib
import numpy as np
import ml_dtypes
import concourse.bass as bass
import concourse.mybir as mybir
from concourse.bass_utils import run_bass_kernel_spmd

F32 = mybir.dt.float32
BF16 = mybir.dt.bfloat16
AF = mybir.ActivationFunctionType
ALU = mybir.AluOpType
AX = mybir.AxisListType
NPBF = ml_dtypes.bfloat16

ENGINES = ("sync", "gpsimd", "scalar", "vector", "tensor")
PAD = 4
T = 4096
NT = 8
TTW = 512
DC = 8
INF = 1 << 40


class Sched:
    def __init__(self):
        self.steps = []
        self.counts = {e: 0 for e in ENGINES}
        self.writers = {}
        self.readers = {}

    @staticmethod
    def _iv(reg, buf=None):
        if reg is None:
            return (0, INF)
        lo, hi = int(reg[0]), int(reg[1])
        if buf == "PS":
            lo = (lo // 512) * 512
            hi = ((hi + 511) // 512) * 512
        return (lo, hi)

    def add(self, engine, emit, reads=(), writes=(), weight=None, tag=""):
        idx = len(self.steps)
        if weight is None:
            weight = 16 if engine == "sync" else 1
        deps = set()
        for (buf, reg) in reads:
            lo, hi = self._iv(reg, buf)
            for (l, h, i) in self.writers.get(buf, ()):
                if l < hi and lo < h:
                    deps.add(i)
        for (buf, reg) in writes:
            lo, hi = self._iv(reg, buf)
            for (l, h, i) in self.writers.get(buf, ()):
                if l < hi and lo < h:
                    deps.add(i)
            for (l, h, i) in self.readers.get(buf, ()):
                if l < hi and lo < h:
                    deps.add(i)
        for (buf, reg) in reads:
            lo, hi = self._iv(reg, buf)
            self.readers.setdefault(buf, []).append((lo, hi, idx))
        for (buf, reg) in writes:
            lo, hi = self._iv(reg, buf)
            wl = self.writers.setdefault(buf, [])
            wl[:] = [e for e in wl if not (lo <= e[0] and e[1] <= hi)]
            wl.append((lo, hi, idx))
            rl = self.readers.setdefault(buf, [])
            rl[:] = [e for e in rl if not (lo <= e[0] and e[1] <= hi)]
        deps.discard(idx)
        self.counts[engine] += weight
        self.steps.append(dict(idx=idx, engine=engine, emit=emit, weight=weight,
                               cum=self.counts[engine], deps=sorted(deps),
                               tag=tag, dma=(weight >= 16)))
        return idx

    def emit(self, block, sems):
        eng_steps = {e: [s for s in self.steps if s["engine"] == e] for e in ENGINES}
        totals = dict(self.counts)
        steps = self.steps

        unsafe_self = ("vector", "scalar", "gpsimd")

        def make(ename):
            def body(eng):
                waited = {}
                for s in eng_steps[ename]:
                    want = {}
                    for d in s["deps"]:
                        de = steps[d]["engine"]
                        if de == ename and ename not in unsafe_self:
                            continue
                        v = steps[d]["cum"]
                        if want.get(de, 0) < v:
                            want[de] = v
                    if s["dma"]:
                        prev = s["cum"] - s["weight"]
                        if prev > 0:
                            want[ename] = max(want.get(ename, 0), prev)
                    for de in ENGINES:
                        v = want.get(de, 0)
                        if v and waited.get(de, 0) < v:
                            eng.wait_ge(sems[de], v)
                            waited[de] = v
                    ins = s["emit"](eng)
                    ins.then_inc(sems[ename], s["weight"])
                if ename == "sync":
                    for e in ENGINES:
                        if totals[e] > 0 and waited.get(e, 0) < totals[e]:
                            eng.wait_ge(sems[e], totals[e])
                elif ename == "gpsimd" and totals[ename] > 0:
                    if waited.get(ename, 0) < totals[ename]:
                        eng.wait_ge(sems[ename], totals[ename])
            return body

        for e in ENGINES:
            getattr(block, e)(make(e))


class Ctx:
    def __init__(self, nc, stack):
        self.nc = nc
        self.stack = stack
        self.S = Sched()
        self.inputs = []
        self.sb = {}
        self.fmap = {}
        self.fvals = []

    FC = 320

    def fconst(self, v):
        v = float(v)
        if v not in self.fmap:
            i = len(self.fvals)
            assert i < 64
            self.fvals.append(v)
            self.fmap[v] = i
            self.nc.const_aps.register(F32, v, self.CONST[:, self.FC + i:self.FC + i + 1])
        return ("CONST", (self.FC, self.FC + 64))

    def sbuf(self, name, shape, dtype):
        t = self.stack.enter_context(self.nc.sbuf_tensor(name, shape, dtype))
        self.sb[name] = t
        return t

    def psum(self, name, shape, dtype):
        return self.stack.enter_context(self.nc.psum_tensor(name, shape, dtype))

    def din(self, name, shape, dtype):
        self.inputs.append(name)
        return self.nc.dram_tensor(name, list(shape), dtype, kind="ExternalInput").ap()

    def dout(self, name, shape, dtype):
        return self.nc.dram_tensor(name, list(shape), dtype, kind="ExternalOutput").ap()

    def dscratch(self, name, shape, dtype):
        return self.nc.dram_tensor(name, list(shape), dtype, kind="Internal").ap()

    debug = False

    def tap(self, name, ap, reads):
        if not self.debug:
            return
        shape = [int(x) for x in ap.shape]
        out = self.dout("tap_" + name, shape, ap.dtype)
        self.S.add("sync", lambda e: e.dma_start(out=out, in_=ap),
                   reads=list(reads), tag="tap_" + name)



def _mla_masks():
    a = np.zeros((128, 4 * 512), dtype=np.float32)
    ti = np.arange(128)[:, None]
    sl = np.arange(512)[None, :]
    for v, off in enumerate((0, 128, 256, 384)):
        a[:, v * 512:(v + 1) * 512] = np.where(sl <= ti + off, 0.0, -1e30)
    return a.astype(NPBF)


def _half2():
    a = np.zeros((128, 2), dtype=NPBF)
    a[0:64, 0] = 1
    a[64:128, 1] = 1
    return a


def _bc2():
    a = np.zeros((2, 128), dtype=NPBF)
    a[0, 0:64] = 1
    a[1, 64:128] = 1
    return a


def _l0f_masks():
    a = np.zeros((128, 8 * 512), dtype=np.float32)
    qi = np.arange(128)[:, None]
    kj = np.arange(512)[None, :]
    for qs in range(4):
        qw = qs * 128 + qi
        a[:, qs * 512:(qs + 1) * 512] = np.where(kj >= qw, 0.0, -1e30)
        a[:, (4 + qs) * 512:(5 + qs) * 512] = np.where(kj <= qw, 0.0, -1e30)
    return a.astype(NPBF)


_L0F_ROT_CACHE = []


def _l0f_rot():
    if _L0F_ROT_CACHE:
        return _L0F_ROT_CACHE[0]
    DH, W = 64, 512
    half = np.arange(0, DH, 2, dtype=np.float64)
    inv_freq = 1.0 / 10000.0 ** (half / DH)
    t = np.arange(T, dtype=np.float64)
    fr = t[:, None] * inv_freq[None, :]
    fr = np.concatenate([fr, fr], axis=1)
    base = (half + 0.4 * DH) / (1.4 * DH)
    power = (t - T // 2) / float(W // 2)
    sc = base[None, :] ** power[:, None]
    sc = np.concatenate([sc, sc], axis=1)
    cos, sin = np.cos(fr), np.sin(fr)
    def two(x):
        return np.ascontiguousarray(np.tile(x.T.astype(np.float32), (2, 1))).astype(NPBF)
    out = (two(cos * sc * 8.0), two(sin * sc * 8.0), two(cos / sc), two(sin / sc))
    _L0F_ROT_CACHE.append(out)
    return out


def _sel16():
    a = np.zeros((16, 16 * 128), dtype=NPBF)
    for r in range(16):
        a[r, r * 128:(r + 1) * 128] = 1
    return a


def _chunk16(v):
    w = v.shape[1]
    return np.ascontiguousarray(v.reshape(16, 128, w).transpose(1, 0, 2).reshape(128, 16 * w))


def _l3a_layer_preps():
    d = {}
    for i in range(2):
        d[f"L3A_inwT{i}"] = (lambda ins, i=i: np.ascontiguousarray(ins["L3_attn_in_proj_w"][i].T))
        d[f"L3A_cw{i}"] = (lambda ins, i=i: _chunk16(ins["L3_attn_conv_w"][i]))
        d[f"L3A_cb{i}"] = (lambda ins, i=i: _colchunks(ins["L3_attn_conv_b"][i]))
        d[f"L3A_xpwT{i}"] = (lambda ins, i=i: np.ascontiguousarray(ins["L3_attn_x_proj_w"][i].T))
        d[f"L3A_dtwT{i}"] = (lambda ins, i=i: np.ascontiguousarray(ins["L3_attn_dt_proj_w"][i].T))
        d[f"L3A_dtb{i}"] = (lambda ins, i=i: _colchunks(ins["L3_attn_dt_proj_b"][i]))
        d[f"L3A_alog{i}"] = (lambda ins, i=i: _chunk16(ins["L3_attn_A_log"][i]))
        d[f"L3A_D{i}"] = (lambda ins, i=i: _colchunks(ins["L3_attn_D"][i]))
        d[f"L3A_owT{i}"] = (lambda ins, i=i: np.ascontiguousarray(ins["L3_attn_out_proj_w"][i].T))
        d[f"L3A_rln{i}"] = (lambda ins, i=i: np.concatenate(
            [_colchunks(ins["L3_attn_rms_w"][i]), _colchunks(ins["L3_attn_ln_w"][i]),
             _colchunks(ins["L3_attn_ln_b"][i])], axis=1))
    return d


def _expbc():
    a = np.zeros((16, 16 * 128), dtype=NPBF)
    for e in range(16):
        a[e, e * 128:(e + 1) * 128] = 1
    return a


def _colchunks(v):
    F = v.shape[0]
    return np.ascontiguousarray(v.reshape(F // 128, 128).T)


PREP = {
    "c_ones_col_bf": lambda ins: np.ones((128, 1), dtype=NPBF),
    "c_ones_row_f": lambda ins: np.ones((1, 128), dtype=np.float32),
    "c_ones_col_f": lambda ins: np.ones((128, 1), dtype=np.float32),
    "c_ident_bf": lambda ins: np.eye(128, dtype=NPBF),
    "c_ident_f": lambda ins: np.eye(128, dtype=np.float32),
    "c_ones_row_bf": lambda ins: np.ones((1, 128), dtype=NPBF),
    "c_expbc": lambda ins: _expbc(),
    "L1f_W1T": lambda ins: np.ascontiguousarray(ins["L1_ffn_W1"].T),
    "L1f_W2T": lambda ins: np.ascontiguousarray(ins["L1_ffn_W2"].T),
    "L1f_b1c": lambda ins: _colchunks(ins["L1_ffn_b1"]),
    "L1f_b2c": lambda ins: _colchunks(ins["L1_ffn_b2"]),
    "L2f_K": lambda ins: np.ascontiguousarray(ins["L2_ffn_kernel"]),
    "L2f_a": lambda ins: np.ascontiguousarray(ins["L2_ffn_a"]),
    "L2f_b": lambda ins: np.ascontiguousarray(ins["L2_ffn_b"]),
    "L2f_bias": lambda ins: _colchunks(ins["L2_ffn_bias"]),
    "L3f_selT": lambda ins: np.ascontiguousarray(ins["L3_ffn_expert_sel"].T),
    "L3f_keys": lambda ins: np.ascontiguousarray(
        ins["L3_ffn_keys"].transpose(1, 0, 2).reshape(1024, 4096)),
    "L3f_vals": lambda ins: np.ascontiguousarray(
        ins["L3_ffn_values"].reshape(4096, 1024)),
    "L3f_rs": lambda ins: np.ascontiguousarray(
        ins["L3_ffn_route_scale"].reshape(1, 1)),
    "L0a_Wqkv": lambda ins: np.ascontiguousarray(ins["L0_attn_Wqkv"]),
    "L0a_bqkv": lambda ins: _colchunks(ins["L0_attn_bqkv"]),
    "L0a_Wout": lambda ins: np.ascontiguousarray(ins["L0_attn_Wout"]),
    "L0a_bout": lambda ins: _colchunks(ins["L0_attn_bout"]),
    "c_half2": lambda ins: _half2(),
    "c_bc2": lambda ins: _bc2(),
    "c_l0f_masks": lambda ins: _l0f_masks(),
    "c_l0f_csq": lambda ins: _l0f_rot()[0],
    "c_l0f_ssq": lambda ins: _l0f_rot()[1],
    "c_l0f_csk": lambda ins: _l0f_rot()[2],
    "c_l0f_ssk": lambda ins: _l0f_rot()[3],
    "c_sel16": lambda ins: _sel16(),
    "L3A_fw1T": lambda ins: np.ascontiguousarray(ins["L3_attn_ff_w1"].T),
    "L3A_fw2T": lambda ins: np.ascontiguousarray(ins["L3_attn_ff_w2"].T),
    "L3A_fb1": lambda ins: _colchunks(ins["L3_attn_ff_b1"]),
    "L3A_fb2": lambda ins: _colchunks(ins["L3_attn_ff_b2"]),
    "c_mla_masks": lambda ins: _mla_masks(),
    "L1a_WdqT": lambda ins: np.ascontiguousarray(ins["L1_attn_W_dq"].T),
    "L1a_Wdq": lambda ins: np.ascontiguousarray(ins["L1_attn_W_dq"]),
    "L1a_Wuq": lambda ins: np.ascontiguousarray(ins["L1_attn_W_uq"]),
    "L1a_Wuk": lambda ins: np.ascontiguousarray(ins["L1_attn_W_uk"]),
    "L1a_WdkvT": lambda ins: np.ascontiguousarray(ins["L1_attn_W_dkv"].T),
    "L1a_Wuv": lambda ins: np.ascontiguousarray(ins["L1_attn_W_uv"]),
    "L1a_WoT": lambda ins: np.ascontiguousarray(ins["L1_attn_W_o"].T),
    "L2a_WdqT": lambda ins: np.ascontiguousarray(ins["L2_attn_W_dq"].T),
    "L2a_Wdq": lambda ins: np.ascontiguousarray(ins["L2_attn_W_dq"]),
    "L2a_Wuq": lambda ins: np.ascontiguousarray(ins["L2_attn_W_uq"]),
    "L2a_Wuk": lambda ins: np.ascontiguousarray(ins["L2_attn_W_uk"]),
    "L2a_WdkvT": lambda ins: np.ascontiguousarray(ins["L2_attn_W_dkv"].T),
    "L2a_Wuv": lambda ins: np.ascontiguousarray(ins["L2_attn_W_uv"]),
    "L2a_WoT": lambda ins: np.ascontiguousarray(ins["L2_attn_W_o"].T),
}
PREP.update(_l3a_layer_preps())


def _l0f_layer_preps():
    d = {}
    for l in range(2):
        d[f"L0f_wqkv{l}"] = (lambda ins, l=l: np.ascontiguousarray(ins["L0_ffn_wqkv"][l]))
        d[f"L0f_wo{l}"] = (lambda ins, l=l: np.ascontiguousarray(ins["L0_ffn_wo"][l]))
        d[f"L0f_w1a{l}"] = (lambda ins, l=l: np.ascontiguousarray(ins["L0_ffn_w1"][l][:, :2730]))
        d[f"L0f_w1g{l}"] = (lambda ins, l=l: np.ascontiguousarray(ins["L0_ffn_w1"][l][:, 2730:]))
        d[f"L0f_w2{l}"] = (lambda ins, l=l: np.ascontiguousarray(ins["L0_ffn_w2"][l]))
        d[f"L0f_qs{l}"] = (lambda ins, l=l: np.ascontiguousarray(
            np.tile(ins["L0_ffn_q_scale"][l], 2).reshape(128, 1)))
        d[f"L0f_ks{l}"] = (lambda ins, l=l: np.ascontiguousarray(
            np.tile(ins["L0_ffn_k_scale"][l], 2).reshape(128, 1)))
        d[f"L0f_lg{l}"] = (lambda ins, l=l: _colchunks(ins["L0_ffn_ln_g"][l]))
        d[f"L0f_lb{l}"] = (lambda ins, l=l: _colchunks(ins["L0_ffn_ln_b"][l]))
        d[f"L0f_b1a{l}"] = (lambda ins, l=l: _colchunks(
            np.concatenate([ins["L0_ffn_b1"][l][:2730],
                            np.zeros(86, np.float32)]).astype(np.float32)))
        d[f"L0f_b1g{l}"] = (lambda ins, l=l: _colchunks(
            np.concatenate([ins["L0_ffn_b1"][l][2730:],
                            np.zeros(86, np.float32)]).astype(np.float32)))
        d[f"L0f_b2{l}"] = (lambda ins, l=l: _colchunks(ins["L0_ffn_b2"][l]))
    return d


PREP.update(_l0f_layer_preps())



def bank(C, b, width=512):
    return C.PS[:, b * 512: b * 512 + width]


def setup_common(C, first):
    S = C.S
    C.xres = [C.sbuf(f"xres{k}", [128, PAD + T], BF16) for k in range(DC)]
    C.PS = C.psum("PS", [128, 8 * 512], F32)
    C.CONST = C.sbuf("CONST", [128, 384], F32)
    Ctx.CONST = property(lambda self: self.sb["CONST"]) if False else None
    C.CONST = C.sb["CONST"]
    C.CONSTB = C.sbuf("CONSTB", [128, 512], BF16)
    C.lnb = C.sbuf("lnb", [128, 2 * TTW], BF16)
    C.lnsm = C.sbuf("lnsm", [128, 3 * TTW], F32)
    C.ones_row_f = C.CONST[0:1, 0:128]
    C.ones_col_f = C.CONST[:, 128:129]
    C.ident_f = C.CONST[:, 129:257]
    C.ones_col_bf = C.CONSTB[:, 0:1]
    C.ident_bf = C.CONSTB[:, 128:256]
    C.ones_row_bf = C.CONSTB[0:1, 256:384]

    d_or = C.din("c_ones_row_f", [1, 128], F32)
    d_ocf = C.din("c_ones_col_f", [128, 1], F32)
    d_oc = C.din("c_ones_col_bf", [128, 1], BF16)
    d_id = C.din("c_ident_bf", [128, 128], BF16)
    d_idf = C.din("c_ident_f", [128, 128], F32)
    d_fv = C.din("c_fvals", [128, 64], F32)
    d_orb = C.din("c_ones_row_bf", [1, 128], BF16)
    S.add("sync", lambda e: e.dma_start(out=C.CONST[0:1, 0:128], in_=d_or),
          writes=[("CONST", (0, 128))], tag="ldc1")
    S.add("sync", lambda e: e.dma_start(out=C.CONST[:, 128:129], in_=d_ocf),
          writes=[("CONST", (128, 129))], tag="ldc2")
    S.add("sync", lambda e: e.dma_start(out=C.CONSTB[:, 0:1], in_=d_oc),
          writes=[("CONSTB", (0, 1))], tag="ldc3")
    S.add("sync", lambda e: e.dma_start(out=C.CONSTB[:, 128:256], in_=d_id),
          writes=[("CONSTB", (128, 256))], tag="ldc4")
    S.add("sync", lambda e: e.dma_start(out=C.CONST[:, 129:257], in_=d_idf),
          writes=[("CONST", (129, 257))], tag="ldc5")
    S.add("sync", lambda e: e.dma_start(out=C.CONST[:, Ctx.FC:Ctx.FC + 64], in_=d_fv),
          writes=[("CONST", (Ctx.FC, Ctx.FC + 64))], tag="ldc6")
    S.add("sync", lambda e: e.dma_start(out=C.CONSTB[0:1, 256:384], in_=d_orb),
          writes=[("CONSTB", (256, 384))], tag="ldc7")

    for k in range(DC):
        S.add("vector", lambda e, k=k: e.memset(C.xres[k][:, 0:PAD], 0.0),
              writes=[(f"xres{k}", (0, PAD))], tag="padz")

    xin = C.din("x_in", [1024, T], F32 if first else BF16)
    eng = "gpsimd" if first else "sync"
    for k in range(DC):
        S.add(eng,
              lambda e, k=k: e.dma_start(out=C.xres[k][:, PAD:PAD + T],
                                         in_=xin[k * 128:(k + 1) * 128, :]),
              writes=[(f"xres{k}", (PAD, PAD + T))], weight=16, tag="ldx")


def finish_common(C, last):
    S = C.S
    xout = C.dout("x_out", [1024, T], F32 if last else BF16)
    eng = "gpsimd" if last else "sync"
    for k in range(DC):
        S.add(eng,
              lambda e, k=k: e.dma_start(out=xout[k * 128:(k + 1) * 128, :],
                                         in_=C.xres[k][:, PAD:PAD + T]),
              reads=[(f"xres{k}", (PAD, PAD + T))], weight=16, tag="stx")


def emit_ln_core(C, src, out_fn, gamma=None, beta=None, rms=False, gb_reads=(), tag=""):
    S = C.S
    ts = TTW
    lnb, sm = C.lnb, C.lnsm
    stat1 = C.PS[0:1, 6 * 512: 7 * 512]
    stat2 = C.PS[0:1, 7 * 512: 8 * 512]
    fk1 = C.fconst(1e-5)
    fk2 = C.fconst(-0.5)
    for k in range(DC):
        ap, rds = src(k)
        if not rms:
            S.add("vector", lambda e, ap=ap: e.tensor_copy(lnb[:, 0:ts], ap),
                  reads=rds, writes=[("lnb", (0, ts))], tag=f"{tag}lncast{k}")
        S.add("scalar", lambda e, ap=ap: e.activation(lnb[:, ts:2 * ts], ap, AF.Square),
              reads=rds, writes=[("lnb", (ts, 2 * ts))], tag=f"{tag}lnsq{k}")
        if not rms:
            S.add("tensor", lambda e, k=k: e.matmul(stat1, C.ones_col_bf, lnb[:, 0:ts],
                                                    start=(k == 0), stop=(k == DC - 1)),
                  reads=[("lnb", (0, ts)), ("CONSTB", (0, 1))],
                  writes=[("PS", (6 * 512, 7 * 512))], tag=f"{tag}lnmm1_{k}")
        S.add("tensor", lambda e, k=k: e.matmul(stat2, C.ones_col_bf, lnb[:, ts:2 * ts],
                                                start=(k == 0), stop=(k == DC - 1)),
              reads=[("lnb", (ts, 2 * ts)), ("CONSTB", (0, 1))],
              writes=[("PS", (7 * 512, 8 * 512))], tag=f"{tag}lnmm2_{k}")
    mean = sm[0:1, 0:ts]
    rstd = sm[0:1, ts:2 * ts]
    msq = sm[0:1, 2 * ts:3 * ts]
    if rms:
        S.add("vector", lambda e: e.tensor_scalar_mul(rstd, stat2, 1.0 / 1024.0),
              reads=[("PS", (7 * 512, 8 * 512))], writes=[("lnsm", (ts, 2 * ts))],
              tag=tag + "lnms")
    else:
        S.add("vector", lambda e: e.tensor_scalar_mul(mean, stat1, 1.0 / 1024.0),
              reads=[("PS", (6 * 512, 7 * 512))], writes=[("lnsm", (0, ts))], tag=tag + "lnmean")
        S.add("scalar", lambda e: e.activation(msq, mean, AF.Square),
              reads=[("lnsm", (0, ts))], writes=[("lnsm", (2 * ts, 3 * ts))], tag=tag + "lnmsq")
        S.add("vector", lambda e: e.scalar_tensor_tensor(rstd, stat2, 1.0 / 1024.0, msq,
                                                         ALU.mult, ALU.subtract),
              reads=[("PS", (7 * 512, 8 * 512)), ("lnsm", (2 * ts, 3 * ts))],
              writes=[("lnsm", (ts, 2 * ts))], tag=tag + "lnvar")
    S.add("scalar", lambda e: e.activation(rstd, rstd, AF.Ln, bias=1e-5),
          reads=[("lnsm", (ts, 2 * ts)), fk1], writes=[("lnsm", (ts, 2 * ts))], tag=tag + "lnln")
    S.add("scalar", lambda e: e.activation(rstd, rstd, AF.Exp, scale=-0.5),
          reads=[("lnsm", (ts, 2 * ts)), fk2], writes=[("lnsm", (ts, 2 * ts))], tag=tag + "lnexp")
    if not rms:
        S.add("tensor", lambda e: e.matmul(bank(C, 4), C.ones_row_f, mean, start=True, stop=True),
              reads=[("lnsm", (0, ts)), ("CONST", (0, 128))],
              writes=[("PS", (4 * 512, 5 * 512))], tag=tag + "lnbcm")
    S.add("tensor", lambda e: e.matmul(bank(C, 5), C.ones_row_f, rstd, start=True, stop=True),
          reads=[("lnsm", (ts, 2 * ts)), ("CONST", (0, 128))],
          writes=[("PS", (5 * 512, 6 * 512))], tag=tag + "lnbcr")
    stg = sm[:, 2 * ts:3 * ts]
    for k in range(DC):
        ap, rds = src(k)
        oap, owr = out_fn(k)
        if rms:
            if gamma is not None:
                S.add("vector", lambda e, ap=ap, oap=oap, k=k: e.scalar_tensor_tensor(
                    oap, ap, gamma[k], bank(C, 5), ALU.mult, ALU.mult),
                    reads=rds + [("PS", (5 * 512, 6 * 512))] + list(gb_reads),
                    writes=[owr], tag=f"{tag}lnout{k}")
            else:
                S.add("vector", lambda e, ap=ap, oap=oap: e.tensor_tensor(
                    oap, ap, bank(C, 5), ALU.mult),
                    reads=rds + [("PS", (5 * 512, 6 * 512))], writes=[owr],
                    tag=f"{tag}lnout{k}")
        else:
            S.add("vector", lambda e, ap=ap: e.tensor_tensor(stg, ap, bank(C, 4), ALU.subtract),
                  reads=rds + [("PS", (4 * 512, 5 * 512))],
                  writes=[("lnsm", (2 * ts, 3 * ts))], tag=f"{tag}lnsub{k}")
            if gamma is None:
                S.add("vector", lambda e, oap=oap: e.tensor_tensor(oap, stg, bank(C, 5), ALU.mult),
                      reads=[("lnsm", (2 * ts, 3 * ts)), ("PS", (5 * 512, 6 * 512))],
                      writes=[owr], tag=f"{tag}lnmul{k}")
            else:
                S.add("vector", lambda e: e.tensor_tensor(stg, stg, bank(C, 5), ALU.mult),
                      reads=[("lnsm", (2 * ts, 3 * ts)), ("PS", (5 * 512, 6 * 512))],
                      writes=[("lnsm", (2 * ts, 3 * ts))], tag=f"{tag}lnmul{k}")
                S.add("vector", lambda e, oap=oap, k=k: e.tensor_scalar(
                    oap, stg, gamma[k], beta[k], ALU.mult, ALU.add),
                    reads=[("lnsm", (2 * ts, 3 * ts))] + list(gb_reads),
                    writes=[owr], tag=f"{tag}lngb{k}")


def emit_postln_tile(C, n, rbuf, rkey):
    ts = TTW
    tsl = (PAD + n * ts, PAD + (n + 1) * ts)

    def src(k):
        return rbuf[:, k * ts:(k + 1) * ts], [(rkey, (k * ts, (k + 1) * ts))]

    def out_fn(k):
        return (C.xres[k][:, PAD + n * ts:PAD + (n + 1) * ts], (f"xres{k}", tsl))

    emit_ln_core(C, src, out_fn, tag=f"p{n}_")



def emit_abs_stat(C, wap, rkey_reg, sv, svkey, idx, op):
    C.S.add("vector", lambda e: e.tensor_reduce(sv[:, idx:idx + 1], wap, AX.X, op,
                                                apply_absolute_value=True),
            reads=[rkey_reg], writes=[(svkey, (idx, idx + 1))], tag=f"astat{idx}")


def emit_reduce_scale(C, sv, svkey, sl, out_idx, op, post):
    S = C.S
    sm = C.lnsm
    lo, hi = sl
    S.add("vector", lambda e: e.tensor_reduce(sm[0:128, 4:5], sv[:, lo:hi], AX.X, op,
                                              apply_absolute_value=False),
          reads=[(svkey, sl)], writes=[("lnsm", (4, 5))], tag="rs_red")
    if op == ALU.add:
        S.add("tensor", lambda e: e.matmul(C.PS[0:1, 6 * 512:6 * 512 + 1], sm[0:128, 4:5],
                                           C.ones_col_f, start=True, stop=True),
              reads=[("lnsm", (4, 5)), ("CONST", (128, 129))],
              writes=[("PS", (6 * 512, 6 * 512 + 1))], tag="rs_mm")
        src = C.PS[0:1, 6 * 512:6 * 512 + 1]
        src_reads = [("PS", (6 * 512, 6 * 512 + 1))]
    else:
        S.add("tensor", lambda e: e.matmul(C.PS[0:1, 6 * 512:6 * 512 + 128], sm[0:128, 4:5],
                                           C.ident_f, start=True, stop=True),
              reads=[("lnsm", (4, 5)), ("CONST", (129, 257))],
              writes=[("PS", (6 * 512, 6 * 512 + 128))], tag="rs_tr")
        S.add("vector", lambda e: e.tensor_reduce(sm[0:1, 5:6], C.PS[0:1, 6 * 512:6 * 512 + 128],
                                                  AX.X, ALU.max),
              reads=[("PS", (6 * 512, 6 * 512 + 128))], writes=[("lnsm", (5, 6))], tag="rs_max")
        src = sm[0:1, 5:6]
        src_reads = [("lnsm", (5, 6))]
    cur = src
    cur_reads = src_reads
    for i, stage in enumerate(post):
        dst = sm[0:1, 6 + i:7 + i]
        if stage == ("recip",):
            S.add("vector", lambda e, cur=cur, dst=dst: e.reciprocal(dst, cur),
                  reads=cur_reads, writes=[("lnsm", (6 + i, 7 + i))], tag=f"rs_post{i}")
        elif len(stage) == 2:
            s1, op0 = stage
            S.add("vector", lambda e, cur=cur, dst=dst, s1=s1, op0=op0:
                  e.tensor_single_scalar(dst, cur, s1, op0),
                  reads=cur_reads, writes=[("lnsm", (6 + i, 7 + i))], tag=f"rs_post{i}")
        else:
            s1, s2, op0, op1 = stage
            S.add("vector", lambda e, cur=cur, dst=dst, s1=s1, s2=s2, op0=op0, op1=op1:
                  e.tensor_scalar(dst, cur, s1, s2, op0, op1),
                  reads=cur_reads, writes=[("lnsm", (6 + i, 7 + i))], tag=f"rs_post{i}")
        cur = dst
        cur_reads = [("lnsm", (6 + i, 7 + i))]
    S.add("tensor", lambda e, cur=cur: e.matmul(C.PS[:, 6 * 512:6 * 512 + 1], C.ones_row_f,
                                                cur, start=True, stop=True),
          reads=cur_reads + [("CONST", (0, 128))],
          writes=[("PS", (6 * 512, 6 * 512 + 1))], tag="rs_bc")
    S.add("vector", lambda e: e.tensor_copy(sv[:, out_idx:out_idx + 1],
                                            C.PS[:, 6 * 512:6 * 512 + 1]),
          reads=[("PS", (6 * 512, 6 * 512 + 1))],
          writes=[(svkey, (out_idx, out_idx + 1))], tag="rs_cp")


def emit_ternarize(C, wap, rkey_reg, out_ap, wkey_reg, thr_ap, thr_read, stg, stg_reg):
    S = C.S
    S.add("vector", lambda e: e.tensor_scalar(stg, wap, -1.0, thr_ap, ALU.mult, ALU.is_ge),
          reads=[rkey_reg, thr_read], writes=[stg_reg], tag="tern_a")
    S.add("vector", lambda e: e.scalar_tensor_tensor(out_ap, wap, thr_ap, stg,
                                                     ALU.is_ge, ALU.subtract),
          reads=[rkey_reg, stg_reg, thr_read], writes=[wkey_reg], tag="tern_b")



def emit_l1ffn(C):
    S = C.S
    W1T = C.din("L1f_W1T", [1024, 4096], F32)
    W2T = C.din("L1f_W2T", [4096, 1024], F32)
    b1c = C.din("L1f_b1c", [128, 32], F32)
    b2c = C.din("L1f_b2c", [128, 8], F32)
    FQ = 4096
    NFC = FQ // 128

    w1t = C.sbuf("l1f_w1t", [128, DC * FQ], BF16)
    wq = C.sbuf("l1f_wq", [128, FQ // 2], F32)
    w2s = C.sbuf("l1f_w2s", [128, FQ], BF16)
    hsb = C.sbuf("l1f_h", [128, NFC * TTW], BF16)
    rb = C.sbuf("l1f_r", [128, DC * TTW], F32)
    sv = C.sbuf("l1f_sv", [128, 52], F32)
    bia = C.sbuf("l1f_bia", [128, 40], F32)

    S.add("sync", lambda e: e.dma_start(out=bia[:, 0:32], in_=b1c),
          writes=[("l1f_bia", (0, 32))], tag="ldb1")
    S.add("sync", lambda e: e.dma_start(out=bia[:, 32:40], in_=b2c),
          writes=[("l1f_bia", (32, 40))], tag="ldb2")

    w2scr = C.dscratch("l1f_w2scr", [4096, 1024], BF16)

    def load_group(dst_ap, dkey, dsl, src, r0, rn):
        if rn == 1:
            S.add("sync", lambda e: e.dma_start(out=dst_ap, in_=src[r0 * 128:(r0 + 1) * 128, :]),
                  writes=[(dkey, dsl)], tag=f"ldw{r0}")
        else:
            S.add("sync", lambda e: e.dma_start(
                out=dst_ap.rearrange("p (c f) -> p c f", c=rn),
                in_=src[r0 * 128:(r0 + rn) * 128, :].rearrange("(c p) f -> p c f", p=128)),
                writes=[(dkey, dsl)], tag=f"ldw{r0}")

    scale_post = [(1.0 / (4096.0 * 1024.0), 1e-8, ALU.mult, ALU.max), (0.5, ALU.mult)]

    H = FQ // 2
    def w1half(e, ch, hf):
        return e.dma_start(out=wq[:, 0:H],
                           in_=W1T[ch * 128:(ch + 1) * 128, hf * H:(hf + 1) * H])
    for ch in range(8):
        for hf in range(2):
            S.add("sync", lambda e, ch=ch, hf=hf: w1half(e, ch, hf),
                  writes=[("l1f_wq", (0, H))], tag=f"ldw1_{ch}_{hf}")
            emit_abs_stat(C, wq[:, 0:H], ("l1f_wq", (0, H)), sv, "l1f_sv",
                          2 * ch + hf, ALU.add)
    emit_reduce_scale(C, sv, "l1f_sv", (0, 16), 48, ALU.add, scale_post)
    for ch in range(8):
        for hf in range(2):
            S.add("sync", lambda e, ch=ch, hf=hf: w1half(e, ch, hf),
                  writes=[("l1f_wq", (0, H))], tag=f"ldw1b_{ch}_{hf}")
            osl = (ch * FQ + hf * H, ch * FQ + (hf + 1) * H)
            emit_ternarize(C, wq[:, 0:H], ("l1f_wq", (0, H)),
                           w1t[:, osl[0]:osl[1]], ("l1f_w1t", osl),
                           sv[:, 48:49], ("l1f_sv", (48, 49)),
                           rb[:, 0:H], ("l1f_r", (0, H)))
    for g in range(16):
        load_group(wq[:, 0:H], "l1f_wq", (0, H), W2T, g * 2, 2)
        for c in range(2):
            ch = g * 2 + c
            csl = (c * 1024, (c + 1) * 1024)
            emit_abs_stat(C, wq[:, csl[0]:csl[1]], ("l1f_wq", csl), sv, "l1f_sv", 16 + ch, ALU.add)
    emit_reduce_scale(C, sv, "l1f_sv", (16, 48), 49, ALU.add, scale_post)
    for g in range(16):
        load_group(wq[:, 0:H], "l1f_wq", (0, H), W2T, g * 2, 2)
        for c in range(2):
            ch = g * 2 + c
            csl = (c * 1024, (c + 1) * 1024)
            ssl = (0, 1024)
            emit_ternarize(C, wq[:, csl[0]:csl[1]], ("l1f_wq", csl),
                           w2s[:, ssl[0]:ssl[1]], ("l1f_w2s", ssl),
                           sv[:, 49:50], ("l1f_sv", (49, 50)),
                           rb[:, 0:1024], ("l1f_r", (0, 1024)))
            S.add("gpsimd", lambda e, ch=ch, ssl=ssl: e.dma_start(
                out=w2scr[ch * 128:(ch + 1) * 128, :], in_=w2s[:, ssl[0]:ssl[1]]),
                reads=[("l1f_w2s", ssl)], writes=[("l1f_w2scr", (ch, ch + 1))],
                weight=16, tag=f"stw2_{ch}")

    for n in range(NT):
        tsl = (PAD + n * TTW, PAD + (n + 1) * TTW)
        for m in range(NFC):
            bnk = m % 2
            def mmh(e, m=m, bnk=bnk, n=n):
                ins = None
                for k in range(DC):
                    ins = e.matmul(bank(C, bnk),
                                   w1t[:, k * FQ + m * 128: k * FQ + (m + 1) * 128],
                                   C.xres[k][:, PAD + n * TTW:PAD + (n + 1) * TTW],
                                   start=(k == 0), stop=(k == DC - 1))
                return ins
            S.add("tensor", mmh,
                  reads=[("l1f_w1t", None)] + [(f"xres{k}", tsl) for k in range(DC)],
                  writes=[("PS", (bnk * 512, bnk * 512 + 512))], tag=f"mmh{n}_{m}")
            S.add("scalar", lambda e, m=m, bnk=bnk: e.activation(
                hsb[:, m * TTW:(m + 1) * TTW], bank(C, bnk), AF.Gelu, bias=bia[:, m:m + 1]),
                reads=[("PS", (bnk * 512, bnk * 512 + 512)), ("l1f_bia", (m, m + 1))],
                writes=[("l1f_h", (m * TTW, (m + 1) * TTW))], tag=f"gelu{n}_{m}")
        for g in range(8):
            dsl = (0, FQ)
            S.add("sync", lambda e, g=g, dsl=dsl: e.dma_start(
                out=w2s[:, dsl[0]:dsl[1]].rearrange("p (c f) -> p c f", c=4),
                in_=w2scr[g * 4 * 128:(g + 1) * 4 * 128, :].rearrange("(c p) f -> p c f", p=128)),
                reads=[("l1f_w2scr", (g * 4, (g + 1) * 4))],
                writes=[("l1f_w2s", dsl)], tag=f"ldw2_{n}_{g}")
            for mo in range(DC):
                def mmo(e, g=g, mo=mo, dsl=dsl):
                    ins = None
                    for c in range(4):
                        ins = e.matmul(bank(C, mo),
                                       w2s[:, dsl[0] + c * 1024 + mo * 128:
                                            dsl[0] + c * 1024 + (mo + 1) * 128],
                                       hsb[:, (g * 4 + c) * TTW:(g * 4 + c + 1) * TTW],
                                       start=(g == 0 and c == 0), stop=(g == 7 and c == 3))
                    return ins
                S.add("tensor", mmo,
                      reads=[("l1f_w2s", dsl)] +
                            [("l1f_h", ((g * 4 + c) * TTW, (g * 4 + c + 1) * TTW))
                             for c in range(4)],
                      writes=[("PS", (mo * 512, mo * 512 + 512))], tag=f"mmo{n}_{g}_{mo}")
        for mo in range(DC):
            S.add("vector", lambda e, mo=mo, n=n: e.scalar_tensor_tensor(
                rb[:, mo * TTW:(mo + 1) * TTW], bank(C, mo), bia[:, 32 + mo:33 + mo],
                C.xres[mo][:, PAD + n * TTW:PAD + (n + 1) * TTW], ALU.add, ALU.add),
                reads=[("PS", (mo * 512, mo * 512 + 512)), ("l1f_bia", (32 + mo, 33 + mo)),
                       (f"xres{mo}", tsl)],
                writes=[("l1f_r", (mo * TTW, (mo + 1) * TTW))], tag=f"res{n}_{mo}")
        if n == 0:
            C.tap("h0", hsb[:, :], [("l1f_h", None)])
            C.tap("r0", rb[:, :], [("l1f_r", None)])
        emit_postln_tile(C, n, rb, "l1f_r")
        if n == 0:
            C.tap("x0n", C.xres[0][:, :], [("xres0", None)])
    C.tap("w1t", w1t[:, :], [("l1f_w1t", None)])
    C.tap("sv", sv[:, :], [("l1f_sv", None)])





def emit_l2ffn(C):
    S = C.S
    KER = C.din("L2f_K", [1024, 1024], F32)
    A = C.din("L2f_a", [1024, 16], F32)
    B = C.din("L2f_b", [16, 1024], F32)
    bic = C.din("L2f_bias", [128, 8], F32)
    BIGC = float(3 << 22)

    ksb = C.sbuf("l2f_k", [128, 8 * 1024], BF16)
    wq = C.sbuf("l2f_wq", [128, 2048], F32)
    stg1 = C.sbuf("l2f_s1", [128, 1024], F32)
    stg2 = C.sbuf("l2f_s2", [128, 1024], F32)
    asb = C.sbuf("l2f_a", [128, 8 * 16], BF16)
    bsb = C.sbuf("l2f_b", [16, 1024], BF16)
    zsb = C.sbuf("l2f_z", [16, T], BF16)
    rb = C.sbuf("l2f_r", [128, DC * TTW], F32)
    sv = C.sbuf("l2f_sv", [128, 20], F32)
    bia = C.sbuf("l2f_bia", [128, 8], F32)

    S.add("sync", lambda e: e.dma_start(out=bia[:, 0:8], in_=bic),
          writes=[("l2f_bia", (0, 8))], tag="ldbias")
    S.add("gpsimd", lambda e: e.dma_start(
        out=asb[:, :].rearrange("p (c f) -> p c f", c=8),
        in_=A.rearrange("(c p) f -> p c f", p=128)),
        writes=[("l2f_a", None)], weight=16, tag="lda")
    S.add("gpsimd", lambda e: e.dma_start(out=bsb[:, :], in_=B),
        writes=[("l2f_b", None)], weight=16, tag="ldb")

    for g in range(4):
        S.add("sync", lambda e, g=g: e.dma_start(
            out=wq[:, :].rearrange("p (c f) -> p c f", c=2),
            in_=KER[g * 256:(g + 1) * 256, :].rearrange("(c p) f -> p c f", p=128)),
            writes=[("l2f_wq", (0, 2048))], tag=f"ldk{g}")
        for c in range(2):
            ch = g * 2 + c
            csl = (c * 1024, (c + 1) * 1024)
            emit_abs_stat(C, wq[:, csl[0]:csl[1]], ("l2f_wq", csl), sv, "l2f_sv", ch, ALU.max)
    emit_reduce_scale(C, sv, "l2f_sv", (0, 8), 16, ALU.max, [(1.0 / 7.0, ALU.mult)])
    emit_reduce_scale(C, sv, "l2f_sv", (0, 8), 17, ALU.max, [("recip",), (7.0, ALU.mult)])
    for g in range(4):
        S.add("sync", lambda e, g=g: e.dma_start(
            out=wq[:, :].rearrange("p (c f) -> p c f", c=2),
            in_=KER[g * 256:(g + 1) * 256, :].rearrange("(c p) f -> p c f", p=128)),
            writes=[("l2f_wq", (0, 2048))], tag=f"ldk2{g}")
        for c in range(2):
            ch = g * 2 + c
            csl = (c * 1024, (c + 1) * 1024)
            S.add("vector", lambda e, csl=csl: e.tensor_scalar(
                stg1[:, :], wq[:, csl[0]:csl[1]], sv[:, 17:18], BIGC, ALU.mult, ALU.add),
                reads=[("l2f_wq", csl), ("l2f_sv", (17, 18))],
                writes=[("l2f_s1", None)], tag=f"q1_{ch}")
            S.add("vector", lambda e: e.tensor_scalar(
                stg2[:, :], stg1[:, :], BIGC, 7.0, ALU.subtract, ALU.min),
                reads=[("l2f_s1", None)], writes=[("l2f_s2", None)], tag=f"q2_{ch}")
            S.add("vector", lambda e, ch=ch: e.tensor_scalar(
                ksb[:, ch * 1024:(ch + 1) * 1024], stg2[:, :], -8.0, sv[:, 16:17],
                ALU.max, ALU.mult),
                reads=[("l2f_s2", None), ("l2f_sv", (16, 17))],
                writes=[("l2f_k", (ch * 1024, (ch + 1) * 1024))], tag=f"q3_{ch}")

    for n in range(NT):
        tsl = (PAD + n * TTW, PAD + (n + 1) * TTW)
        zsl = (n * TTW, (n + 1) * TTW)
        def mmz(e, n=n):
            ins = None
            for k in range(DC):
                ins = e.matmul(C.PS[0:16, 2 * 512:3 * 512],
                               asb[:, k * 16:(k + 1) * 16],
                               C.xres[k][:, PAD + n * TTW:PAD + (n + 1) * TTW],
                               start=(k == 0), stop=(k == DC - 1))
            return ins
        S.add("tensor", mmz,
              reads=[("l2f_a", None)] + [(f"xres{k}", tsl) for k in range(DC)],
              writes=[("PS", (2 * 512, 3 * 512))], tag=f"mmz{n}")
        S.add("vector", lambda e, zsl=zsl: e.tensor_copy(zsb[0:16, zsl[0]:zsl[1]],
                                                         C.PS[0:16, 2 * 512:3 * 512]),
              reads=[("PS", (2 * 512, 3 * 512))], writes=[("l2f_z", zsl)], tag=f"zcp{n}")
        for mo in range(DC):
            def mmo(e, mo=mo, n=n, zsl=zsl):
                for k in range(DC):
                    e.matmul(bank(C, mo),
                             ksb[:, k * 1024 + mo * 128: k * 1024 + (mo + 1) * 128],
                             C.xres[k][:, PAD + n * TTW:PAD + (n + 1) * TTW],
                             start=(k == 0), stop=False)
                return e.matmul(bank(C, mo), bsb[:, mo * 128:(mo + 1) * 128],
                                zsb[0:16, zsl[0]:zsl[1]], start=False, stop=True)
            S.add("tensor", mmo,
                  reads=[("l2f_k", None), ("l2f_b", None), ("l2f_z", zsl)] +
                        [(f"xres{k}", tsl) for k in range(DC)],
                  writes=[("PS", (mo * 512, mo * 512 + 512))], tag=f"l2mmo{n}_{mo}")
        for mo in range(DC):
            S.add("vector", lambda e, mo=mo, n=n: e.scalar_tensor_tensor(
                rb[:, mo * TTW:(mo + 1) * TTW], bank(C, mo), bia[:, mo:mo + 1],
                C.xres[mo][:, PAD + n * TTW:PAD + (n + 1) * TTW], ALU.add, ALU.add),
                reads=[("PS", (mo * 512, mo * 512 + 512)), ("l2f_bia", (mo, mo + 1)),
                       (f"xres{mo}", tsl)],
                writes=[("l2f_r", (mo * TTW, (mo + 1) * TTW))], tag=f"l2res{n}_{mo}")
        emit_postln_tile(C, n, rb, "l2f_r")





def emit_l3ffn(C):
    S = C.S
    SELT = C.din("L3f_selT", [1024, 16], F32)
    KEYS = C.din("L3f_keys", [1024, 4096], F32)
    VALS = C.din("L3f_vals", [4096, 1024], F32)
    RS = C.din("L3f_rs", [1, 1], F32)

    keysb = C.sbuf("l3f_keys", [128, 8 * 4096], BF16)
    selb = C.sbuf("l3f_sel", [128, 8 * 16], BF16)
    usb = C.sbuf("l3f_u", [128, 32 * TTW], BF16)
    vsb = C.sbuf("l3f_v", [128, 2048], BF16)
    rb = C.sbuf("l3f_r", [128, DC * TTW], F32)
    ptm = C.sbuf("l3f_p", [128, 64], F32)
    mosc = C.sbuf("l3f_m", [128, 304], F32)
    wtm = C.sbuf("l3f_wtm", [128, 32 * 16], BF16)
    wfm = C.sbuf("l3f_wfm", [16, T], BF16)
    brcsb = C.sbuf("l3f_brc", [128, TTW], BF16)
    rssb = C.sbuf("l3f_rs", [1, 4], F32)
    expbc = C.sbuf("l3f_expbc", [16, 16 * 128], BF16)
    EXPBC = C.din("c_expbc", [16, 16 * 128], BF16)
    S.add("sync", lambda e: e.dma_start(out=expbc[:, :], in_=EXPBC),
          writes=[("l3f_expbc", None)], tag="ldexpbc")

    S.add("gpsimd", lambda e: e.dma_start(
        out=selb[:, :].rearrange("p (c f) -> p c f", c=8),
        in_=SELT.rearrange("(c p) f -> p c f", p=128)),
        writes=[("l3f_sel", None)], weight=16, tag="ldsel")
    for g in range(2):
        S.add("gpsimd", lambda e, g=g: e.dma_start(
            out=keysb[:, g * 4 * 4096:(g + 1) * 4 * 4096].rearrange("p (c f) -> p c f", c=4),
            in_=KEYS[g * 4 * 128:(g + 1) * 4 * 128, :].rearrange("(c p) f -> p c f", p=128)),
            writes=[("l3f_keys", (g * 4 * 4096, (g + 1) * 4 * 4096))], weight=16, tag=f"ldkeys{g}")
    S.add("sync", lambda e: e.dma_start(out=rssb[0:1, 0:1], in_=RS),
          writes=[("l3f_rs", (0, 1))], tag="ldrs")
    S.add("tensor", lambda e: e.matmul(C.PS[:, 2 * 512:2 * 512 + 1], C.ones_row_f,
                                       rssb[0:1, 0:1], start=True, stop=True),
          reads=[("l3f_rs", (0, 1)), ("CONST", (0, 128))],
          writes=[("PS", (2 * 512, 2 * 512 + 1))], tag="rsbc")
    S.add("vector", lambda e: e.tensor_copy(mosc[:, 291:292], C.PS[:, 2 * 512:2 * 512 + 1]),
          reads=[("PS", (2 * 512, 2 * 512 + 1))], writes=[("l3f_m", (291, 292))], tag="rscp")

    for tc in range(32):
        xsl = (PAD + tc * 128, PAD + (tc + 1) * 128)
        psl = ((tc % 4) * 16, (tc % 4) * 16 + 16)
        def mml(e, tc=tc):
            ins = None
            for k in range(DC):
                ins = e.matmul(C.PS[:, 2 * 512:2 * 512 + 16],
                               C.xres[k][:, PAD + tc * 128:PAD + (tc + 1) * 128],
                               selb[:, k * 16:(k + 1) * 16],
                               start=(k == 0), stop=(k == DC - 1))
            return ins
        S.add("tensor", mml,
              reads=[("l3f_sel", None)] + [(f"xres{k}", xsl) for k in range(DC)],
              writes=[("PS", (2 * 512, 2 * 512 + 16))], tag=f"mml{tc}")
        S.add("scalar", lambda e, psl=psl: e.activation(ptm[:, psl[0]:psl[1]],
                                                        C.PS[:, 2 * 512:2 * 512 + 16], AF.Sigmoid),
              reads=[("PS", (2 * 512, 2 * 512 + 16))], writes=[("l3f_p", psl)], tag=f"sig{tc}")
        p16 = ptm[:, psl[0]:psl[1]]
        in0 = p16.unsqueeze(2).to_broadcast((128, 16, 16))
        in1 = p16.unsqueeze(1).to_broadcast((128, 16, 16))
        S.add("vector", lambda e, in0=in0, in1=in1: e.tensor_tensor(
            mosc[:, 0:256].rearrange("p (a b) -> p a b", a=16), in0, in1, ALU.is_gt),
            reads=[("l3f_p", psl)], writes=[("l3f_m", (0, 256))], tag=f"cmp{tc}")
        S.add("vector", lambda e: e.tensor_reduce(
            mosc[:, 256:272], mosc[:, 0:256].rearrange("p (a b) -> p a b", a=16),
            AX.X, ALU.add),
            reads=[("l3f_m", (0, 256))], writes=[("l3f_m", (256, 272))], tag=f"cnt{tc}")
        S.add("vector", lambda e: e.tensor_single_scalar(mosc[:, 256:272], mosc[:, 256:272],
                                                         12.0, ALU.is_ge),
              reads=[("l3f_m", (256, 272))], writes=[("l3f_m", (256, 272))], tag=f"msk{tc}")
        S.add("vector", lambda e, p16=p16: e.tensor_tensor(mosc[:, 272:288], p16,
                                                           mosc[:, 256:272], ALU.mult),
              reads=[("l3f_p", psl), ("l3f_m", (256, 272))],
              writes=[("l3f_m", (272, 288))], tag=f"pm{tc}")
        S.add("vector", lambda e: e.tensor_reduce(mosc[:, 288:289], mosc[:, 272:288],
                                                  AX.X, ALU.add),
              reads=[("l3f_m", (272, 288))], writes=[("l3f_m", (288, 289))], tag=f"zz{tc}")
        S.add("vector", lambda e: e.tensor_single_scalar(mosc[:, 289:290], mosc[:, 288:289],
                                                         1e-9, ALU.max),
              reads=[("l3f_m", (288, 289))], writes=[("l3f_m", (289, 290))], tag=f"zc{tc}")
        S.add("vector", lambda e: e.reciprocal(mosc[:, 290:291], mosc[:, 289:290]),
              reads=[("l3f_m", (289, 290))], writes=[("l3f_m", (290, 291))], tag=f"rec{tc}")
        S.add("vector", lambda e, tc=tc: e.tensor_scalar(
            wtm[:, tc * 16:(tc + 1) * 16], mosc[:, 272:288], mosc[:, 290:291],
            mosc[:, 291:292], ALU.mult, ALU.mult),
            reads=[("l3f_m", (272, 292))],
            writes=[("l3f_wtm", (tc * 16, (tc + 1) * 16))], tag=f"wgt{tc}")
        psb = C.PS[:, 3 * 512:4 * 512].bitcast(BF16)
        S.add("tensor", lambda e, tc=tc, psb=psb: e.transpose(
            psb[0:16, 0:128], wtm[:, tc * 16:(tc + 1) * 16], C.ident_bf),
            reads=[("l3f_wtm", (tc * 16, (tc + 1) * 16)), ("CONSTB", (128, 256))],
            writes=[("PS", (3 * 512, 4 * 512))], tag=f"wtr{tc}")
        S.add("vector", lambda e, tc=tc, psb=psb: e.tensor_copy(
            wfm[0:16, tc * 128:(tc + 1) * 128], psb[0:16, 0:128]),
            reads=[("PS", (3 * 512, 4 * 512))],
            writes=[("l3f_wfm", (tc * 128, (tc + 1) * 128))], tag=f"wcp{tc}")

    for n in range(NT):
        tsl = (PAD + n * TTW, PAD + (n + 1) * TTW)
        for e_i in range(16):
            S.add("tensor", lambda e, e_i=e_i, n=n: e.matmul(
                C.PS[:, 2 * 512:3 * 512], expbc[:, e_i * 128:(e_i + 1) * 128],
                wfm[0:16, n * TTW:(n + 1) * TTW], start=True, stop=True),
                reads=[("l3f_wfm", (n * TTW, (n + 1) * TTW)), ("l3f_expbc", None)],
                writes=[("PS", (2 * 512, 3 * 512))], tag=f"brc{n}_{e_i}")
            S.add("vector", lambda e: e.tensor_copy(brcsb[:, :], C.PS[:, 2 * 512:3 * 512]),
                  reads=[("PS", (2 * 512, 3 * 512))], writes=[("l3f_brc", None)],
                  tag=f"brcp{n}_{e_i}")
            for mh in range(2):
                uc = e_i * 2 + mh
                bnk = uc % 2
                def mms(e, e_i=e_i, mh=mh, bnk=bnk, n=n):
                    ins = None
                    for k in range(DC):
                        ins = e.matmul(bank(C, bnk),
                                       keysb[:, k * 4096 + e_i * 256 + mh * 128:
                                             k * 4096 + e_i * 256 + (mh + 1) * 128],
                                       C.xres[k][:, PAD + n * TTW:PAD + (n + 1) * TTW],
                                       start=(k == 0), stop=(k == DC - 1))
                    return ins
                S.add("tensor", mms,
                      reads=[("l3f_keys", None)] + [(f"xres{k}", tsl) for k in range(DC)],
                      writes=[("PS", (bnk * 512, bnk * 512 + 512))], tag=f"mms{n}_{uc}")
                S.add("vector", lambda e, uc=uc, bnk=bnk: e.scalar_tensor_tensor(
                    usb[:, uc * TTW:(uc + 1) * TTW], bank(C, bnk), 0.0, brcsb[:, :],
                    ALU.max, ALU.mult),
                    reads=[("PS", (bnk * 512, bnk * 512 + 512)), ("l3f_brc", None)],
                    writes=[("l3f_u", (uc * TTW, (uc + 1) * TTW))], tag=f"relu{n}_{uc}")
        for g in range(16):
            S.add("gpsimd", lambda e, g=g: e.dma_start(
                out=vsb[:, :].rearrange("p (c f) -> p c f", c=2),
                in_=VALS[g * 256:(g + 1) * 256, :].rearrange("(c p) f -> p c f", p=128)),
                writes=[("l3f_v", None)], weight=16, tag=f"ldv{n}_{g}")
            for mo in range(DC):
                def mmo(e, g=g, mo=mo):
                    ins = None
                    for c in range(2):
                        ins = e.matmul(bank(C, mo),
                                       vsb[:, c * 1024 + mo * 128: c * 1024 + (mo + 1) * 128],
                                       usb[:, (g * 2 + c) * TTW:(g * 2 + c + 1) * TTW],
                                       start=(g == 0 and c == 0), stop=(g == 15 and c == 1))
                    return ins
                S.add("tensor", mmo,
                      reads=[("l3f_v", None)] +
                            [("l3f_u", ((g * 2 + c) * TTW, (g * 2 + c + 1) * TTW))
                             for c in range(2)],
                      writes=[("PS", (mo * 512, mo * 512 + 512))], tag=f"l3mmo{n}_{g}_{mo}")
        for mo in range(DC):
            S.add("vector", lambda e, mo=mo, n=n: e.tensor_tensor(
                rb[:, mo * TTW:(mo + 1) * TTW], bank(C, mo),
                C.xres[mo][:, PAD + n * TTW:PAD + (n + 1) * TTW], ALU.add),
                reads=[("PS", (mo * 512, mo * 512 + 512)), (f"xres{mo}", tsl)],
                writes=[("l3f_r", (mo * TTW, (mo + 1) * TTW))], tag=f"l3res{n}_{mo}")
        emit_postln_tile(C, n, rb, "l3f_r")





def emit_mla(C, pfx, LAT):
    S = C.S
    NL = LAT // 128
    H = 16
    WdqT = C.din(pfx + "_WdqT", [1024, LAT], F32)
    Wdq = C.din(pfx + "_Wdq", [LAT, 1024], F32)
    Wuq = C.din(pfx + "_Wuq", [1024, LAT], F32)
    Wuk = C.din(pfx + "_Wuk", [1024, LAT], F32)
    WdkvT = C.din(pfx + "_WdkvT", [1024, LAT], F32)
    Wuv = C.din(pfx + "_Wuv", [1024, LAT], F32)
    WoT = C.din(pfx + "_WoT", [1024, 1024], F32)
    MSK = C.din("c_mla_masks", [128, 4 * 512], BF16)

    p = pfx
    wdqT = C.sbuf(p + "_wdqT", [128, DC * LAT], BF16)
    wuq = C.sbuf(p + "_wuq", [128, DC * LAT], BF16)
    wdkvT = C.sbuf(p + "_wdkvT", [128, DC * LAT], BF16)
    wuv = C.sbuf(p + "_wuv", [128, DC * LAT], BF16)
    wdqn = C.sbuf(p + "_wdqn", [128, NL * 1024], BF16)
    wstr = C.sbuf(p + "_wstr", [128, 1024], BF16)
    m1 = C.sbuf(p + "_m1", [128, NL * LAT], BF16)
    kef = C.sbuf(p + "_kef", [128, DC * LAT], BF16)
    ves = C.sbuf(p + "_ves", [128, NL * 1024], BF16)
    ckvf = C.sbuf(p + "_ckvf", [128, NL * T], BF16)
    cktm = C.sbuf(p + "_cktm", [128, 32 * LAT], BF16)
    cqb = C.sbuf(p + "_cqb", [128, NL * 512], BF16)
    qkt = C.sbuf(p + "_qkt", [128, NL * LAT], BF16)
    qlat = C.sbuf(p + "_qlat", [128, NL * 512], BF16)
    pnb = C.sbuf(p + "_pnb", [128, 4096], BF16)
    ptb = C.sbuf(p + "_ptb", [128, 32 * 512], BF16)
    msks = C.sbuf(p + "_msks", [128, 4 * 512], BF16)
    mst = C.sbuf(p + "_mst", [128, 8], F32)
    ctsb = C.sbuf(p + "_cts", [128, NL * 512], BF16)
    rb = C.sbuf(p + "_rb", [128, DC * TTW], F32)

    kmsk = p + "msk"
    S.add("sync", lambda e: e.dma_start(out=msks[:, :], in_=MSK),
          writes=[(kmsk, None)], tag="ldmsk")

    def ldw(dst, dkey, src, F):
        for g in range(2):
            S.add("gpsimd", lambda e, g=g: e.dma_start(
                out=dst[:, g * 4 * F:(g + 1) * 4 * F].rearrange("p (c f) -> p c f", c=4),
                in_=src[g * 4 * 128:(g + 1) * 4 * 128, :].rearrange("(c p) f -> p c f", p=128)),
                writes=[(dkey, (g * 4 * F, (g + 1) * 4 * F))], weight=16, tag="ldw_" + dkey)

    ldw(wdqT, p + "wdqT", WdqT, LAT)
    ldw(wuq, p + "wuq", Wuq, LAT)
    ldw(wdkvT, p + "wdkvT", WdkvT, LAT)
    ldw(wuv, p + "wuv", Wuv, LAT)
    ldw(kef, p + "kef", Wuk, LAT)
    for g in range(NL):
        S.add("gpsimd", lambda e, g=g: e.dma_start(
            out=wdqn[:, g * 1024:(g + 1) * 1024], in_=Wdq[g * 128:(g + 1) * 128, :]),
            writes=[(p + "wdqn", (g * 1024, (g + 1) * 1024))], weight=16, tag=f"ldwdqn{g}")

    for mc in range(NL):
        def mm1(e, mc=mc):
            ins = None
            for k in range(DC):
                ins = e.matmul(C.PS[:, 0:LAT],
                               wuq[:, k * LAT + mc * 128:k * LAT + (mc + 1) * 128],
                               kef[:, k * LAT:(k + 1) * LAT],
                               start=(k == 0), stop=(k == DC - 1))
            return ins
        S.add("tensor", mm1, reads=[(p + "wuq", None), (p + "kef", None)],
              writes=[("PS", (0, LAT))], tag=f"m1_{mc}")
        S.add("vector", lambda e, mc=mc: e.tensor_copy(m1[:, mc * LAT:(mc + 1) * LAT],
                                                       C.PS[:, 0:LAT]),
              reads=[("PS", (0, LAT))], writes=[(p + "m1", (mc * LAT, (mc + 1) * LAT))],
              tag=f"m1cp{mc}")
    for ma in range(DC):
        def mmk(e, ma=ma):
            ins = None
            for lc in range(NL):
                ins = e.matmul(C.PS[:, 0:LAT],
                               wdqn[:, lc * 1024 + ma * 128:lc * 1024 + (ma + 1) * 128],
                               m1[:, lc * LAT:(lc + 1) * LAT],
                               start=(lc == 0), stop=(lc == NL - 1))
            return ins
        S.add("tensor", mmk, reads=[(p + "wdqn", None), (p + "m1", None)],
              writes=[("PS", (0, LAT))], tag=f"kef{ma}")
        S.add("vector", lambda e, ma=ma: e.tensor_scalar_mul(
            kef[:, ma * LAT:(ma + 1) * LAT], C.PS[:, 0:LAT], 0.125),
            reads=[("PS", (0, LAT))], writes=[(p + "kef", (ma * LAT, (ma + 1) * LAT))],
            tag=f"kefcp{ma}")
    for ch in range(DC):
        S.add("gpsimd", lambda e, ch=ch: e.dma_start(
            out=wstr[:, :], in_=WoT[ch * 128:(ch + 1) * 128, :]),
            writes=[(p + "wstr", None)], weight=16, tag=f"ldwo{ch}")
        for mc in range(NL):
            for nn in range(2):
                bk = mc * 2 + nn
                S.add("tensor", lambda e, ch=ch, mc=mc, nn=nn, bk=bk: e.matmul(
                    bank(C, bk),
                    wuv[:, ch * LAT + mc * 128:ch * LAT + (mc + 1) * 128],
                    wstr[:, nn * 512:(nn + 1) * 512],
                    start=(ch == 0), stop=(ch == DC - 1)),
                    reads=[(p + "wuv", None), (p + "wstr", None)],
                    writes=[("PS", (bk * 512, (bk + 1) * 512))], tag=f"ve{ch}_{bk}")
    for mc in range(NL):
        for nn in range(2):
            bk = mc * 2 + nn
            S.add("vector", lambda e, mc=mc, nn=nn, bk=bk: e.tensor_copy(
                ves[:, mc * 1024 + nn * 512:mc * 1024 + (nn + 1) * 512], bank(C, bk)),
                reads=[("PS", (bk * 512, (bk + 1) * 512))],
                writes=[(p + "ves", (mc * 1024 + nn * 512, mc * 1024 + (nn + 1) * 512))],
                tag=f"vecp{bk}")
    for lc in range(NL):
        for n in range(NT):
            def mmc(e, lc=lc, n=n):
                ins = None
                for k in range(DC):
                    ins = e.matmul(C.PS[:, 4 * 512:5 * 512],
                                   wdkvT[:, k * LAT + lc * 128:k * LAT + (lc + 1) * 128],
                                   C.xres[k][:, PAD + n * TTW:PAD + (n + 1) * TTW],
                                   start=(k == 0), stop=(k == DC - 1))
                return ins
            S.add("tensor", mmc,
                  reads=[(p + "wdkvT", None)] +
                        [(f"xres{k}", (PAD + n * TTW, PAD + (n + 1) * TTW)) for k in range(DC)],
                  writes=[("PS", (4 * 512, 5 * 512))], tag=f"ckv{lc}_{n}")
            S.add("vector", lambda e, lc=lc, n=n: e.tensor_copy(
                ckvf[:, lc * T + n * TTW:lc * T + (n + 1) * TTW], C.PS[:, 4 * 512:5 * 512]),
                reads=[("PS", (4 * 512, 5 * 512))],
                writes=[(p + "ckvf", (lc * T + n * TTW, lc * T + (n + 1) * TTW))],
                tag=f"ckvcp{lc}_{n}")
    for tc in range(32):
        def mmt(e, tc=tc):
            ins = None
            for k in range(DC):
                ins = e.matmul(C.PS[:, 5 * 512:5 * 512 + LAT],
                               C.xres[k][:, PAD + tc * 128:PAD + (tc + 1) * 128],
                               wdkvT[:, k * LAT:(k + 1) * LAT],
                               start=(k == 0), stop=(k == DC - 1))
            return ins
        S.add("tensor", mmt,
              reads=[(p + "wdkvT", None)] +
                    [(f"xres{k}", (PAD + tc * 128, PAD + (tc + 1) * 128)) for k in range(DC)],
              writes=[("PS", (5 * 512, 6 * 512))], tag=f"cktm{tc}")
        S.add("vector", lambda e, tc=tc: e.tensor_copy(
            cktm[:, tc * LAT:(tc + 1) * LAT], C.PS[:, 5 * 512:5 * 512 + LAT]),
            reads=[("PS", (5 * 512, 6 * 512))],
            writes=[(p + "cktm", (tc * LAT, (tc + 1) * LAT))], tag=f"cktmcp{tc}")

    for b in range(NT):
        bsl = (PAD + b * TTW, PAD + (b + 1) * TTW)
        for lc in range(NL):
            def mmq(e, lc=lc, b=b):
                ins = None
                for k in range(DC):
                    ins = e.matmul(C.PS[:, 4 * 512:5 * 512],
                                   wdqT[:, k * LAT + lc * 128:k * LAT + (lc + 1) * 128],
                                   C.xres[k][:, PAD + b * TTW:PAD + (b + 1) * TTW],
                                   start=(k == 0), stop=(k == DC - 1))
                return ins
            S.add("tensor", mmq,
                  reads=[(p + "wdqT", None)] + [(f"xres{k}", bsl) for k in range(DC)],
                  writes=[("PS", (4 * 512, 5 * 512))], tag=f"cq{b}_{lc}")
            S.add("vector", lambda e, lc=lc: e.tensor_copy(
                cqb[:, lc * 512:(lc + 1) * 512], C.PS[:, 4 * 512:5 * 512]),
                reads=[("PS", (4 * 512, 5 * 512))],
                writes=[(p + "cqb", (lc * 512, (lc + 1) * 512))], tag=f"cqcp{b}_{lc}")
        for h in range(H):
            hc, ho = h // 2, (h % 2) * 64
            for qc in range(NL):
                S.add("tensor", lambda e, qc=qc, hc=hc, ho=ho: e.matmul(
                    C.PS[:, 4 * 512:4 * 512 + LAT],
                    wuq[ho:ho + 64, hc * LAT + qc * 128:hc * LAT + (qc + 1) * 128],
                    kef[ho:ho + 64, hc * LAT:(hc + 1) * LAT],
                    start=True, stop=True),
                    reads=[(p + "wuq", None), (p + "kef", None)],
                    writes=[("PS", (4 * 512, 4 * 512 + LAT))], tag=f"qk{b}_{h}_{qc}")
                S.add("vector", lambda e, qc=qc: e.tensor_copy(
                    qkt[:, qc * LAT:(qc + 1) * LAT], C.PS[:, 4 * 512:4 * 512 + LAT]),
                    reads=[("PS", (4 * 512, 4 * 512 + LAT))],
                    writes=[(p + "qkt", (qc * LAT, (qc + 1) * LAT))], tag=f"qkcp{b}_{h}_{qc}")
            for mc in range(NL):
                def mmql(e, mc=mc):
                    ins = None
                    for qc in range(NL):
                        ins = e.matmul(C.PS[:, 4 * 512:5 * 512],
                                       qkt[:, qc * LAT + mc * 128:qc * LAT + (mc + 1) * 128],
                                       cqb[:, qc * 512:(qc + 1) * 512],
                                       start=(qc == 0), stop=(qc == NL - 1))
                    return ins
                S.add("tensor", mmql, reads=[(p + "qkt", None), (p + "cqb", None)],
                      writes=[("PS", (4 * 512, 5 * 512))], tag=f"ql{b}_{h}_{mc}")
                S.add("vector", lambda e, mc=mc: e.tensor_copy(
                    qlat[:, mc * 512:(mc + 1) * 512], C.PS[:, 4 * 512:5 * 512]),
                    reads=[("PS", (4 * 512, 5 * 512))],
                    writes=[(p + "qlat", (mc * 512, (mc + 1) * 512))], tag=f"qlcp{b}_{h}_{mc}")
            for qi in range(4):
                qt = b * 4 + qi
                nkb = qt // 4 + 1
                nsc = 4 * nkb
                def mms(e, qi=qi, nkb=nkb):
                    ins = None
                    for kb in range(nkb):
                        for lc in range(NL):
                            ins = e.matmul(bank(C, kb),
                                           qlat[:, lc * 512 + qi * 128:lc * 512 + (qi + 1) * 128],
                                           ckvf[:, lc * T + kb * 512:lc * T + (kb + 1) * 512],
                                           start=(lc == 0), stop=(lc == NL - 1))
                    return ins
                S.add("tensor", mms, reads=[(p + "qlat", None), (p + "ckvf", None)],
                      writes=[("PS", (0, nkb * 512))], tag=f"S{b}_{h}_{qi}")
                S.add("vector", lambda e, nkb=nkb, qi=qi: e.tensor_tensor(
                    bank(C, nkb - 1), bank(C, nkb - 1),
                    msks[:, qi * 512:(qi + 1) * 512], ALU.add),
                    reads=[("PS", ((nkb - 1) * 512, nkb * 512)), (kmsk, None)],
                    writes=[("PS", ((nkb - 1) * 512, nkb * 512))], tag=f"mka{b}_{h}_{qi}")
                S.add("vector", lambda e, nkb=nkb: e.reduce_max(
                    mst[:, 0:1], C.PS[:, 0:nkb * 512], axis=AX.X, negate=True),
                    reads=[("PS", (0, nkb * 512))], writes=[(p + "mst", (0, 1))],
                    tag=f"nm{b}_{h}_{qi}")
                S.add("scalar", lambda e, nkb=nkb: e.activation(
                    pnb[:, 0:nkb * 512], C.PS[:, 0:nkb * 512], AF.Exp,
                    bias=mst[:, 0:1], accum_out=mst[:, 1:2]),
                    reads=[("PS", (0, nkb * 512)), (p + "mst", (0, 1))],
                    writes=[(p + "pnb", (0, nkb * 512)), (p + "mst", (1, 2))],
                    tag=f"exp{b}_{h}_{qi}")
                S.add("vector", lambda e: e.reciprocal(mst[:, 2:3], mst[:, 1:2]),
                      reads=[(p + "mst", (1, 2))], writes=[(p + "mst", (2, 3))],
                      tag=f"rin{b}_{h}_{qi}")
                S.add("vector", lambda e, nkb=nkb: e.tensor_scalar_mul(
                    pnb[:, 0:nkb * 512], pnb[:, 0:nkb * 512], mst[:, 2:3]),
                    reads=[(p + "pnb", (0, nkb * 512)), (p + "mst", (2, 3))],
                    writes=[(p + "pnb", (0, nkb * 512))], tag=f"nrm{b}_{h}_{qi}")
                psb = C.PS[:, 5 * 512:6 * 512].bitcast(BF16)
                for tg in range((nsc + 3) // 4):
                    ns = min(4, nsc - tg * 4)
                    def ttr(e, tg=tg, ns=ns, psb=psb):
                        ins = None
                        for j in range(ns):
                            ins = e.transpose(psb[:, j * 128:(j + 1) * 128],
                                              pnb[:, (tg * 4 + j) * 128:(tg * 4 + j + 1) * 128],
                                              C.ident_bf)
                        return ins
                    S.add("tensor", ttr,
                          reads=[(p + "pnb", (tg * 512, tg * 512 + ns * 128)),
                                 ("CONSTB", (128, 256))],
                          writes=[("PS", (5 * 512, 6 * 512))], tag=f"tr{b}_{h}_{qi}_{tg}")
                    def tcp(e, tg=tg, ns=ns, qi=qi, psb=psb):
                        ins = None
                        for j in range(ns):
                            sc = tg * 4 + j
                            ins = e.tensor_copy(
                                ptb[:, sc * 512 + qi * 128: sc * 512 + (qi + 1) * 128],
                                psb[:, j * 128:(j + 1) * 128])
                        return ins
                    S.add("vector", tcp,
                          reads=[("PS", (5 * 512, 6 * 512))],
                          writes=[(p + "ptb", (tg * 4 * 512 + qi * 128,
                                               (tg * 4 + ns - 1) * 512 + (qi + 1) * 128))],
                          tag=f"tcp{b}_{h}_{qi}_{tg}")
            nscb = 4 * (b + 1)
            for mc in range(NL):
                def mmx(e, mc=mc, nscb=nscb, b=b):
                    ins = None
                    for sc in range(nscb):
                        toff = (max(4 * b, sc) - 4 * b) * 128
                        ins = e.matmul(C.PS[:, (2 + mc) * 512 + toff:(3 + mc) * 512],
                                       cktm[:, sc * LAT + mc * 128:sc * LAT + (mc + 1) * 128],
                                       ptb[:, sc * 512 + toff:(sc + 1) * 512],
                                       start=(sc == 0), stop=(sc == nscb - 1))
                    return ins
                S.add("tensor", mmx, reads=[(p + "cktm", None), (p + "ptb", None)],
                      writes=[("PS", ((2 + mc) * 512, (3 + mc) * 512))], tag=f"ctx{b}_{h}_{mc}")
                S.add("vector", lambda e, mc=mc: e.tensor_copy(
                    ctsb[:, mc * 512:(mc + 1) * 512], C.PS[:, (2 + mc) * 512:(3 + mc) * 512]),
                    reads=[("PS", ((2 + mc) * 512, (3 + mc) * 512))],
                    writes=[(p + "cts", (mc * 512, (mc + 1) * 512))], tag=f"ctxcp{b}_{h}_{mc}")
            def mmy(e, h=h):
                ins = None
                for mc in range(NL):
                    ins = e.matmul(C.PS[0:64, 4 * 512:5 * 512],
                                   ves[:, mc * 1024 + h * 64:mc * 1024 + h * 64 + 64],
                                   ctsb[:, mc * 512:(mc + 1) * 512],
                                   start=(mc == 0), stop=(mc == NL - 1))
                return ins
            S.add("tensor", mmy, reads=[(p + "ves", None), (p + "cts", None)],
                  writes=[("PS", (4 * 512, 5 * 512))], tag=f"y{b}_{h}")
            S.add("vector", lambda e, hc=hc, ho=ho, b=b: e.tensor_tensor(
                rb[ho:ho + 64, hc * TTW:(hc + 1) * TTW],
                C.PS[0:64, 4 * 512:5 * 512],
                C.xres[hc][ho:ho + 64, PAD + b * TTW:PAD + (b + 1) * TTW], ALU.add),
                reads=[("PS", (4 * 512, 5 * 512)), (f"xres{hc}", bsl)],
                writes=[(p + "rb", (hc * TTW, (hc + 1) * TTW))],
                tag=f"yres{b}_{h}")
        emit_postln_tile(C, b, rb, p + "rb")


def emit_l1attn(C):
    emit_mla(C, "L1a", 256)


def emit_l2attn(C):
    emit_mla(C, "L2a", 128)





def emit_l0attn(C):
    S = C.S
    H, M, DH = 16, 64, 64
    Wqkv = C.din("L0a_Wqkv", [1024, 3072], F32)
    bqkv = C.din("L0a_bqkv", [128, 24], F32)
    Wout = C.din("L0a_Wout", [1024, 1024], F32)
    bout = C.din("L0a_bout", [128, 8], F32)

    wcol = C.sbuf("l0a_wcol", [128, 2 * 1024], BF16)
    qstage = C.sbuf("l0a_qst", [128, T], BF16)
    qlf = C.sbuf("l0a_qlf", [128, 8 * 64], F32)
    klf = C.sbuf("l0a_klf", [128, 8 * 64], F32)
    bia = C.sbuf("l0a_bia", [128, 32], F32)
    qh = C.sbuf("l0a_qh", [64, T], BF16)
    kh = C.sbuf("l0a_kh", [64, T], BF16)
    vh = C.sbuf("l0a_vh", [64, T], BF16)
    vht = C.sbuf("l0a_vht", [128, 32 * 64], BF16)
    s3b = C.sbuf("l0a_s3b", [64, T], BF16)
    a3t = C.sbuf("l0a_a3t", [128, 32 * 64], BF16)
    aos = C.sbuf("l0a_aos", [64, T], BF16)
    psc = C.sbuf("l0a_psc", [64, 8 * 64], BF16)
    a1b = C.sbuf("l0a_a1b", [128, 64], BF16)
    a1ts = C.sbuf("l0a_a1ts", [64, 128], BF16)
    mst = C.sbuf("l0a_mst", [128, 8], F32)
    wosb = C.sbuf("l0a_wo", [128, 8 * 1024], BF16)
    aobuf = C.sbuf("l0a_ao", [128, 8 * 512], BF16)
    rb = C.sbuf("l0a_rb", [128, DC * TTW], F32)

    q_scr = C.dscratch("l0a_qscr", [1024, T], BF16)
    k_scr = C.dscratch("l0a_kscr", [1024, T], BF16)
    v_scr = C.dscratch("l0a_vscr", [1024, T], BF16)
    ao_scr = C.dscratch("l0a_aoscr", [1024, T], BF16)

    S.add("sync", lambda e: e.dma_start(out=bia[:, 0:24], in_=bqkv),
          writes=[("l0a_bia", (0, 24))], tag="ldbqkv")
    S.add("sync", lambda e: e.dma_start(out=bia[:, 24:32], in_=bout),
          writes=[("l0a_bia", (24, 32))], tag="ldbout")
    for g in range(2):
        S.add("gpsimd", lambda e, g=g: e.dma_start(
            out=wosb[:, g * 4 * 1024:(g + 1) * 4 * 1024].rearrange("p (c f) -> p c f", c=4),
            in_=Wout[g * 4 * 128:(g + 1) * 4 * 128, :].rearrange("(c p) f -> p c f", p=128)),
            writes=[("l0a_wo", (g * 4096, (g + 1) * 4096))], weight=16, tag=f"ldwo{g}")

    for fc in range(24):
        slot = (fc % 2) * 1024
        S.add("gpsimd", lambda e, fc=fc, slot=slot: e.dma_start(
            out=wcol[:, slot:slot + 1024].rearrange("p (c f) -> p c f", c=8),
            in_=Wqkv[:, fc * 128:(fc + 1) * 128].rearrange("(c p) f -> p c f", p=128)),
            writes=[("l0a_wcol", (slot, slot + 1024))], weight=16, tag=f"ldwq{fc}")
        for n in range(NT):
            bk = n % 2
            def mmp(e, fc=fc, slot=slot, n=n, bk=bk):
                ins = None
                for k in range(DC):
                    ins = e.matmul(bank(C, bk),
                                   wcol[:, slot + k * 128:slot + (k + 1) * 128],
                                   C.xres[k][:, PAD + n * TTW:PAD + (n + 1) * TTW],
                                   start=(k == 0), stop=(k == DC - 1))
                return ins
            S.add("tensor", mmp,
                  reads=[("l0a_wcol", (slot, slot + 1024))] +
                        [(f"xres{k}", (PAD + n * TTW, PAD + (n + 1) * TTW)) for k in range(DC)],
                  writes=[("PS", (bk * 512, (bk + 1) * 512))], tag=f"qkv{fc}_{n}")
            if fc < 8:
                S.add("vector", lambda e, fc=fc, n=n, bk=bk: e.tensor_scalar(
                    qstage[:, n * 512:(n + 1) * 512], bank(C, bk), bia[:, fc:fc + 1],
                    0.125, ALU.add, ALU.mult),
                    reads=[("PS", (bk * 512, (bk + 1) * 512)), ("l0a_bia", (fc, fc + 1))],
                    writes=[("l0a_qst", (n * 512, (n + 1) * 512))], tag=f"qe{fc}_{n}")
            else:
                S.add("vector", lambda e, fc=fc, n=n, bk=bk: e.tensor_scalar_add(
                    qstage[:, n * 512:(n + 1) * 512], bank(C, bk), bia[:, fc:fc + 1]),
                    reads=[("PS", (bk * 512, (bk + 1) * 512)), ("l0a_bia", (fc, fc + 1))],
                    writes=[("l0a_qst", (n * 512, (n + 1) * 512))], tag=f"qe{fc}_{n}")
            if fc < 16:
                dstlf = qlf if fc < 8 else klf
                dkey = "l0a_qlf" if fc < 8 else "l0a_klf"
                cc = fc % 8
                S.add("vector", lambda e, cc=cc, n=n, dstlf=dstlf: e.tensor_reduce(
                    dstlf[:, cc * 64 + n * 8:cc * 64 + (n + 1) * 8],
                    qstage[:, n * 512:(n + 1) * 512].rearrange("p (g l) -> p g l", g=8),
                    AX.X, ALU.add),
                    reads=[("l0a_qst", (n * 512, (n + 1) * 512))],
                    writes=[(dkey, (cc * 64 + n * 8, cc * 64 + (n + 1) * 8))],
                    tag=f"lm{fc}_{n}")
        scr = q_scr if fc < 8 else (k_scr if fc < 16 else v_scr)
        skey = "l0a_qscr" if fc < 8 else ("l0a_kscr" if fc < 16 else "l0a_vscr")
        cc = fc % 8
        S.add("sync", lambda e, cc=cc, scr=scr: e.dma_start(
            out=scr[cc * 128:(cc + 1) * 128, :], in_=qstage[:, :]),
            reads=[("l0a_qst", None)], writes=[(skey, (cc, cc + 1))], tag=f"st{fc}")
    qlb = C.sbuf("l0a_qlb", [64, 16 * 64], BF16)
    klb = C.sbuf("l0a_klb", [64, 16 * 64], BF16)
    for hh in range(16):
        hcc, hoo = hh // 2, (hh % 2) * 64
        S.add("vector", lambda e, hh=hh, hcc=hcc, hoo=hoo: e.tensor_scalar_mul(
            qlb[:, hh * 64:(hh + 1) * 64], qlf[hoo:hoo + 64, hcc * 64:(hcc + 1) * 64],
            1.0 / 64.0),
            reads=[("l0a_qlf", (hcc * 64, (hcc + 1) * 64))],
            writes=[("l0a_qlb", (hh * 64, (hh + 1) * 64))], tag=f"qlm{hh}")
        S.add("vector", lambda e, hh=hh, hcc=hcc, hoo=hoo: e.tensor_scalar_mul(
            klb[:, hh * 64:(hh + 1) * 64], klf[hoo:hoo + 64, hcc * 64:(hcc + 1) * 64],
            1.0 / 64.0),
            reads=[("l0a_klf", (hcc * 64, (hcc + 1) * 64))],
            writes=[("l0a_klb", (hh * 64, (hh + 1) * 64))], tag=f"klm{hh}")

    I64 = C.ident_bf[0:64, 0:64]
    psb = C.PS[:, 5 * 512:6 * 512].bitcast(BF16)

    for h in range(H):
        hc, ho = h // 2, (h % 2) * 64
        S.add("sync", lambda e, h=h: e.dma_start(out=qh[:, :], in_=q_scr[h * 64:(h + 1) * 64, :]),
              reads=[("l0a_qscr", (hc, hc + 1))], writes=[("l0a_qh", None)], tag=f"ldq{h}")
        S.add("sync", lambda e, h=h: e.dma_start(out=kh[:, :], in_=k_scr[h * 64:(h + 1) * 64, :]),
              reads=[("l0a_kscr", (hc, hc + 1))], writes=[("l0a_kh", None)], tag=f"ldk{h}")
        S.add("sync", lambda e, h=h: e.dma_start(out=vh[:, :], in_=v_scr[h * 64:(h + 1) * 64, :]),
              reads=[("l0a_vscr", (hc, hc + 1))], writes=[("l0a_vh", None)], tag=f"ldv{h}")
        for tg in range(8):
            def vtr(e, tg=tg):
                ins = None
                for j in range(4):
                    tc = tg * 4 + j
                    ins = e.transpose(psb[0:128, j * 64:(j + 1) * 64],
                                      vh[:, tc * 128:(tc + 1) * 128], I64)
                return ins
            S.add("tensor", vtr,
                  reads=[("l0a_vh", (tg * 512, (tg + 1) * 512)), ("CONSTB", (128, 256))],
                  writes=[("PS", (5 * 512, 6 * 512))], tag=f"vtr{h}_{tg}")
            def vcp(e, tg=tg):
                ins = None
                for j in range(4):
                    tc = tg * 4 + j
                    ins = e.tensor_copy(vht[:, tc * 64:(tc + 1) * 64],
                                        psb[0:128, j * 64:(j + 1) * 64])
                return ins
            S.add("vector", vcp, reads=[("PS", (5 * 512, 6 * 512))],
                  writes=[("l0a_vht", (tg * 256, (tg + 1) * 256))], tag=f"vcp{h}_{tg}")
        def mm3(e, h=h):
            ins = None
            for n in range(NT):
                ins = e.matmul(C.PS[0:64, n * 512:(n + 1) * 512],
                               qlb[:, h * 64:(h + 1) * 64],
                               kh[:, n * 512:(n + 1) * 512], start=True, stop=True)
            return ins
        S.add("tensor", mm3,
              reads=[("l0a_qlb", (h * 64, (h + 1) * 64)), ("l0a_kh", None)],
              writes=[("PS", (0, 4096))], tag=f"s3{h}")
        S.add("vector", lambda e: e.reduce_max(mst[0:64, 0:1], C.PS[0:64, :], axis=AX.X,
                                               negate=True),
              reads=[("PS", (0, 4096))], writes=[("l0a_mst", (0, 1))], tag=f"s3m{h}")
        S.add("scalar", lambda e: e.activation(s3b[:, :], C.PS[0:64, :], AF.Exp,
                                               bias=mst[0:64, 0:1], accum_out=mst[0:64, 1:2]),
              reads=[("PS", (0, 4096)), ("l0a_mst", (0, 1))],
              writes=[("l0a_s3b", None), ("l0a_mst", (1, 2))], tag=f"s3e{h}")
        S.add("vector", lambda e: e.reciprocal(mst[0:64, 2:3], mst[0:64, 1:2]),
              reads=[("l0a_mst", (1, 2))], writes=[("l0a_mst", (2, 3))], tag=f"s3r{h}")
        for tg in range(8):
            def atr(e, tg=tg):
                ins = None
                for j in range(4):
                    tc = tg * 4 + j
                    ins = e.transpose(psb[0:128, j * 64:(j + 1) * 64],
                                      s3b[:, tc * 128:(tc + 1) * 128], I64)
                return ins
            S.add("tensor", atr,
                  reads=[("l0a_s3b", (tg * 512, (tg + 1) * 512)), ("CONSTB", (128, 256))],
                  writes=[("PS", (5 * 512, 6 * 512))], tag=f"atr{h}_{tg}")
            def acp(e, tg=tg):
                ins = None
                for j in range(4):
                    tc = tg * 4 + j
                    ins = e.tensor_copy(a3t[:, tc * 64:(tc + 1) * 64],
                                       psb[0:128, j * 64:(j + 1) * 64])
                return ins
            S.add("vector", acp, reads=[("PS", (5 * 512, 6 * 512))],
                  writes=[("l0a_a3t", (tg * 256, (tg + 1) * 256))], tag=f"acp{h}_{tg}")
        def mmu(e):
            ins = None
            for sc in range(32):
                ins = e.matmul(C.PS[0:64, 4 * 512:4 * 512 + 64],
                               a3t[:, sc * 64:(sc + 1) * 64],
                               vht[:, sc * 64:(sc + 1) * 64],
                               start=(sc == 0), stop=(sc == 31))
            return ins
        S.add("tensor", mmu, reads=[("l0a_a3t", None), ("l0a_vht", None)],
              writes=[("PS", (4 * 512, 4 * 512 + 64))], tag=f"U{h}")
        S.add("vector", lambda e: e.tensor_scalar_mul(psc[:, 7 * 64:8 * 64],
                                                      C.PS[0:64, 4 * 512:4 * 512 + 64],
                                                      mst[0:64, 2:3]),
              reads=[("PS", (4 * 512, 4 * 512 + 64)), ("l0a_mst", (2, 3))],
              writes=[("l0a_psc", (7 * 64, 8 * 64))], tag=f"Ue{h}")
        S.add("tensor", lambda e, h=h: e.matmul(
            C.PS[0:64, 4 * 512:4 * 512 + 64], qlb[:, h * 64:(h + 1) * 64],
            klb[:, h * 64:(h + 1) * 64], start=True, stop=True),
            reads=[("l0a_qlb", (h * 64, (h + 1) * 64)), ("l0a_klb", (h * 64, (h + 1) * 64))],
            writes=[("PS", (4 * 512, 4 * 512 + 64))], tag=f"s2{h}")
        S.add("vector", lambda e: e.reduce_max(mst[0:64, 3:4], C.PS[0:64, 4 * 512:4 * 512 + 64],
                                               axis=AX.X, negate=True),
              reads=[("PS", (4 * 512, 4 * 512 + 64))], writes=[("l0a_mst", (3, 4))],
              tag=f"s2m{h}")
        S.add("scalar", lambda e: e.activation(psc[:, 0:64], C.PS[0:64, 4 * 512:4 * 512 + 64],
                                               AF.Exp, bias=mst[0:64, 3:4],
                                               accum_out=mst[0:64, 4:5]),
              reads=[("PS", (4 * 512, 4 * 512 + 64)), ("l0a_mst", (3, 4))],
              writes=[("l0a_psc", (0, 64)), ("l0a_mst", (4, 5))], tag=f"s2e{h}")
        S.add("vector", lambda e: e.reciprocal(mst[0:64, 5:6], mst[0:64, 4:5]),
              reads=[("l0a_mst", (4, 5))], writes=[("l0a_mst", (5, 6))], tag=f"s2r{h}")
        S.add("vector", lambda e: e.tensor_scalar_mul(psc[:, 0:64], psc[:, 0:64],
                                                      mst[0:64, 5:6]),
              reads=[("l0a_psc", (0, 64)), ("l0a_mst", (5, 6))],
              writes=[("l0a_psc", (0, 64))], tag=f"s2n{h}")
        S.add("vector", lambda e: e.tensor_copy(psc[:, 2 * 64:3 * 64], psc[:, 0:64]),
              reads=[("l0a_psc", (0, 64))], writes=[("l0a_psc", (2 * 64, 3 * 64))],
              tag=f"vi{h}")
        def tr64(src_sl, dst_sl, tg):
            S.add("tensor", lambda e: e.transpose(psb[0:64, 0:64],
                                                  psc[:, src_sl[0]:src_sl[1]], I64),
                  reads=[("l0a_psc", src_sl), ("CONSTB", (128, 256))],
                  writes=[("PS", (5 * 512, 6 * 512))], tag=f"t64a{tg}")
            S.add("vector", lambda e: e.tensor_copy(psc[:, dst_sl[0]:dst_sl[1]],
                                                    psb[0:64, 0:64]),
                  reads=[("PS", (5 * 512, 6 * 512))], writes=[("l0a_psc", dst_sl)],
                  tag=f"t64b{tg}")
        tr64((0, 64), (64, 128), f"a2T{h}")
        for it in range(6):
            S.add("tensor", lambda e: e.matmul(C.PS[0:64, 4 * 512:4 * 512 + 64],
                                               psc[:, 64:128], psc[:, 2 * 64:3 * 64],
                                               start=True, stop=True),
                  reads=[("l0a_psc", (64, 128)), ("l0a_psc", (2 * 64, 3 * 64))],
                  writes=[("PS", (4 * 512, 4 * 512 + 64))], tag=f"kv{h}_{it}")
            S.add("vector", lambda e: e.tensor_copy(psc[:, 4 * 64:5 * 64],
                                                    C.PS[0:64, 4 * 512:4 * 512 + 64]),
                  reads=[("PS", (4 * 512, 4 * 512 + 64))],
                  writes=[("l0a_psc", (4 * 64, 5 * 64))], tag=f"kvc{h}_{it}")
            tr64((4 * 64, 5 * 64), (5 * 64, 6 * 64), f"kvT{h}_{it}")
            S.add("vector", lambda e: e.scalar_tensor_tensor(
                psc[:, 6 * 64:7 * 64], I64, 7.0, psc[:, 4 * 64:5 * 64],
                ALU.mult, ALU.subtract),
                reads=[("CONSTB", (128, 256)), ("l0a_psc", (4 * 64, 5 * 64))],
                writes=[("l0a_psc", (6 * 64, 7 * 64))], tag=f"t7{h}_{it}")
            S.add("tensor", lambda e: e.matmul(C.PS[0:64, 4 * 512:4 * 512 + 64],
                                               psc[:, 5 * 64:6 * 64], psc[:, 6 * 64:7 * 64],
                                               start=True, stop=True),
                  reads=[("l0a_psc", (5 * 64, 6 * 64)), ("l0a_psc", (6 * 64, 7 * 64))],
                  writes=[("PS", (4 * 512, 4 * 512 + 64))], tag=f"kt{h}_{it}")
            S.add("vector", lambda e: e.scalar_tensor_tensor(
                psc[:, 6 * 64:7 * 64], I64, 15.0, C.PS[0:64, 4 * 512:4 * 512 + 64],
                ALU.mult, ALU.subtract),
                reads=[("CONSTB", (128, 256)), ("PS", (4 * 512, 4 * 512 + 64))],
                writes=[("l0a_psc", (6 * 64, 7 * 64))], tag=f"t15{h}_{it}")
            S.add("tensor", lambda e: e.matmul(C.PS[0:64, 4 * 512:4 * 512 + 64],
                                               psc[:, 5 * 64:6 * 64], psc[:, 6 * 64:7 * 64],
                                               start=True, stop=True),
                  reads=[("l0a_psc", (5 * 64, 6 * 64)), ("l0a_psc", (6 * 64, 7 * 64))],
                  writes=[("PS", (4 * 512, 4 * 512 + 64))], tag=f"kt2{h}_{it}")
            S.add("vector", lambda e: e.scalar_tensor_tensor(
                psc[:, 6 * 64:7 * 64], I64, 13.0, C.PS[0:64, 4 * 512:4 * 512 + 64],
                ALU.mult, ALU.subtract),
                reads=[("CONSTB", (128, 256)), ("PS", (4 * 512, 4 * 512 + 64))],
                writes=[("l0a_psc", (6 * 64, 7 * 64))], tag=f"t13{h}_{it}")
            tr64((2 * 64, 3 * 64), (3 * 64, 4 * 64), f"vT{h}_{it}")
            S.add("tensor", lambda e: e.matmul(C.PS[0:64, 4 * 512:4 * 512 + 64],
                                               psc[:, 3 * 64:4 * 64], psc[:, 6 * 64:7 * 64],
                                               start=True, stop=True),
                  reads=[("l0a_psc", (3 * 64, 4 * 64)), ("l0a_psc", (6 * 64, 7 * 64))],
                  writes=[("PS", (4 * 512, 4 * 512 + 64))], tag=f"vn{h}_{it}")
            S.add("vector", lambda e: e.tensor_scalar_mul(psc[:, 2 * 64:3 * 64],
                                                          C.PS[0:64, 4 * 512:4 * 512 + 64],
                                                          0.25),
                  reads=[("PS", (4 * 512, 4 * 512 + 64))],
                  writes=[("l0a_psc", (2 * 64, 3 * 64))], tag=f"vne{h}_{it}")
        tr64((2 * 64, 3 * 64), (3 * 64, 4 * 64), f"vTf{h}")
        S.add("tensor", lambda e: e.matmul(C.PS[0:64, 4 * 512:4 * 512 + 64],
                                           psc[:, 3 * 64:4 * 64], psc[:, 7 * 64:8 * 64],
                                           start=True, stop=True),
              reads=[("l0a_psc", (3 * 64, 4 * 64)), ("l0a_psc", (7 * 64, 8 * 64))],
              writes=[("PS", (4 * 512, 4 * 512 + 64))], tag=f"wm{h}")
        S.add("vector", lambda e: e.tensor_copy(psc[:, 6 * 64:7 * 64],
                                                C.PS[0:64, 4 * 512:4 * 512 + 64]),
              reads=[("PS", (4 * 512, 4 * 512 + 64))],
              writes=[("l0a_psc", (6 * 64, 7 * 64))], tag=f"wmc{h}")
        for tc in range(32):
            S.add("tensor", lambda e, tc=tc, h=h: e.matmul(
                C.PS[:, 4 * 512 + 128:4 * 512 + 192], qh[:, tc * 128:(tc + 1) * 128],
                klb[:, h * 64:(h + 1) * 64], start=True, stop=True),
                reads=[("l0a_qh", (tc * 128, (tc + 1) * 128)),
                       ("l0a_klb", (h * 64, (h + 1) * 64))],
                writes=[("PS", (4 * 512 + 128, 4 * 512 + 192))], tag=f"s1{h}_{tc}")
            S.add("vector", lambda e: e.reduce_max(mst[:, 6:7],
                                                   C.PS[:, 4 * 512 + 128:4 * 512 + 192],
                                                   axis=AX.X, negate=True),
                  reads=[("PS", (4 * 512 + 128, 4 * 512 + 192))],
                  writes=[("l0a_mst", (6, 7))], tag=f"s1m{h}_{tc}")
            S.add("scalar", lambda e: e.activation(a1b[:, :],
                                                   C.PS[:, 4 * 512 + 128:4 * 512 + 192],
                                                   AF.Exp, bias=mst[:, 6:7],
                                                   accum_out=mst[:, 7:8]),
                  reads=[("PS", (4 * 512 + 128, 4 * 512 + 192)), ("l0a_mst", (6, 7))],
                  writes=[("l0a_a1b", None), ("l0a_mst", (7, 8))], tag=f"s1e{h}_{tc}")
            S.add("vector", lambda e: e.reciprocal(mst[:, 6:7], mst[:, 7:8]),
                  reads=[("l0a_mst", (7, 8))], writes=[("l0a_mst", (6, 7))],
                  tag=f"s1r{h}_{tc}")
            S.add("vector", lambda e: e.tensor_scalar_mul(a1b[:, :], a1b[:, :], mst[:, 6:7]),
                  reads=[("l0a_a1b", None), ("l0a_mst", (6, 7))],
                  writes=[("l0a_a1b", None)], tag=f"s1n{h}_{tc}")
            S.add("tensor", lambda e: e.transpose(psb[0:64, 0:128], a1b[:, :], C.ident_bf),
                  reads=[("l0a_a1b", None), ("CONSTB", (128, 256))],
                  writes=[("PS", (5 * 512, 6 * 512))], tag=f"a1t{h}_{tc}")
            S.add("vector", lambda e: e.tensor_copy(a1ts[:, :], psb[0:64, 0:128]),
                  reads=[("PS", (5 * 512, 6 * 512))],
                  writes=[("l0a_a1ts", None)], tag=f"a1c{h}_{tc}")
            S.add("tensor", lambda e: e.matmul(C.PS[0:64, 4 * 512 + 256:4 * 512 + 384],
                                               psc[:, 6 * 64:7 * 64], a1ts[:, :],
                                               start=True, stop=True),
                  reads=[("l0a_psc", (6 * 64, 7 * 64)), ("l0a_a1ts", None)],
                  writes=[("PS", (4 * 512 + 256, 4 * 512 + 384))], tag=f"oh{h}_{tc}")
            S.add("vector", lambda e, tc=tc: e.tensor_copy(
                aos[:, tc * 128:(tc + 1) * 128], C.PS[0:64, 4 * 512 + 256:4 * 512 + 384]),
                reads=[("PS", (4 * 512 + 256, 4 * 512 + 384))],
                writes=[("l0a_aos", (tc * 128, (tc + 1) * 128))], tag=f"ohc{h}_{tc}")
        S.add("sync", lambda e, h=h: e.dma_start(out=ao_scr[h * 64:(h + 1) * 64, :],
                                                 in_=aos[:, :]),
              reads=[("l0a_aos", None)], writes=[("l0a_aoscr", (hc, hc + 1))], tag=f"stao{h}")

    for n in range(NT):
        tsl = (PAD + n * TTW, PAD + (n + 1) * TTW)
        S.add("sync", lambda e, n=n: e.dma_start(
            out=aobuf[:, :].rearrange("p (c f) -> p c f", c=8),
            in_=ao_scr[:, n * TTW:(n + 1) * TTW].rearrange("(c p) f -> p c f", p=128)),
            reads=[("l0a_aoscr", (0, 8))], writes=[("l0a_ao", None)], tag=f"ldao{n}")
        for mo in range(DC):
            def mmo(e, mo=mo):
                ins = None
                for k in range(DC):
                    ins = e.matmul(bank(C, mo),
                                   wosb[:, k * 1024 + mo * 128:k * 1024 + (mo + 1) * 128],
                                   aobuf[:, k * 512:(k + 1) * 512],
                                   start=(k == 0), stop=(k == DC - 1))
                return ins
            S.add("tensor", mmo, reads=[("l0a_wo", None), ("l0a_ao", None)],
                  writes=[("PS", (mo * 512, (mo + 1) * 512))], tag=f"l0mmo{n}_{mo}")
        for mo in range(DC):
            S.add("vector", lambda e, mo=mo, n=n: e.scalar_tensor_tensor(
                rb[:, mo * TTW:(mo + 1) * TTW], bank(C, mo), bia[:, 24 + mo:25 + mo],
                C.xres[mo][:, PAD + n * TTW:PAD + (n + 1) * TTW], ALU.add, ALU.add),
                reads=[("PS", (mo * 512, (mo + 1) * 512)), ("l0a_bia", (24 + mo, 25 + mo)),
                       (f"xres{mo}", tsl)],
                writes=[("l0a_rb", (mo * TTW, (mo + 1) * TTW))], tag=f"l0res{n}_{mo}")
        emit_postln_tile(C, n, rb, "l0a_rb")





def emit_l3attn(C):
    S = C.S
    EC = 16
    NST = 16
    GA_COLS = 50176

    ga = C.sbuf("l3ga", [128, GA_COLS], BF16)
    K_GA = "l3ga"
    O_XIN = 0
    O_YST = 0
    O_ZST = 8240
    O_XCS = 16432
    O_DLS = 24624
    O_NRM = 24624
    O_DUS = 32816
    O_H0T = 32816
    O_WCL = 41008
    O_SEG = 43056
    O_BCB = 45104
    O_WO2 = 43056
    O_HOS = 41008
    O_H1R = 0
    O_FWC = 32768
    O_AST = 33792
    O_GST = 37888
    O_ALD = 41984
    O_VST = 46080
    O_FW2 = 0
    O_VGR = 8192

    def gap(a, b):
        return ga[:, a:b]

    def grg(a, b):
        return (K_GA, (a, b))

    sm = C.sbuf("l3a_sm", [128, 216], F32)
    acol = C.sbuf("l3a_ac", [128, 256], F32)
    hcar = C.sbuf("l3a_hc", [128, 256], F32)
    dtsb = C.sbuf("l3a_dtw", [64, 2048], BF16)
    xpsb = C.sbuf("l3a_xpw", [128, 16 * 96], BF16)
    sel16 = C.sbuf("l3a_sel", [16, 16 * 128], BF16)
    dts = C.sbuf("l3a_dts", [64, 512], BF16)
    bcur = C.sbuf("l3a_bcu", [16, 512], BF16)
    ccur = C.sbuf("l3a_ccu", [16, 512], BF16)
    xcar = C.sbuf("l3a_xcar", [128, 48], BF16)
    rb = C.sbuf("l3a_rb", [128, 4096], F32)
    K_RB = "l3a_rb"

    SEL = C.din("c_sel16", [16, 16 * 128], BF16)
    S.add("sync", lambda e: e.dma_start(out=sel16[:, :], in_=SEL),
          writes=[("l3a_sel", None)], tag="ldsel16")

    FW1 = C.din("L3A_fw1T", [1024, 8192], F32)
    FW2 = C.din("L3A_fw2T", [4096, 1024], F32)
    FB1 = C.din("L3A_fb1", [128, 64], F32)
    FB2 = C.din("L3A_fb2", [128, 8], F32)

    h_scr = [C.dscratch(f"l3a_h{i}", [1024, T], BF16) for i in range(2)]
    au_scr = C.dscratch("l3a_au", [4096, T], BF16)
    v_scr = C.dscratch("l3a_v", [4096, T], BF16)
    inw_scr = [C.dscratch(f"l3a_inw{i}", [1024, 4096], BF16) for i in range(2)]
    ow_scr = [C.dscratch(f"l3a_ow{i}", [2048, 1024], BF16) for i in range(2)]
    fw2_scr = C.dscratch("l3a_fw2s", [4096, 1024], BF16)

    fc1 = C.fconst(1.0)
    fcm1 = C.fconst(-1.0)

    def precast(src, scr, skey, rows, F, tagp):
        for c in range(rows // 128):
            S.add("gpsimd", lambda e, c=c: e.dma_start(
                out=gap(O_XIN, O_XIN + F), in_=src[c * 128:(c + 1) * 128, :]),
                writes=[grg(O_XIN, O_XIN + F)], weight=16, tag=f"{tagp}c{c}")
            S.add("sync", lambda e, c=c: e.dma_start(
                out=scr[c * 128:(c + 1) * 128, :], in_=gap(O_XIN, O_XIN + F)),
                reads=[grg(O_XIN, O_XIN + F)], writes=[(skey, (c, c + 1))],
                tag=f"{tagp}s{c}")

    for i in range(2):
        INW = C.din(f"L3A_inwT{i}", [1024, 4096], F32)
        OW = C.din(f"L3A_owT{i}", [2048, 1024], F32)
        precast(INW, inw_scr[i], f"l3a_inwscr{i}", 1024, 4096, f"pci{i}")
        precast(OW, ow_scr[i], f"l3a_owscr{i}", 2048, 1024, f"pco{i}")
    precast(FW2, fw2_scr, "l3a_fw2scr", 4096, 1024, "pcf")

    for i in range(2):
        CW = C.din(f"L3A_cw{i}", [128, 64], F32)
        CB = C.din(f"L3A_cb{i}", [128, 16], F32)
        DTB = C.din(f"L3A_dtb{i}", [128, 16], F32)
        DD = C.din(f"L3A_D{i}", [128, 16], F32)
        RLN = C.din(f"L3A_rln{i}", [128, 24], F32)
        ALOG = C.din(f"L3A_alog{i}", [128, 256], F32)
        XPW = C.din(f"L3A_xpwT{i}", [2048, 96], F32)
        DTW = C.din(f"L3A_dtwT{i}", [64, 2048], F32)

        S.add("sync", lambda e, CW=CW: e.dma_start(out=sm[:, 0:64], in_=CW),
              writes=[("l3a_sm", (0, 64))], tag=f"ldcw{i}")
        S.add("sync", lambda e, CB=CB: e.dma_start(out=sm[:, 64:80], in_=CB),
              writes=[("l3a_sm", (64, 80))], tag=f"ldcb{i}")
        S.add("sync", lambda e, DTB=DTB: e.dma_start(out=sm[:, 80:96], in_=DTB),
              writes=[("l3a_sm", (80, 96))], tag=f"lddtb{i}")
        S.add("sync", lambda e, DD=DD: e.dma_start(out=sm[:, 96:112], in_=DD),
              writes=[("l3a_sm", (96, 112))], tag=f"ldD{i}")
        S.add("sync", lambda e, RLN=RLN: e.dma_start(out=sm[:, 112:136], in_=RLN),
              writes=[("l3a_sm", (112, 136))], tag=f"ldrln{i}")
        S.add("sync", lambda e, ALOG=ALOG: e.dma_start(out=acol[:, :], in_=ALOG),
              writes=[("l3a_ac", None)], tag=f"ldal{i}")
        S.add("gpsimd", lambda e, XPW=XPW: e.dma_start(
            out=xpsb[:, :].rearrange("p (c f) -> p c f", c=16),
            in_=XPW.rearrange("(c p) f -> p c f", p=128)),
            writes=[("l3a_xpw", None)], weight=16, tag=f"ldxpw{i}")
        S.add("gpsimd", lambda e, DTW=DTW: e.dma_start(out=dtsb[:, :], in_=DTW),
              writes=[("l3a_dtw", None)], weight=16, tag=f"lddtw{i}")
        S.add("scalar", lambda e: e.activation(acol[:, :], acol[:, :], AF.Exp),
              reads=[("l3a_ac", None)], writes=[("l3a_ac", None)], tag=f"aexp{i}")
        S.add("vector", lambda e: e.tensor_scalar_mul(acol[:, :], acol[:, :], -1.0),
              reads=[("l3a_ac", None)], writes=[("l3a_ac", None)], tag=f"aneg{i}")
        S.add("vector", lambda e: e.memset(hcar[:, :], 0.0),
              writes=[("l3a_hc", None)], tag=f"hc0{i}")
        S.add("vector", lambda e: e.memset(xcar[:, :], 0.0),
              writes=[("l3a_xcar", None)], tag=f"xc0{i}")

        rmsw = [sm[:, 112 + k:113 + k] for k in range(8)]
        lnw = [sm[:, 120 + k:121 + k] for k in range(8)]
        lnbv = [sm[:, 128 + k:129 + k] for k in range(8)]

        for n in range(NT):
            tsl = (PAD + n * TTW, PAD + (n + 1) * TTW)
            if i == 0:
                def src(k, n=n):
                    return (C.xres[k][:, PAD + n * TTW:PAD + (n + 1) * TTW],
                            [(f"xres{k}", tsl)])
            else:
                S.add("sync", lambda e, n=n: e.dma_start(
                    out=gap(O_H0T, O_H0T + 4096).rearrange("p (c f) -> p c f", c=8),
                    in_=h_scr[0][:, n * TTW:(n + 1) * TTW].rearrange("(c p) f -> p c f", p=128)),
                    reads=[("l3a_hscr0", (0, 8))],
                    writes=[grg(O_H0T, O_H0T + 4096)], tag=f"ldh0_{n}")
                for k in range(DC):
                    S.add("vector", lambda e, k=k, n=n: e.tensor_tensor(
                        rb[:, k * 512:(k + 1) * 512],
                        C.xres[k][:, PAD + n * TTW:PAD + (n + 1) * TTW],
                        gap(O_H0T + k * 512, O_H0T + (k + 1) * 512), ALU.add),
                        reads=[(f"xres{k}", tsl), grg(O_H0T + k * 512, O_H0T + (k + 1) * 512)],
                        writes=[(K_RB, (k * 512, (k + 1) * 512))], tag=f"res1_{n}_{k}")
                def src(k, n=n):
                    return (rb[:, k * 512:(k + 1) * 512], [(K_RB, (k * 512, (k + 1) * 512))])

            def nout(k):
                return (gap(O_NRM + k * 512, O_NRM + (k + 1) * 512),
                        grg(O_NRM + k * 512, O_NRM + (k + 1) * 512))
            emit_ln_core(C, src, nout, gamma=rmsw, rms=True,
                         gb_reads=[("l3a_sm", (112, 120))], tag=f"m{i}n{n}_")
            for ep in range(16):
                S.add("sync", lambda e, ep=ep, i=i: e.dma_start(
                    out=gap(O_WCL, O_WCL + 2048).rearrange("p (c f) -> p c f", c=8),
                    in_=inw_scr[i][:, ep * 256:(ep + 1) * 256].rearrange("(c p) f -> p c f", p=128)),
                    reads=[(f"l3a_inwscr{i}", (0, 8))],
                    writes=[grg(O_WCL, O_WCL + 2048)], tag=f"ldinw{i}_{n}_{ep}")
                for c2 in range(2):
                    eci = ep * 2 + c2
                    bk = eci % 2
                    def mmi(e, c2=c2, bk=bk):
                        ins = None
                        for k in range(DC):
                            ins = e.matmul(bank(C, bk),
                                           gap(O_WCL + k * 256 + c2 * 128,
                                               O_WCL + k * 256 + (c2 + 1) * 128),
                                           gap(O_NRM + k * 512, O_NRM + (k + 1) * 512),
                                           start=(k == 0), stop=(k == DC - 1))
                        return ins
                    S.add("tensor", mmi,
                          reads=[grg(O_WCL, O_WCL + 2048), grg(O_NRM, O_NRM + 4096)],
                          writes=[("PS", (bk * 512, (bk + 1) * 512))],
                          tag=f"mmi{i}_{n}_{eci}")
                    if eci < 16:
                        dst = (O_XIN + eci * 515 + 3, O_XIN + eci * 515 + 515)
                    else:
                        dst = (O_ZST + (eci - 16) * 512, O_ZST + (eci - 15) * 512)
                    S.add("vector", lambda e, dst=dst, bk=bk: e.tensor_copy(
                        gap(dst[0], dst[1]), bank(C, bk)),
                        reads=[("PS", (bk * 512, (bk + 1) * 512))],
                        writes=[grg(dst[0], dst[1])], tag=f"ev{i}_{n}_{eci}")
            xin3 = gap(O_XIN, O_XIN + 16 * 515).rearrange("p (c w) -> p c w", c=16)
            S.add("vector", lambda e: e.tensor_copy(xin3[:, :, 0:3],
                                                    xcar[:, :].rearrange("p (c w) -> p c w", c=16)),
                  reads=[("l3a_xcar", None)], writes=[grg(O_XIN, O_XIN + 16 * 515)],
                  tag=f"cin{i}_{n}")
            for ec in range(EC):
                base = O_XIN + ec * 515
                ct = gap(O_SEG, O_SEG + 512)
                S.add("vector", lambda e, base=base, ec=ec: e.tensor_single_scalar(
                    ct, gap(base + 0, base + 512), sm[:, ec * 4:ec * 4 + 1], ALU.mult),
                    reads=[grg(base, base + 515), ("l3a_sm", (0, 64))],
                    writes=[grg(O_SEG, O_SEG + 512)], tag=f"cv{i}_{n}_{ec}_0")
                for kk in range(1, 4):
                    S.add("vector", lambda e, base=base, kk=kk, ec=ec: e.scalar_tensor_tensor(
                        ct, gap(base + kk, base + kk + 512), sm[:, ec * 4 + kk:ec * 4 + kk + 1],
                        ct, ALU.mult, ALU.add),
                        reads=[grg(base, base + 515), ("l3a_sm", (0, 64)),
                               grg(O_SEG, O_SEG + 512)],
                        writes=[grg(O_SEG, O_SEG + 512)], tag=f"cv{i}_{n}_{ec}_{kk}")
                S.add("scalar", lambda e, ec=ec: e.activation(
                    gap(O_XCS + ec * 512, O_XCS + (ec + 1) * 512), ct, AF.Silu,
                    bias=sm[:, 64 + ec:65 + ec]),
                    reads=[grg(O_SEG, O_SEG + 512), ("l3a_sm", (64, 80))],
                    writes=[grg(O_XCS + ec * 512, O_XCS + (ec + 1) * 512)],
                    tag=f"cs{i}_{n}_{ec}")
            S.add("vector", lambda e: e.tensor_copy(xcar[:, :].rearrange("p (c w) -> p c w", c=16),
                                                    xin3[:, :, 512:515]),
                  reads=[grg(O_XIN, O_XIN + 16 * 515)], writes=[("l3a_xcar", None)],
                  tag=f"cout{i}_{n}")
            for ec in range(EC):
                def mmxd(e, ec=ec):
                    e.matmul(C.PS[0:64, 2 * 512:3 * 512], xpsb[:, ec * 96:ec * 96 + 64],
                             gap(O_XCS + ec * 512, O_XCS + (ec + 1) * 512),
                             start=(ec == 0), stop=(ec == EC - 1))
                    e.matmul(C.PS[0:16, 3 * 512:4 * 512], xpsb[:, ec * 96 + 64:ec * 96 + 80],
                             gap(O_XCS + ec * 512, O_XCS + (ec + 1) * 512),
                             start=(ec == 0), stop=(ec == EC - 1))
                    return e.matmul(C.PS[0:16, 4 * 512:5 * 512],
                                    xpsb[:, ec * 96 + 80:ec * 96 + 96],
                                    gap(O_XCS + ec * 512, O_XCS + (ec + 1) * 512),
                                    start=(ec == 0), stop=(ec == EC - 1))
                S.add("tensor", mmxd,
                    reads=[("l3a_xpw", None), grg(O_XCS + ec * 512, O_XCS + (ec + 1) * 512)],
                    writes=[("PS", (2 * 512, 5 * 512))], tag=f"xd{i}_{n}_{ec}")
            S.add("vector", lambda e: e.tensor_copy(dts[:, :], C.PS[0:64, 2 * 512:3 * 512]),
                  reads=[("PS", (2 * 512, 3 * 512))], writes=[("l3a_dts", None)],
                  tag=f"dtc{i}_{n}")
            S.add("vector", lambda e: e.tensor_copy(bcur[:, :], C.PS[0:16, 3 * 512:4 * 512]),
                  reads=[("PS", (3 * 512, 4 * 512))], writes=[("l3a_bcu", None)],
                  tag=f"bcc{i}_{n}")
            S.add("vector", lambda e: e.tensor_copy(ccur[:, :], C.PS[0:16, 4 * 512:5 * 512]),
                  reads=[("PS", (4 * 512, 5 * 512))], writes=[("l3a_ccu", None)],
                  tag=f"ccc{i}_{n}")
            for ec in range(EC):
                S.add("tensor", lambda e, ec=ec: e.matmul(
                    C.PS[:, 3 * 512:4 * 512], dtsb[:, ec * 128:(ec + 1) * 128],
                    dts[:, :], start=True, stop=True),
                    reads=[("l3a_dtw", None), ("l3a_dts", None)],
                    writes=[("PS", (3 * 512, 4 * 512))], tag=f"dmm{i}_{n}_{ec}")
                t1 = rb[:, 0:512]
                t2 = rb[:, 512:1024]
                S.add("scalar", lambda e, ec=ec: e.activation(
                    t2, C.PS[:, 3 * 512:4 * 512], AF.Abs, bias=sm[:, 80 + ec:81 + ec]),
                    reads=[("PS", (3 * 512, 4 * 512)), ("l3a_sm", (80, 96))],
                    writes=[(K_RB, (512, 1024))], tag=f"sp1{i}_{n}_{ec}")
                S.add("scalar", lambda e: e.activation(t2, t2, AF.Exp, scale=-1.0),
                      reads=[(K_RB, (512, 1024)), fcm1], writes=[(K_RB, (512, 1024))],
                      tag=f"sp2{i}_{n}_{ec}")
                S.add("scalar", lambda e: e.activation(t2, t2, AF.Ln, bias=1.0),
                      reads=[(K_RB, (512, 1024)), fc1], writes=[(K_RB, (512, 1024))],
                      tag=f"sp3{i}_{n}_{ec}")
                S.add("scalar", lambda e, ec=ec: e.activation(
                    t1, C.PS[:, 3 * 512:4 * 512], AF.Relu, bias=sm[:, 80 + ec:81 + ec]),
                    reads=[("PS", (3 * 512, 4 * 512)), ("l3a_sm", (80, 96))],
                    writes=[(K_RB, (0, 512))], tag=f"sp4{i}_{n}_{ec}")
                S.add("vector", lambda e, ec=ec: e.tensor_tensor(
                    gap(O_DLS + ec * 512, O_DLS + (ec + 1) * 512), t1, t2, ALU.add),
                    reads=[(K_RB, (0, 1024))],
                    writes=[grg(O_DLS + ec * 512, O_DLS + (ec + 1) * 512)],
                    tag=f"sp5{i}_{n}_{ec}")
            S.add("vector", lambda e: e.tensor_tensor(
                gap(O_DUS, O_DUS + 8192), gap(O_DLS, O_DLS + 8192),
                gap(O_XCS, O_XCS + 8192), ALU.mult),
                reads=[grg(O_DLS, O_DLS + 8192), grg(O_XCS, O_XCS + 8192)],
                writes=[grg(O_DUS, O_DUS + 8192)], tag=f"du{i}_{n}")
            S.add("scalar", lambda e: e.activation(gap(O_ZST, O_ZST + 8192),
                                                   gap(O_ZST, O_ZST + 8192), AF.Silu),
                  reads=[grg(O_ZST, O_ZST + 8192)], writes=[grg(O_ZST, O_ZST + 8192)],
                  tag=f"zs{i}_{n}")
            for nst in range(NST):
                S.add("tensor", lambda e, nst=nst: e.matmul(
                    C.PS[:, 2 * 512:3 * 512], sel16[:, nst * 128:(nst + 1) * 128],
                    bcur[:, :], start=True, stop=True),
                    reads=[("l3a_sel", None), ("l3a_bcu", None)],
                    writes=[("PS", (2 * 512, 3 * 512))], tag=f"bb{i}_{n}_{nst}")
                S.add("vector", lambda e: e.tensor_copy(gap(O_BCB, O_BCB + 512),
                                                        C.PS[:, 2 * 512:3 * 512]),
                      reads=[("PS", (2 * 512, 3 * 512))], writes=[grg(O_BCB, O_BCB + 512)],
                      tag=f"bbc{i}_{n}_{nst}")
                S.add("tensor", lambda e, nst=nst: e.matmul(
                    C.PS[:, 2 * 512:3 * 512], sel16[:, nst * 128:(nst + 1) * 128],
                    ccur[:, :], start=True, stop=True),
                    reads=[("l3a_sel", None), ("l3a_ccu", None)],
                    writes=[("PS", (2 * 512, 3 * 512))], tag=f"cb{i}_{n}_{nst}")
                S.add("vector", lambda e: e.tensor_copy(gap(O_BCB + 512, O_BCB + 1024),
                                                        C.PS[:, 2 * 512:3 * 512]),
                      reads=[("PS", (2 * 512, 3 * 512))],
                      writes=[grg(O_BCB + 512, O_BCB + 1024)], tag=f"cbc{i}_{n}_{nst}")
                for ec in range(EC):
                    cidx = ec * 16 + nst
                    dA = gap(O_SEG, O_SEG + 512)
                    dB = gap(O_SEG + 512, O_SEG + 1024)
                    hh = gap(O_SEG + 1024, O_SEG + 1536)
                    hC = gap(O_SEG + 1536, O_SEG + 2048)
                    S.add("scalar", lambda e, ec=ec, cidx=cidx: e.activation(
                        dA, gap(O_DLS + ec * 512, O_DLS + (ec + 1) * 512), AF.Exp,
                        scale=acol[:, cidx:cidx + 1]),
                        reads=[grg(O_DLS + ec * 512, O_DLS + (ec + 1) * 512),
                               ("l3a_ac", (cidx, cidx + 1))],
                        writes=[grg(O_SEG, O_SEG + 512)], tag=f"dA{i}_{n}_{nst}_{ec}")
                    S.add("vector", lambda e, ec=ec: e.tensor_tensor(
                        dB, gap(O_DUS + ec * 512, O_DUS + (ec + 1) * 512),
                        gap(O_BCB, O_BCB + 512), ALU.mult),
                        reads=[grg(O_DUS + ec * 512, O_DUS + (ec + 1) * 512),
                               grg(O_BCB, O_BCB + 512)],
                        writes=[grg(O_SEG + 512, O_SEG + 1024)], tag=f"dB{i}_{n}_{nst}_{ec}")
                    S.add("vector", lambda e, cidx=cidx: e.tensor_tensor_scan(
                        hh, dA, dB, hcar[:, cidx:cidx + 1], ALU.mult, ALU.add),
                        reads=[grg(O_SEG, O_SEG + 1024), ("l3a_hc", (cidx, cidx + 1))],
                        writes=[grg(O_SEG + 1024, O_SEG + 1536)],
                        tag=f"sc{i}_{n}_{nst}_{ec}")
                    S.add("vector", lambda e, cidx=cidx: e.tensor_copy(
                        hcar[:, cidx:cidx + 1], hh[:, 511:512]),
                        reads=[grg(O_SEG + 1024, O_SEG + 1536)],
                        writes=[("l3a_hc", (cidx, cidx + 1))], tag=f"hc{i}_{n}_{nst}_{ec}")
                    S.add("vector", lambda e: e.tensor_tensor(
                        hC, hh, gap(O_BCB + 512, O_BCB + 1024), ALU.mult),
                        reads=[grg(O_SEG + 1024, O_SEG + 1536), grg(O_BCB + 512, O_BCB + 1024)],
                        writes=[grg(O_SEG + 1536, O_SEG + 2048)], tag=f"hC{i}_{n}_{nst}_{ec}")
                    if nst == 0:
                        S.add("vector", lambda e, ec=ec: e.tensor_copy(
                            gap(O_YST + ec * 512, O_YST + (ec + 1) * 512), hC),
                            reads=[grg(O_SEG + 1536, O_SEG + 2048)],
                            writes=[grg(O_YST + ec * 512, O_YST + (ec + 1) * 512)],
                            tag=f"y0{i}_{n}_{nst}_{ec}")
                    else:
                        S.add("vector", lambda e, ec=ec: e.tensor_tensor(
                            gap(O_YST + ec * 512, O_YST + (ec + 1) * 512),
                            gap(O_YST + ec * 512, O_YST + (ec + 1) * 512), hC, ALU.add),
                            reads=[grg(O_YST + ec * 512, O_YST + (ec + 1) * 512),
                                   grg(O_SEG + 1536, O_SEG + 2048)],
                            writes=[grg(O_YST + ec * 512, O_YST + (ec + 1) * 512)],
                            tag=f"ya{i}_{n}_{nst}_{ec}")
            for ec in range(EC):
                S.add("vector", lambda e, ec=ec: e.scalar_tensor_tensor(
                    gap(O_YST + ec * 512, O_YST + (ec + 1) * 512),
                    gap(O_XCS + ec * 512, O_XCS + (ec + 1) * 512),
                    sm[:, 96 + ec:97 + ec],
                    gap(O_YST + ec * 512, O_YST + (ec + 1) * 512), ALU.mult, ALU.add),
                    reads=[grg(O_XCS + ec * 512, O_XCS + (ec + 1) * 512),
                           ("l3a_sm", (96, 112)),
                           grg(O_YST + ec * 512, O_YST + (ec + 1) * 512)],
                    writes=[grg(O_YST + ec * 512, O_YST + (ec + 1) * 512)],
                    tag=f"y2{i}_{n}_{ec}")
                S.add("vector", lambda e, ec=ec: e.tensor_tensor(
                    gap(O_YST + ec * 512, O_YST + (ec + 1) * 512),
                    gap(O_YST + ec * 512, O_YST + (ec + 1) * 512),
                    gap(O_ZST + ec * 512, O_ZST + (ec + 1) * 512), ALU.mult),
                    reads=[grg(O_YST + ec * 512, O_YST + (ec + 1) * 512),
                           grg(O_ZST + ec * 512, O_ZST + (ec + 1) * 512)],
                    writes=[grg(O_YST + ec * 512, O_YST + (ec + 1) * 512)],
                    tag=f"yg{i}_{n}_{ec}")
            for g in range(4):
                S.add("sync", lambda e, g=g, i=i: e.dma_start(
                    out=gap(O_WO2, O_WO2 + 4096).rearrange("p (c f) -> p c f", c=4),
                    in_=ow_scr[i][g * 512:(g + 1) * 512, :].rearrange("(c p) f -> p c f", p=128)),
                    reads=[(f"l3a_owscr{i}", (0, 16))],
                    writes=[grg(O_WO2, O_WO2 + 4096)], tag=f"ldow{i}_{n}_{g}")
                for mo in range(DC):
                    def mmo(e, g=g, mo=mo):
                        ins = None
                        for c in range(4):
                            ins = e.matmul(bank(C, mo),
                                           gap(O_WO2 + c * 1024 + mo * 128,
                                               O_WO2 + c * 1024 + (mo + 1) * 128),
                                           gap(O_YST + (g * 4 + c) * 512,
                                               O_YST + (g * 4 + c + 1) * 512),
                                           start=(g == 0 and c == 0),
                                           stop=(g == 3 and c == 3))
                        return ins
                    S.add("tensor", mmo,
                          reads=[grg(O_WO2, O_WO2 + 4096), grg(O_YST, O_YST + 8192)],
                          writes=[("PS", (mo * 512, (mo + 1) * 512))],
                          tag=f"op{i}_{n}_{g}_{mo}")
            for mo in range(DC):
                S.add("vector", lambda e, mo=mo: e.tensor_copy(
                    rb[:, mo * 512:(mo + 1) * 512], bank(C, mo)),
                    reads=[("PS", (mo * 512, (mo + 1) * 512))],
                    writes=[(K_RB, (mo * 512, (mo + 1) * 512))], tag=f"oe{i}_{n}_{mo}")
            def src2(k):
                return (rb[:, k * 512:(k + 1) * 512], [(K_RB, (k * 512, (k + 1) * 512))])
            def hout(k):
                return (gap(O_HOS + k * 512, O_HOS + (k + 1) * 512),
                        grg(O_HOS + k * 512, O_HOS + (k + 1) * 512))
            emit_ln_core(C, src2, hout, gamma=lnw, beta=lnbv,
                         gb_reads=[("l3a_sm", (120, 136))], tag=f"o{i}n{n}_")
            S.add("sync", lambda e, n=n, i=i: e.dma_start(
                out=h_scr[i][:, n * TTW:(n + 1) * TTW].rearrange("(c p) f -> p c f", p=128),
                in_=gap(O_HOS, O_HOS + 4096).rearrange("p (c f) -> p c f", c=8)),
                reads=[grg(O_HOS, O_HOS + 4096)], writes=[(f"l3a_hscr{i}", (0, 8))],
                tag=f"sth{i}_{n}")

    S.add("sync", lambda e: e.dma_start(out=sm[:, 136:200], in_=FB1),
          writes=[("l3a_sm", (136, 200))], tag="ldfb1")
    S.add("sync", lambda e: e.dma_start(out=sm[:, 200:208], in_=FB2),
          writes=[("l3a_sm", (200, 208))], tag="ldfb2")
    for g in range(2):
        S.add("sync", lambda e, g=g: e.dma_start(
            out=gap(O_H1R + g * 16384, O_H1R + (g + 1) * 16384).rearrange("p (c f) -> p c f", c=4),
            in_=h_scr[1][g * 512:(g + 1) * 512, :].rearrange("(c p) f -> p c f", p=128)),
            reads=[("l3a_hscr1", (0, 8))],
            writes=[grg(O_H1R + g * 16384, O_H1R + (g + 1) * 16384)], tag=f"ldh1_{g}")
    for fc in range(64):
        S.add("gpsimd", lambda e, fc=fc: e.dma_start(
            out=gap(O_FWC, O_FWC + 1024).rearrange("p (c f) -> p c f", c=8),
            in_=FW1[:, fc * 128:(fc + 1) * 128].rearrange("(c p) f -> p c f", p=128)),
            writes=[grg(O_FWC, O_FWC + 1024)], weight=16, tag=f"ldfw1_{fc}")
        for n in range(NT):
            bk = n % 2
            def mmf(e, fc=fc, n=n, bk=bk):
                ins = None
                for k in range(DC):
                    ins = e.matmul(bank(C, bk),
                                   gap(O_FWC + k * 128, O_FWC + (k + 1) * 128),
                                   gap(O_H1R + k * 4096 + n * 512, O_H1R + k * 4096 + (n + 1) * 512),
                                   start=(k == 0), stop=(k == DC - 1))
                return ins
            S.add("tensor", mmf,
                  reads=[grg(O_FWC, O_FWC + 1024), grg(O_H1R, O_H1R + 32768)],
                  writes=[("PS", (bk * 512, (bk + 1) * 512))], tag=f"ff1_{fc}_{n}")
            if fc < 32:
                S.add("vector", lambda e, fc=fc, n=n, bk=bk: e.tensor_scalar_add(
                    gap(O_AST + n * 512, O_AST + (n + 1) * 512), bank(C, bk),
                    sm[:, 136 + fc:137 + fc]),
                    reads=[("PS", (bk * 512, (bk + 1) * 512)), ("l3a_sm", (136, 200))],
                    writes=[grg(O_AST + n * 512, O_AST + (n + 1) * 512)],
                    tag=f"ae{fc}_{n}")
            else:
                S.add("scalar", lambda e, fc=fc, n=n, bk=bk: e.activation(
                    gap(O_GST + n * 512, O_GST + (n + 1) * 512), bank(C, bk), AF.Gelu,
                    bias=sm[:, 136 + fc:137 + fc]),
                    reads=[("PS", (bk * 512, (bk + 1) * 512)), ("l3a_sm", (136, 200))],
                    writes=[grg(O_GST + n * 512, O_GST + (n + 1) * 512)],
                    tag=f"ge{fc}_{n}")
        if fc < 32:
            S.add("sync", lambda e, fc=fc: e.dma_start(
                out=au_scr[fc * 128:(fc + 1) * 128, :], in_=gap(O_AST, O_AST + 4096)),
                reads=[grg(O_AST, O_AST + 4096)], writes=[("l3a_auscr", (fc, fc + 1))],
                tag=f"stau{fc}")
        else:
            S.add("sync", lambda e, fc=fc: e.dma_start(
                out=gap(O_ALD, O_ALD + 4096), in_=au_scr[(fc - 32) * 128:(fc - 31) * 128, :]),
                reads=[("l3a_auscr", (fc - 32, fc - 31))],
                writes=[grg(O_ALD, O_ALD + 4096)], tag=f"ldau{fc}")
            S.add("vector", lambda e: e.tensor_tensor(
                gap(O_VST, O_VST + 4096), gap(O_ALD, O_ALD + 4096),
                gap(O_GST, O_GST + 4096), ALU.mult),
                reads=[grg(O_ALD, O_ALD + 4096), grg(O_GST, O_GST + 4096)],
                writes=[grg(O_VST, O_VST + 4096)], tag=f"vg{fc}")
            S.add("sync", lambda e, fc=fc: e.dma_start(
                out=v_scr[(fc - 32) * 128:(fc - 31) * 128, :], in_=gap(O_VST, O_VST + 4096)),
                reads=[grg(O_VST, O_VST + 4096)], writes=[("l3a_vscr", (fc - 32, fc - 31))],
                tag=f"stv{fc}")
    for n in range(NT):
        tsl = (PAD + n * TTW, PAD + (n + 1) * TTW)
        for g in range(4):
            S.add("sync", lambda e, g=g: e.dma_start(
                out=gap(O_FW2, O_FW2 + 8192).rearrange("p (c f) -> p c f", c=8),
                in_=fw2_scr[g * 1024:(g + 1) * 1024, :].rearrange("(c p) f -> p c f", p=128)),
                reads=[("l3a_fw2scr", (0, 32))],
                writes=[grg(O_FW2, O_FW2 + 8192)], tag=f"ldfw2_{n}_{g}")
            S.add("sync", lambda e, g=g, n=n: e.dma_start(
                out=gap(O_VGR, O_VGR + 4096).rearrange("p (c f) -> p c f", c=8),
                in_=v_scr[g * 1024:(g + 1) * 1024, n * 512:(n + 1) * 512].rearrange(
                    "(c p) f -> p c f", p=128)),
                reads=[("l3a_vscr", (g * 8, (g + 1) * 8))],
                writes=[grg(O_VGR, O_VGR + 4096)], tag=f"ldv_{n}_{g}")
            for mo in range(DC):
                def mmo2(e, g=g, mo=mo):
                    ins = None
                    for c in range(8):
                        ins = e.matmul(bank(C, mo),
                                       gap(O_FW2 + c * 1024 + mo * 128,
                                           O_FW2 + c * 1024 + (mo + 1) * 128),
                                       gap(O_VGR + c * 512, O_VGR + (c + 1) * 512),
                                       start=(g == 0 and c == 0), stop=(g == 3 and c == 7))
                    return ins
                S.add("tensor", mmo2,
                      reads=[grg(O_FW2, O_FW2 + 8192), grg(O_VGR, O_VGR + 4096)],
                      writes=[("PS", (mo * 512, (mo + 1) * 512))], tag=f"ff2_{n}_{g}_{mo}")
        for mo in range(DC):
            S.add("vector", lambda e, mo=mo, n=n: e.scalar_tensor_tensor(
                rb[:, mo * 512:(mo + 1) * 512], bank(C, mo), sm[:, 200 + mo:201 + mo],
                C.xres[mo][:, PAD + n * TTW:PAD + (n + 1) * TTW], ALU.add, ALU.add),
                reads=[("PS", (mo * 512, (mo + 1) * 512)), ("l3a_sm", (200, 208)),
                       (f"xres{mo}", tsl)],
                writes=[(K_RB, (mo * 512, (mo + 1) * 512))], tag=f"ffr{n}_{mo}")
        emit_postln_tile(C, n, rb, K_RB)





L0F_LIMIT = None


def emit_l0ffn(C):
    S = C.S
    H = 8

    ga = C.sbuf("l0ga", [128, 30720], BF16)
    K_GA = "l0ga"

    def gap(a, b):
        return ga[:, a:b]

    def grg(a, b):
        return (K_GA, (a, b))

    Q_HST = 0
    Q_ROT = 4104
    Q_STG = 6152
    Q_TMP = 12296
    Q_WCL = 14344
    Q_OUT = 15368
    A_XI = 0
    A_QW = 4104
    A_KW = 6152
    A_VW = 10248
    A_VT = 14344
    A_PN = 14856
    A_PTB = 15880
    A_OFM = 16904
    A_NST = 18952
    A_AST = 23048
    A_VOG = 23560
    A_VIG = 25608
    A_W2C = 27656
    A_WCL = 28680
    A_WOS = 23048

    msks = C.sbuf("l0f_msk", [128, 8 * 512], BF16)
    sml = C.sbuf("l0f_sm", [128, 72], F32)
    half2 = C.sbuf("l0f_h2", [128, 2], BF16)
    bc2 = C.sbuf("l0f_bc2", [2, 128], BF16)
    rr = C.sbuf("l0f_rr", [2, 512], BF16)
    rrf = C.sbuf("l0f_rrf", [2, 512], F32)
    mst = C.sbuf("l0f_mst", [128, 8], F32)
    xc2 = C.sbuf("l0f_xc2", [128, 4], BF16)
    rb = C.sbuf("l0f_rb", [128, 4096], F32)
    K_RB = "l0f_rb"

    MSK = C.din("c_l0f_masks", [128, 8 * 512], BF16)
    H2 = C.din("c_half2", [128, 2], BF16)
    BC2 = C.din("c_bc2", [2, 128], BF16)
    CSQ = C.din("c_l0f_csq", [128, T], BF16)
    SSQ = C.din("c_l0f_ssq", [128, T], BF16)
    CSK = C.din("c_l0f_csk", [128, T], BF16)
    SSK = C.din("c_l0f_ssk", [128, T], BF16)
    QS = [C.din(f"L0f_qs{l}", [128, 1], F32) for l in range(2)]
    KS = [C.din(f"L0f_ks{l}", [128, 1], F32) for l in range(2)]
    LG = [C.din(f"L0f_lg{l}", [128, 8], F32) for l in range(2)]
    LB = [C.din(f"L0f_lb{l}", [128, 8], F32) for l in range(2)]
    B1A = [C.din(f"L0f_b1a{l}", [128, 22], F32) for l in range(2)]
    B1G = [C.din(f"L0f_b1g{l}", [128, 22], F32) for l in range(2)]
    B2C = [C.din(f"L0f_b2{l}", [128, 8], F32) for l in range(2)]

    S.add("sync", lambda e: e.dma_start(out=msks[:, :], in_=MSK),
          writes=[("l0f_msk", None)], tag="ldmsk0")
    S.add("sync", lambda e: e.dma_start(out=half2[:, :], in_=H2),
          writes=[("l0f_h2", None)], tag="ldh2")
    S.add("sync", lambda e: e.dma_start(out=bc2[:, :], in_=BC2),
          writes=[("l0f_bc2", None)], tag="ldbc2")

    xint = C.dscratch("l0f_xint", [1024, 1 + T], BF16)
    q_scr = C.dscratch("l0f_q", [512, T], BF16)
    k_scr = C.dscratch("l0f_k", [512, T], BF16)
    v_scr = C.dscratch("l0f_v", [512, T], BF16)
    vf_scr = C.dscratch("l0f_vf", [2730, T], BF16)
    wqkv_scr = [C.dscratch(f"l0f_wqkv{l}", [1024, 1536], BF16) for l in range(2)]
    wo_scr = [C.dscratch(f"l0f_wo{l}", [512, 1024], BF16) for l in range(2)]
    w1a_scr = [C.dscratch(f"l0f_w1a{l}", [1024, 2730], BF16) for l in range(2)]
    w1g_scr = [C.dscratch(f"l0f_w1g{l}", [1024, 2730], BF16) for l in range(2)]
    w2_scr = [C.dscratch(f"l0f_w2{l}", [2730, 1024], BF16) for l in range(2)]

    def precast(src, scr, skey, rows, F, tagp):
        nch = (rows + 127) // 128
        for c in range(nch):
            r0 = c * 128
            r1 = min(rows, r0 + 128)
            S.add("gpsimd", lambda e, r0=r0, r1=r1: e.dma_start(
                out=gap(0, F)[0:r1 - r0, :], in_=src[r0:r1, :]),
                writes=[grg(0, F)], weight=16, tag=f"{tagp}c{c}")
            S.add("sync", lambda e, r0=r0, r1=r1: e.dma_start(
                out=scr[r0:r1, :], in_=gap(0, F)[0:r1 - r0, :]),
                reads=[grg(0, F)], writes=[(skey, (c, c + 1))], tag=f"{tagp}s{c}")

    for l in range(2):
        WQKV = C.din(f"L0f_wqkv{l}", [1024, 1536], F32)
        WO = C.din(f"L0f_wo{l}", [512, 1024], F32)
        W1A = C.din(f"L0f_w1a{l}", [1024, 2730], F32)
        W1G = C.din(f"L0f_w1g{l}", [1024, 2730], F32)
        W2 = C.din(f"L0f_w2{l}", [2730, 1024], F32)
        precast(WQKV, wqkv_scr[l], f"l0f_wqkvscr{l}", 1024, 1536, f"pq{l}")
        precast(WO, wo_scr[l], f"l0f_woscr{l}", 512, 1024, f"pw{l}")
        precast(W1A, w1a_scr[l], f"l0f_w1ascr{l}", 1024, 2730, f"pa{l}")
        precast(W1G, w1g_scr[l], f"l0f_w1gscr{l}", 1024, 2730, f"pg{l}")
        precast(W2, w2_scr[l], f"l0f_w2scr{l}", 2730, 1024, f"p2{l}")

    S.add("vector", lambda e: e.memset(xc2[:, 0:1], 0.0), writes=[("l0f_xc2", (0, 1))],
          tag="z0")
    S.add("sync", lambda e: e.dma_start(
        out=xint[:, 0:1].rearrange("(c p) f -> p c f", p=128),
        in_=xc2[:, 0:1].to_broadcast((128, 8, 1)) if False else xc2[:, 0:1]),
        reads=[("l0f_xc2", (0, 1))], writes=[("l0f_xintc", (0, 1))], tag="z1") if False else None
    for k in range(DC):
        S.add("sync", lambda e, k=k: e.dma_start(out=xint[k * 128:(k + 1) * 128, 0:1],
                                                 in_=xc2[:, 0:1],
                                                 allow_slow_non_contiguous=True),
              reads=[("l0f_xc2", (0, 1))], writes=[("l0f_xintz", (k, k + 1))], tag=f"z{k}")
    for k in range(DC):
        S.add("sync", lambda e, k=k: e.dma_start(out=xint[k * 128:(k + 1) * 128, 1:1 + T],
                                                 in_=C.xres[k][:, PAD:PAD + T]),
              reads=[(f"xres{k}", (PAD, PAD + T))], writes=[("l0f_xints", (k, k + 1))],
              tag=f"xi{k}")

    for l in range(1 if L0F_LIMIT in ("q0", "a0") else 2):
        S.add("vector", lambda e: e.memset(xc2[:, 0:4], 0.0),
              writes=[("l0f_xc2", (0, 4))], tag=f"xc2z{l}")
        S.add("sync", lambda e, l=l: e.dma_start(out=sml[:, 0:1], in_=QS[l]),
              writes=[("l0f_sm", (0, 1))], tag=f"ldqs{l}")
        S.add("sync", lambda e, l=l: e.dma_start(out=sml[:, 1:2], in_=KS[l]),
              writes=[("l0f_sm", (1, 2))], tag=f"ldks{l}")
        S.add("sync", lambda e, l=l: e.dma_start(out=sml[:, 2:10], in_=LG[l]),
              writes=[("l0f_sm", (2, 10))], tag=f"ldlg{l}")
        S.add("sync", lambda e, l=l: e.dma_start(out=sml[:, 10:18], in_=LB[l]),
              writes=[("l0f_sm", (10, 18))], tag=f"ldlb{l}")
        S.add("sync", lambda e, l=l: e.dma_start(out=sml[:, 18:40], in_=B1A[l]),
              writes=[("l0f_sm", (18, 40))], tag=f"ldb1a{l}")
        S.add("sync", lambda e, l=l: e.dma_start(out=sml[:, 40:62], in_=B1G[l]),
              writes=[("l0f_sm", (40, 62))], tag=f"ldb1g{l}")
        S.add("sync", lambda e, l=l: e.dma_start(out=sml[:, 62:70], in_=B2C[l]),
              writes=[("l0f_sm", (62, 70))], tag=f"ldb2{l}")

        for n in range(NT):
            S.add("sync", lambda e, n=n: e.dma_start(
                out=gap(Q_HST, Q_HST + 2048).rearrange("p (c f) -> p c f", c=4),
                in_=xint[0:512, 1 + n * 512:1 + (n + 1) * 512].rearrange("(c p) f -> p c f", p=128)),
                reads=[("l0f_xints", (0, 4)), ("l0f_xintz", (0, 4))],
                writes=[grg(Q_HST, Q_HST + 2048)], tag=f"hs{l}_{n}a")
            S.add("sync", lambda e, n=n: e.dma_start(
                out=gap(Q_HST + 2048, Q_HST + 4096).rearrange("p (c f) -> p c f", c=4),
                in_=xint[512:1024, n * 512:(n + 1) * 512].rearrange("(c p) f -> p c f", p=128)),
                reads=[("l0f_xints", (4, 8)), ("l0f_xintz", (4, 8))],
                writes=[grg(Q_HST + 2048, Q_HST + 4096)], tag=f"hs{l}_{n}b")
            for (tb, off) in ((CSQ, 0), (SSQ, 512), (CSK, 1024), (SSK, 1536)):
                S.add("sync", lambda e, tb=tb, off=off, n=n: e.dma_start(
                    out=gap(Q_ROT + off, Q_ROT + off + 512),
                    in_=tb[:, n * 512:(n + 1) * 512]),
                    writes=[grg(Q_ROT + off, Q_ROT + off + 512)], tag=f"rt{l}_{n}_{off}")
            for fc in range(12):
                S.add("sync", lambda e, fc=fc, l=l: e.dma_start(
                    out=gap(Q_WCL, Q_WCL + 1024).rearrange("p (c f) -> p c f", c=8),
                    in_=wqkv_scr[l][:, fc * 128:(fc + 1) * 128].rearrange("(c p) f -> p c f", p=128)),
                    reads=[(f"l0f_wqkvscr{l}", (0, 8))],
                    writes=[grg(Q_WCL, Q_WCL + 1024)], tag=f"qw{l}_{n}_{fc}")
                bk = fc % 2
                def mmq(e, bk=bk):
                    ins = None
                    for k in range(DC):
                        ins = e.matmul(bank(C, bk), gap(Q_WCL + k * 128, Q_WCL + (k + 1) * 128),
                                       gap(Q_HST + k * 512, Q_HST + (k + 1) * 512),
                                       start=(k == 0), stop=(k == DC - 1))
                    return ins
                S.add("tensor", mmq,
                      reads=[grg(Q_WCL, Q_WCL + 1024), grg(Q_HST, Q_HST + 4096)],
                      writes=[("PS", (bk * 512, (bk + 1) * 512))], tag=f"mq{l}_{n}_{fc}")
                st = Q_STG + (fc % 4) * 512
                S.add("vector", lambda e, st=st, bk=bk: e.tensor_copy(
                    gap(st, st + 512), bank(C, bk)),
                    reads=[("PS", (bk * 512, (bk + 1) * 512))],
                    writes=[grg(st, st + 512)], tag=f"qe{l}_{n}_{fc}")
                if fc >= 8:
                    S.add("sync", lambda e, fc=fc, n=n, st=st: e.dma_start(
                        out=v_scr[(fc - 8) * 128:(fc - 7) * 128, n * 512:(n + 1) * 512],
                        in_=gap(st, st + 512)),
                        reads=[grg(st, st + 512)], writes=[(f"l0f_vscr", ((fc - 8) * 8 + n, (fc - 8) * 8 + n + 1))],
                        tag=f"vs{l}_{n}_{fc}")
                    continue
                isq = fc < 4
                scol = sml[:, 0:1] if isq else sml[:, 1:2]
                srd = ("l0f_sm", (0, 1)) if isq else ("l0f_sm", (1, 2))
                cso = Q_ROT + (0 if isq else 1024)
                sso = Q_ROT + (512 if isq else 1536)
                S.add("scalar", lambda e, st=st: e.activation(gap(Q_TMP, Q_TMP + 512),
                                                              gap(st, st + 512), AF.Square),
                      reads=[grg(st, st + 512)], writes=[grg(Q_TMP, Q_TMP + 512)],
                      tag=f"sq{l}_{n}_{fc}")
                S.add("tensor", lambda e: e.matmul(C.PS[0:2, 2 * 512:3 * 512], half2[:, :],
                                                   gap(Q_TMP, Q_TMP + 512), start=True, stop=True),
                      reads=[grg(Q_TMP, Q_TMP + 512), ("l0f_h2", None)],
                      writes=[("PS", (2 * 512, 3 * 512))], tag=f"ss{l}_{n}_{fc}")
                S.add("scalar", lambda e: e.activation(rrf[:, :], C.PS[0:2, 2 * 512:3 * 512],
                                                       AF.Sqrt),
                      reads=[("PS", (2 * 512, 3 * 512))], writes=[("l0f_rrf", None)],
                      tag=f"sr{l}_{n}_{fc}")
                S.add("vector", lambda e: e.tensor_scalar_max(rrf[:, :], rrf[:, :], 1e-12),
                      reads=[("l0f_rrf", None)], writes=[("l0f_rrf", None)],
                      tag=f"sm{l}_{n}_{fc}")
                S.add("vector", lambda e: e.reciprocal(rrf[:, :], rrf[:, :]),
                      reads=[("l0f_rrf", None)], writes=[("l0f_rrf", None)],
                      tag=f"sc{l}_{n}_{fc}")
                S.add("vector", lambda e: e.tensor_copy(rr[:, :], rrf[:, :]),
                      reads=[("l0f_rrf", None)], writes=[("l0f_rr", None)],
                      tag=f"sc2{l}_{n}_{fc}")
                S.add("tensor", lambda e: e.matmul(C.PS[:, 2 * 512:3 * 512], bc2[:, :],
                                                   rr[:, :], start=True, stop=True),
                      reads=[("l0f_rr", None), ("l0f_bc2", None)],
                      writes=[("PS", (2 * 512, 3 * 512))], tag=f"sb{l}_{n}_{fc}")
                S.add("vector", lambda e, st=st, scol=scol, srd=srd: e.scalar_tensor_tensor(
                    gap(Q_TMP, Q_TMP + 512), gap(st, st + 512), scol,
                    C.PS[:, 2 * 512:3 * 512], ALU.mult, ALU.mult),
                    reads=[grg(st, st + 512), srd, ("PS", (2 * 512, 3 * 512))],
                    writes=[grg(Q_TMP, Q_TMP + 512)], tag=f"qn{l}_{n}_{fc}")
                for (o1, o2, sgn) in ((0, 32, -1.0), (32, 0, 1.0), (64, 96, -1.0), (96, 64, 1.0)):
                    S.add("vector", lambda e, o1=o1, o2=o2, sgn=sgn: e.tensor_scalar_mul(
                        gap(Q_TMP + 512, Q_TMP + 1024)[o1:o1 + 32, :],
                        gap(Q_TMP, Q_TMP + 512)[o2:o2 + 32, :], sgn),
                        reads=[grg(Q_TMP, Q_TMP + 512)],
                        writes=[grg(Q_TMP + 512, Q_TMP + 1024)], tag=f"rh{l}_{n}_{fc}_{o1}")
                S.add("vector", lambda e, cso=cso: e.tensor_tensor(
                    gap(Q_TMP, Q_TMP + 512), gap(Q_TMP, Q_TMP + 512),
                    gap(cso, cso + 512), ALU.mult),
                    reads=[grg(Q_TMP, Q_TMP + 512), grg(cso, cso + 512)],
                    writes=[grg(Q_TMP, Q_TMP + 512)], tag=f"rc{l}_{n}_{fc}")
                S.add("vector", lambda e, sso=sso: e.tensor_tensor(
                    gap(Q_TMP + 512, Q_TMP + 1024), gap(Q_TMP + 512, Q_TMP + 1024),
                    gap(sso, sso + 512), ALU.mult),
                    reads=[grg(Q_TMP + 512, Q_TMP + 1024), grg(sso, sso + 512)],
                    writes=[grg(Q_TMP + 512, Q_TMP + 1024)], tag=f"rs{l}_{n}_{fc}")
                oslot = Q_OUT + (fc % 4) * 512
                S.add("vector", lambda e, oslot=oslot: e.tensor_tensor(
                    gap(oslot, oslot + 512), gap(Q_TMP, Q_TMP + 512),
                    gap(Q_TMP + 512, Q_TMP + 1024), ALU.add),
                    reads=[grg(Q_TMP, Q_TMP + 1024)],
                    writes=[grg(oslot, oslot + 512)], tag=f"ro{l}_{n}_{fc}")
                scr = q_scr if isq else k_scr
                skey = "l0f_qscr" if isq else "l0f_kscr"
                S.add("sync", lambda e, fc=fc, n=n, oslot=oslot, scr=scr: e.dma_start(
                    out=scr[(fc % 4) * 128:(fc % 4 + 1) * 128, n * 512:(n + 1) * 512],
                    in_=gap(oslot, oslot + 512)),
                    reads=[grg(oslot, oslot + 512)],
                    writes=[(skey, ((fc % 4) * 8 + n, (fc % 4) * 8 + n + 1))],
                    tag=f"qs{l}_{n}_{fc}")

        if L0F_LIMIT == "q0" and l == 0:
            break
        for n in range(NT):
            w = n
            nkb = 2 if w > 0 else 1
            kv0 = (w - 1) * 512 if w > 0 else 0
            kvw = nkb * 512
            S.add("sync", lambda e, n=n: e.dma_start(
                out=gap(A_XI, A_XI + 4096).rearrange("p (c f) -> p c f", c=8),
                in_=xint[:, 1 + n * 512:1 + (n + 1) * 512].rearrange("(c p) f -> p c f", p=128)),
                reads=[("l0f_xints", (0, 8)), ("l0f_xintw", (0, 100))],
                writes=[grg(A_XI, A_XI + 4096)], tag=f"xl{l}_{n}")
            S.add("sync", lambda e, n=n: e.dma_start(
                out=gap(A_QW, A_QW + 2048).rearrange("p (c f) -> p c f", c=4),
                in_=q_scr[:, n * 512:(n + 1) * 512].rearrange("(c p) f -> p c f", p=128)),
                reads=[("l0f_qscr", None)], writes=[grg(A_QW, A_QW + 2048)], tag=f"ql{l}_{n}")
            S.add("sync", lambda e, kv0=kv0, kvw=kvw: e.dma_start(
                out=gap(A_KW, A_KW + 4 * kvw).rearrange("p (c f) -> p c f", c=4),
                in_=k_scr[:, kv0:kv0 + kvw].rearrange("(c p) f -> p c f", p=128)),
                reads=[("l0f_kscr", None)], writes=[grg(A_KW, A_KW + 4096)], tag=f"kl{l}_{n}")
            S.add("sync", lambda e, kv0=kv0, kvw=kvw: e.dma_start(
                out=gap(A_VW, A_VW + 4 * kvw).rearrange("p (c f) -> p c f", c=4),
                in_=v_scr[:, kv0:kv0 + kvw].rearrange("(c p) f -> p c f", p=128)),
                reads=[("l0f_vscr", None)], writes=[grg(A_VW, A_VW + 4096)], tag=f"vl{l}_{n}")
            if C.debug and l == 0 and n == 0:
                C.tap("qw0", gap(A_QW, A_QW + 2048), [grg(A_QW, A_QW + 2048)])
                C.tap("kw0", gap(A_KW, A_KW + 4096), [grg(A_KW, A_KW + 4096)])
                C.tap("vw0", gap(A_VW, A_VW + 4096), [grg(A_VW, A_VW + 4096)])
            psb = C.PS[:, 5 * 512:6 * 512].bitcast(BF16)
            for h in range(H):
                hc, ho = h // 2, (h % 2) * 64
                nsc = kvw // 128
                for tg in range((nsc + 3) // 4):
                    ns = min(4, nsc - tg * 4)
                    def vtr(e, tg=tg, ns=ns, hc=hc, ho=ho, kvw=kvw):
                        ins = None
                        for j in range(ns):
                            sc = tg * 4 + j
                            ins = e.transpose(psb[0:128, j * 64:(j + 1) * 64],
                                              gap(A_VW + hc * kvw + sc * 128,
                                                  A_VW + hc * kvw + (sc + 1) * 128)[ho:ho + 64, :],
                                              C.ident_bf[ho:ho + 64, ho:ho + 64])
                        return ins
                    S.add("tensor", vtr, reads=[grg(A_VW, A_VW + 4096), ("CONSTB", (128, 256))],
                          writes=[("PS", (5 * 512, 6 * 512))], tag=f"vt{l}_{n}_{h}_{tg}")
                    def vcp(e, tg=tg, ns=ns):
                        ins = None
                        for j in range(ns):
                            sc = tg * 4 + j
                            ins = e.tensor_copy(gap(A_VT + sc * 64, A_VT + (sc + 1) * 64),
                                                psb[0:128, j * 64:(j + 1) * 64])
                        return ins
                    S.add("vector", vcp, reads=[("PS", (5 * 512, 6 * 512))],
                          writes=[grg(A_VT + tg * 256, A_VT + (tg * 4 + ns) * 64)],
                          tag=f"vc{l}_{n}_{h}_{tg}")
                for qs in range(4):
                    def mms(e, qs=qs, hc=hc, ho=ho, nkb=nkb, kvw=kvw):
                        ins = None
                        for kb in range(nkb):
                            ins = e.matmul(bank(C, kb),
                                           gap(A_QW + hc * 512 + qs * 128,
                                               A_QW + hc * 512 + (qs + 1) * 128)[ho:ho + 64, :],
                                           gap(A_KW + hc * kvw + kb * 512,
                                               A_KW + hc * kvw + (kb + 1) * 512)[ho:ho + 64, :],
                                           start=True, stop=True)
                        return ins
                    S.add("tensor", mms,
                          reads=[grg(A_QW, A_QW + 2048), grg(A_KW, A_KW + 4096)],
                          writes=[("PS", (0, nkb * 512))], tag=f"S{l}_{n}_{h}_{qs}")
                    if w > 0:
                        S.add("vector", lambda e, qs=qs: e.tensor_tensor(
                            bank(C, 0), bank(C, 0), msks[:, qs * 512:(qs + 1) * 512], ALU.add),
                            reads=[("PS", (0, 512)), ("l0f_msk", None)],
                            writes=[("PS", (0, 512))], tag=f"mk0{l}_{n}_{h}_{qs}")
                    S.add("vector", lambda e, qs=qs, nkb=nkb: e.tensor_tensor(
                        bank(C, nkb - 1), bank(C, nkb - 1),
                        msks[:, (4 + qs) * 512:(5 + qs) * 512], ALU.add),
                        reads=[("PS", ((nkb - 1) * 512, nkb * 512)), ("l0f_msk", None)],
                        writes=[("PS", ((nkb - 1) * 512, nkb * 512))], tag=f"mk1{l}_{n}_{h}_{qs}")
                    S.add("vector", lambda e, nkb=nkb: e.reduce_max(
                        mst[:, 0:1], C.PS[:, 0:nkb * 512], axis=AX.X, negate=True),
                        reads=[("PS", (0, nkb * 512))], writes=[("l0f_mst", (0, 1))],
                        tag=f"nm{l}_{n}_{h}_{qs}")
                    S.add("scalar", lambda e, nkb=nkb: e.activation(
                        gap(A_PN, A_PN + nkb * 512), C.PS[:, 0:nkb * 512], AF.Exp,
                        bias=mst[:, 0:1], accum_out=mst[:, 1:2]),
                        reads=[("PS", (0, nkb * 512)), ("l0f_mst", (0, 1))],
                        writes=[grg(A_PN, A_PN + 1024), ("l0f_mst", (1, 2))],
                        tag=f"ex{l}_{n}_{h}_{qs}")
                    S.add("vector", lambda e: e.reciprocal(mst[:, 2:3], mst[:, 1:2]),
                          reads=[("l0f_mst", (1, 2))], writes=[("l0f_mst", (2, 3))],
                          tag=f"ri{l}_{n}_{h}_{qs}")
                    S.add("vector", lambda e, nkb=nkb: e.tensor_scalar_mul(
                        gap(A_PN, A_PN + nkb * 512), gap(A_PN, A_PN + nkb * 512), mst[:, 2:3]),
                        reads=[grg(A_PN, A_PN + 1024), ("l0f_mst", (2, 3))],
                        writes=[grg(A_PN, A_PN + 1024)], tag=f"nr{l}_{n}_{h}_{qs}")
                    if C.debug and l == 0 and n == 0 and h == 2 and qs == 0:
                        C.tap("pnb2", gap(A_PN, A_PN + 512), [grg(A_PN, A_PN + 1024)])
                    if C.debug and l == 0 and n == 0 and h == 2 and qs == 0:
                        C.tap("mst2", mst[:, 0:4], [("l0f_mst", (0, 4))])
                    def ptr(e, nkb=nkb):
                        ins = None
                        for sc in range(nkb * 4):
                            ins = e.transpose(psb[:, 512 + sc * 0:512] if False else
                                              psb[:, sc * 128 - sc * 128:128] if False else
                                              psb[:, 0:128] if False else
                                              psb[:, (sc % 4) * 128:(sc % 4 + 1) * 128],
                                              gap(A_PN + sc * 128, A_PN + (sc + 1) * 128),
                                              C.ident_bf)
                        return ins
                    for tg in range((nkb * 4 + 3) // 4):
                        ns = min(4, nkb * 4 - tg * 4)
                        def ptr2(e, tg=tg, ns=ns):
                            ins = None
                            for j in range(ns):
                                sc = tg * 4 + j
                                ins = e.transpose(psb[:, j * 128:(j + 1) * 128],
                                                  gap(A_PN + sc * 128, A_PN + (sc + 1) * 128),
                                                  C.ident_bf)
                            return ins
                        S.add("tensor", ptr2,
                              reads=[grg(A_PN, A_PN + 1024), ("CONSTB", (128, 256))],
                              writes=[("PS", (5 * 512, 6 * 512))], tag=f"pt{l}_{n}_{h}_{qs}_{tg}")
                        def pcp(e, tg=tg, ns=ns):
                            ins = None
                            for j in range(ns):
                                sc = tg * 4 + j
                                ins = e.tensor_copy(gap(A_PTB + sc * 128, A_PTB + (sc + 1) * 128),
                                                    psb[:, j * 128:(j + 1) * 128])
                            return ins
                        S.add("vector", pcp, reads=[("PS", (5 * 512, 6 * 512))],
                              writes=[grg(A_PTB + tg * 512, A_PTB + (tg * 4 + ns) * 128)],
                              tag=f"pc{l}_{n}_{h}_{qs}_{tg}")
                    def mmv(e, nkb=nkb):
                        ins = None
                        for sc in range(nkb * 4):
                            ins = e.matmul(C.PS[0:64, 4 * 512:4 * 512 + 128],
                                           gap(A_VT + sc * 64, A_VT + (sc + 1) * 64),
                                           gap(A_PTB + sc * 128, A_PTB + (sc + 1) * 128),
                                           start=(sc == 0), stop=(sc == nkb * 4 - 1))
                        return ins
                    S.add("tensor", mmv,
                          reads=[grg(A_VT, A_VT + 512), grg(A_PTB, A_PTB + 1024)],
                          writes=[("PS", (4 * 512, 5 * 512))], tag=f"pv{l}_{n}_{h}_{qs}")
                    S.add("vector", lambda e, qs=qs, hc=hc, ho=ho: e.tensor_copy(
                        gap(A_OFM + hc * 512 + qs * 128,
                            A_OFM + hc * 512 + (qs + 1) * 128)[ho:ho + 64, :],
                        C.PS[0:64, 4 * 512:4 * 512 + 128]),
                        reads=[("PS", (4 * 512, 5 * 512))],
                        writes=[grg(A_OFM + hc * 512, A_OFM + (hc + 1) * 512)],
                        tag=f"oe{l}_{n}_{h}_{qs}")
            S.add("sync", lambda e, l=l: e.dma_start(
                out=gap(A_WOS, A_WOS + 4096).rearrange("p (c f) -> p c f", c=4),
                in_=wo_scr[l].rearrange("(c p) f -> p c f", p=128)),
                reads=[(f"l0f_woscr{l}", (0, 4))], writes=[grg(A_WOS, A_WOS + 4096)],
                tag=f"wol{l}_{n}")
            for mo in range(DC):
                def mmw(e, mo=mo):
                    ins = None
                    for c in range(4):
                        ins = e.matmul(bank(C, mo),
                                       gap(A_WOS + c * 1024 + mo * 128,
                                           A_WOS + c * 1024 + (mo + 1) * 128),
                                       gap(A_OFM + c * 512, A_OFM + (c + 1) * 512),
                                       start=(c == 0), stop=(c == 3))
                    return ins
                S.add("tensor", mmw,
                      reads=[grg(A_WOS, A_WOS + 4096), grg(A_OFM, A_OFM + 2048)],
                      writes=[("PS", (mo * 512, (mo + 1) * 512))], tag=f"wo{l}_{n}_{mo}")
            for mo in range(DC):
                S.add("vector", lambda e, mo=mo: e.tensor_tensor(
                    gap(A_XI + mo * 512, A_XI + (mo + 1) * 512),
                    gap(A_XI + mo * 512, A_XI + (mo + 1) * 512), bank(C, mo), ALU.add),
                    reads=[grg(A_XI + mo * 512, A_XI + (mo + 1) * 512),
                           ("PS", (mo * 512, (mo + 1) * 512))],
                    writes=[grg(A_XI + mo * 512, A_XI + (mo + 1) * 512)],
                    tag=f"xr{l}_{n}_{mo}")
            if C.debug and l == 0 and n == 0:
                C.tap("xiattn0", gap(A_XI, A_XI + 4096), [grg(A_XI, A_XI + 4096)])
                C.tap("ofm0", gap(A_OFM, A_OFM + 2048), [grg(A_OFM, A_OFM + 2048)])
            for mo in range(DC):
                if mo < 4:
                    S.add("vector", lambda e, mo=mo: e.tensor_copy(
                        gap(A_NST + mo * 512, A_NST + (mo + 1) * 512),
                        gap(A_XI + mo * 512, A_XI + (mo + 1) * 512)),
                        reads=[grg(A_XI + mo * 512, A_XI + (mo + 1) * 512)],
                        writes=[grg(A_NST + mo * 512, A_NST + (mo + 1) * 512)],
                        tag=f"sh{l}_{n}_{mo}")
                else:
                    S.add("vector", lambda e, mo=mo: e.tensor_copy(
                        gap(A_NST + mo * 512 + 1, A_NST + (mo + 1) * 512),
                        gap(A_XI + mo * 512, A_XI + (mo + 1) * 512 - 1)),
                        reads=[grg(A_XI + mo * 512, A_XI + (mo + 1) * 512)],
                        writes=[grg(A_NST + mo * 512, A_NST + (mo + 1) * 512)],
                        tag=f"sh{l}_{n}_{mo}")
                    S.add("vector", lambda e, mo=mo: e.tensor_copy(
                        gap(A_NST + mo * 512, A_NST + mo * 512 + 1),
                        xc2[:, mo - 4:mo - 3]),
                        reads=[("l0f_xc2", (mo - 4, mo - 3))],
                        writes=[grg(A_NST + mo * 512, A_NST + mo * 512 + 1)],
                        tag=f"shc{l}_{n}_{mo}")
            for mo in range(4, 8):
                S.add("vector", lambda e, mo=mo: e.tensor_copy(
                    xc2[:, mo - 4:mo - 3], gap(A_XI + (mo + 1) * 512 - 1, A_XI + (mo + 1) * 512)),
                    reads=[grg(A_XI + (mo + 1) * 512 - 1, A_XI + (mo + 1) * 512)],
                    writes=[("l0f_xc2", (mo - 4, mo - 3))], tag=f"cs{l}_{n}_{mo}")
            lg = [sml[:, 2 + k:3 + k] for k in range(8)]
            lb = [sml[:, 10 + k:11 + k] for k in range(8)]
            def src(k):
                return (gap(A_NST + k * 512, A_NST + (k + 1) * 512),
                        [grg(A_NST + k * 512, A_NST + (k + 1) * 512)])
            def nout(k):
                return (gap(A_NST + k * 512, A_NST + (k + 1) * 512),
                        grg(A_NST + k * 512, A_NST + (k + 1) * 512))
            emit_ln_core(C, src, nout, gamma=lg, beta=lb,
                         gb_reads=[("l0f_sm", (2, 18))], tag=f"f{l}n{n}_")
            if C.debug and l == 0 and n == 0:
                C.tap("nst0", gap(A_NST, A_NST + 4096), [grg(A_NST, A_NST + 4096)])
            for j in range(22):
                jw = 128 if j < 21 else 42
                S.add("sync", lambda e, j=j, l=l, jw=jw: e.dma_start(
                    out=gap(A_WCL, A_WCL + 8 * jw).rearrange("p (c f) -> p c f", c=8),
                    in_=w1a_scr[l][:, j * 128:j * 128 + jw].rearrange("(c p) f -> p c f", p=128)),
                    reads=[(f"l0f_w1ascr{l}", (0, 8))], writes=[grg(A_WCL, A_WCL + 1024)],
                    tag=f"wa{l}_{n}_{j}")
                def mma(e, jw=jw):
                    ins = None
                    for k in range(DC):
                        ins = e.matmul(C.PS[0:jw, 0:512], gap(A_WCL + k * jw, A_WCL + (k + 1) * jw),
                                       gap(A_NST + k * 512, A_NST + (k + 1) * 512),
                                       start=(k == 0), stop=(k == DC - 1))
                    return ins
                S.add("tensor", mma, reads=[grg(A_WCL, A_WCL + 1024), grg(A_NST, A_NST + 4096)],
                      writes=[("PS", (0, 512))], tag=f"ma{l}_{n}_{j}")
                S.add("vector", lambda e, jw=jw, j=j: e.tensor_scalar_add(
                    gap(A_AST, A_AST + 512)[0:jw, :], C.PS[0:jw, 0:512],
                    sml[0:jw, 18 + j:19 + j]),
                      reads=[("PS", (0, 512)), ("l0f_sm", (18, 40))],
                      writes=[grg(A_AST, A_AST + 512)],
                      tag=f"ae{l}_{n}_{j}")
                S.add("sync", lambda e, j=j, l=l, jw=jw: e.dma_start(
                    out=gap(A_WCL, A_WCL + 8 * jw).rearrange("p (c f) -> p c f", c=8),
                    in_=w1g_scr[l][:, j * 128:j * 128 + jw].rearrange("(c p) f -> p c f", p=128)),
                    reads=[(f"l0f_w1gscr{l}", (0, 8))], writes=[grg(A_WCL, A_WCL + 1024)],
                    tag=f"wg{l}_{n}_{j}")
                S.add("tensor", mma, reads=[grg(A_WCL, A_WCL + 1024), grg(A_NST, A_NST + 4096)],
                      writes=[("PS", (0, 512))], tag=f"mg{l}_{n}_{j}")
                vslot = A_VOG + (j % 4) * 512
                S.add("scalar", lambda e, jw=jw, vslot=vslot, j=j: e.activation(
                    gap(vslot, vslot + 512)[0:jw, :], C.PS[0:jw, 0:512], AF.Gelu,
                    bias=sml[0:jw, 40 + j:41 + j]),
                    reads=[("PS", (0, 512)), ("l0f_sm", (40, 62))],
                    writes=[grg(vslot, vslot + 512)],
                    tag=f"gg{l}_{n}_{j}")
                S.add("vector", lambda e, jw=jw, vslot=vslot: e.tensor_tensor(
                    gap(vslot, vslot + 512)[0:jw, :], gap(A_AST, A_AST + 512)[0:jw, :],
                    gap(vslot, vslot + 512)[0:jw, :], ALU.mult),
                    reads=[grg(A_AST, A_AST + 512), grg(vslot, vslot + 512)],
                    writes=[grg(vslot, vslot + 512)], tag=f"vm{l}_{n}_{j}")
                if j % 4 == 3 and j < 20:
                    j0 = (j // 4) * 4
                    S.add("sync", lambda e, j0=j0, n=n: e.dma_start(
                        out=vf_scr[j0 * 128:(j0 + 4) * 128, n * 512:(n + 1) * 512].rearrange(
                            "(c p) f -> p c f", p=128),
                        in_=gap(A_VOG, A_VOG + 4 * 512).rearrange("p (c f) -> p c f", c=4)),
                        reads=[grg(A_VOG, A_VOG + 2048)],
                        writes=[("l0f_vfscr", (j0, j0 + 4))], tag=f"vf{l}_{n}_{j}")
                elif j == 21:
                    S.add("sync", lambda e, n=n: e.dma_start(
                        out=vf_scr[20 * 128:21 * 128, n * 512:(n + 1) * 512],
                        in_=gap(A_VOG, A_VOG + 512)),
                        reads=[grg(A_VOG, A_VOG + 2048)],
                        writes=[("l0f_vfscr", (20, 21))], tag=f"vf{l}_{n}_20")
                    S.add("sync", lambda e, n=n: e.dma_start(
                        out=vf_scr[21 * 128:21 * 128 + 42, n * 512:(n + 1) * 512],
                        in_=gap(A_VOG + 512, A_VOG + 1024)[0:42, :]),
                        reads=[grg(A_VOG, A_VOG + 2048)],
                        writes=[("l0f_vfscr", (21, 22))], tag=f"vf{l}_{n}_21")
            for j in range(22):
                jw = 128 if j < 21 else 42
                S.add("sync", lambda e, j=j, l=l, jw=jw: e.dma_start(
                    out=gap(A_W2C, A_W2C + 1024)[0:jw, :],
                    in_=w2_scr[l][j * 128:j * 128 + jw, :]),
                    reads=[(f"l0f_w2scr{l}", (j, j + 1))], writes=[grg(A_W2C, A_W2C + 1024)],
                    tag=f"w2l{l}_{n}_{j}")
                S.add("sync", lambda e, j=j, n=n, jw=jw: e.dma_start(
                    out=gap(A_VIG, A_VIG + 512)[0:jw, :],
                    in_=vf_scr[j * 128:j * 128 + jw, n * 512:(n + 1) * 512]),
                    reads=[("l0f_vfscr", (j, j + 1))], writes=[grg(A_VIG, A_VIG + 512)],
                    tag=f"v2l{l}_{n}_{j}")
                def mm2(e, j=j, jw=jw):
                    ins = None
                    for mo in range(DC):
                        ins = e.matmul(bank(C, mo),
                                       gap(A_W2C + mo * 128, A_W2C + (mo + 1) * 128)[0:jw, :],
                                       gap(A_VIG, A_VIG + 512)[0:jw, :],
                                       start=(j == 0), stop=(j == 21))
                    return ins
                S.add("tensor", mm2,
                      reads=[grg(A_W2C, A_W2C + 1024), grg(A_VIG, A_VIG + 512)],
                      writes=[("PS", (0, 4096))], tag=f"m2{l}_{n}_{j}")
            for mo in range(DC):
                S.add("vector", lambda e, mo=mo: e.scalar_tensor_tensor(
                    gap(A_XI + mo * 512, A_XI + (mo + 1) * 512), bank(C, mo),
                    sml[:, 62 + mo:63 + mo],
                    gap(A_XI + mo * 512, A_XI + (mo + 1) * 512), ALU.add, ALU.add),
                    reads=[grg(A_XI + mo * 512, A_XI + (mo + 1) * 512),
                           ("PS", (mo * 512, (mo + 1) * 512)), ("l0f_sm", (62, 70))],
                    writes=[grg(A_XI + mo * 512, A_XI + (mo + 1) * 512)],
                    tag=f"x2{l}_{n}_{mo}")
            S.add("sync", lambda e, n=n: e.dma_start(
                out=xint[:, 1 + n * 512:1 + (n + 1) * 512].rearrange("(c p) f -> p c f", p=128),
                in_=gap(A_XI, A_XI + 4096).rearrange("p (c f) -> p c f", c=8)),
                reads=[grg(A_XI, A_XI + 4096)], writes=[("l0f_xintw", (n, n + 1)),
                                                        ("l0f_xints", (0, 8))],
                tag=f"xw{l}_{n}")

    if L0F_LIMIT == "a0":
        pass
    C.tap("l0f_q", q_scr[:, :], [("l0f_qscr", None)])
    C.tap("l0f_k", k_scr[:, :], [("l0f_kscr", None)])
    C.tap("l0f_v", v_scr[:, :], [("l0f_vscr", None)])
    C.tap("l0f_xint", xint[:, :], [("l0f_xintw", None), ("l0f_xints", None)])
    C.tap("l0f_vf", vf_scr[0:128, :], [("l0f_vfscr", None)])
    for n in range(NT):
        tsl = (PAD + n * TTW, PAD + (n + 1) * TTW)
        S.add("sync", lambda e, n=n: e.dma_start(
            out=gap(A_XI, A_XI + 4096).rearrange("p (c f) -> p c f", c=8),
            in_=xint[:, 1 + n * 512:1 + (n + 1) * 512].rearrange("(c p) f -> p c f", p=128)),
            reads=[("l0f_xintw", (0, 100)), ("l0f_xints", (0, 8))],
            writes=[grg(A_XI, A_XI + 4096)], tag=f"fxl{n}")
        for mo in range(DC):
            S.add("vector", lambda e, mo=mo, n=n: e.tensor_tensor(
                rb[:, mo * 512:(mo + 1) * 512],
                C.xres[mo][:, PAD + n * TTW:PAD + (n + 1) * TTW],
                gap(A_XI + mo * 512, A_XI + (mo + 1) * 512), ALU.add),
                reads=[(f"xres{mo}", tsl), grg(A_XI + mo * 512, A_XI + (mo + 1) * 512)],
                writes=[(K_RB, (mo * 512, (mo + 1) * 512))], tag=f"fres{n}_{mo}")
        emit_postln_tile(C, n, rb, K_RB)



EMITTERS = {
    "l0_attn": emit_l0attn,
    "l0_ffn": emit_l0ffn,
    "l1_attn": emit_l1attn,
    "l1_ffn": emit_l1ffn,
    "l2_attn": emit_l2attn,
    "l2_ffn": emit_l2ffn,
    "l3_attn": emit_l3attn,
    "l3_ffn": emit_l3ffn,
}

BLOCK_ORDER = ["l0_attn", "l0_ffn", "l1_attn", "l1_ffn",
               "l2_attn", "l2_ffn", "l3_attn", "l3_ffn"]


def build_program(blocks, first, last, debug=False):
    nc = bass.Bass("TRN2", target_bir_lowering=False)
    stack = contextlib.ExitStack()
    with stack:
        C = Ctx(nc, stack)
        C.debug = debug
        with contextlib.ExitStack() as semstack:
            sems = {e: semstack.enter_context(nc.semaphore(f"sem_{e}")) for e in ENGINES}
            blk = semstack.enter_context(nc.Block())
            setup_common(C, first)
            for b in blocks:
                EMITTERS[b](C)
            finish_common(C, last)
            C.S.emit(blk, sems)
    fv = np.zeros((128, 64), dtype=np.float32)
    fv[:, :len(C.fvals)] = np.asarray(C.fvals, dtype=np.float32)[None, :]
    return nc, C.inputs, fv


def kernel(**inputs):
    n_cores = 8
    x_host = np.asarray(inputs["L0_attn_hidden_states"], dtype=np.float32)
    xs = [np.ascontiguousarray(x_host[c].T) for c in range(n_cores)]
    prepped = {}

    def get_prep(name):
        if name not in prepped:
            prepped[name] = np.ascontiguousarray(PREP[name](inputs))
        return prepped[name]

    blocks = [b for b in BLOCK_ORDER if b in EMITTERS]
    outs = xs
    for i, b in enumerate(blocks):
        first = (i == 0)
        last = (i == len(blocks) - 1)
        nc, innames, fv = build_program([b], first, last)
        in_maps = []
        for c in range(n_cores):
            m = {}
            for nm in innames:
                if nm == "x_in":
                    m[nm] = outs[c]
                elif nm == "c_fvals":
                    m[nm] = fv
                else:
                    m[nm] = get_prep(nm)
            in_maps.append(m)
        res = run_bass_kernel_spmd(nc, in_maps, core_ids=list(range(n_cores)))
        outs = [np.asarray(r["x_out"]) for r in res.results]
    return np.stack([o.T for o in outs]).astype(np.float32)


# version 04
# speedup vs baseline: 1.0147x; 1.0147x over previous
import contextlib
import numpy as np
import ml_dtypes
import concourse.bass as bass
import concourse.mybir as mybir
from concourse.bass_utils import run_bass_kernel_spmd

F32 = mybir.dt.float32
BF16 = mybir.dt.bfloat16
AF = mybir.ActivationFunctionType
ALU = mybir.AluOpType
AX = mybir.AxisListType
NPBF = ml_dtypes.bfloat16

ENGINES = ("sync", "gpsimd", "scalar", "vector", "tensor")
PAD = 4
T = 4096
NT = 8
TTW = 512
DC = 8
INF = 1 << 40


class Sched:
    def __init__(self):
        self.steps = []
        self.counts = {e: 0 for e in ENGINES}
        self.writers = {}
        self.readers = {}

    @staticmethod
    def _iv(reg, buf=None):
        if reg is None:
            return (0, INF)
        lo, hi = int(reg[0]), int(reg[1])
        if buf == "PS":
            lo = (lo // 512) * 512
            hi = ((hi + 511) // 512) * 512
        return (lo, hi)

    translate = None

    def add(self, engine, emit, reads=(), writes=(), weight=None, tag=""):
        idx = len(self.steps)
        if weight is None:
            weight = 16 if engine == "sync" else 1
        if self.translate is not None:
            reads = [self.translate(b, r) for (b, r) in reads]
            writes = [self.translate(b, r) for (b, r) in writes]
        deps = set()
        for (buf, reg) in reads:
            lo, hi = self._iv(reg, buf)
            for (l, h, i) in self.writers.get(buf, ()):
                if l < hi and lo < h:
                    deps.add(i)
        for (buf, reg) in writes:
            lo, hi = self._iv(reg, buf)
            for (l, h, i) in self.writers.get(buf, ()):
                if l < hi and lo < h:
                    deps.add(i)
            for (l, h, i) in self.readers.get(buf, ()):
                if l < hi and lo < h:
                    deps.add(i)
        for (buf, reg) in reads:
            lo, hi = self._iv(reg, buf)
            self.readers.setdefault(buf, []).append((lo, hi, idx))
        for (buf, reg) in writes:
            lo, hi = self._iv(reg, buf)
            wl = self.writers.setdefault(buf, [])
            wl[:] = [e for e in wl if not (lo <= e[0] and e[1] <= hi)]
            wl.append((lo, hi, idx))
            rl = self.readers.setdefault(buf, [])
            rl[:] = [e for e in rl if not (lo <= e[0] and e[1] <= hi)]
        deps.discard(idx)
        self.counts[engine] += weight
        self.steps.append(dict(idx=idx, engine=engine, emit=emit, weight=weight,
                               cum=self.counts[engine], deps=sorted(deps),
                               tag=tag, dma=(weight >= 16)))
        return idx

    def emit(self, block, sems):
        eng_steps = {e: [s for s in self.steps if s["engine"] == e] for e in ENGINES}
        totals = dict(self.counts)
        steps = self.steps

        unsafe_self = ("vector", "scalar", "gpsimd")

        def make(ename):
            def body(eng):
                waited = {}
                for s in eng_steps[ename]:
                    want = {}
                    for d in s["deps"]:
                        de = steps[d]["engine"]
                        if de == ename and ename not in unsafe_self:
                            continue
                        v = steps[d]["cum"]
                        if want.get(de, 0) < v:
                            want[de] = v
                    if s["dma"]:
                        prev = s["cum"] - s["weight"]
                        if prev > 0:
                            want[ename] = max(want.get(ename, 0), prev)
                    for de in ENGINES:
                        v = want.get(de, 0)
                        if v and waited.get(de, 0) < v:
                            eng.wait_ge(sems[de], v)
                            waited[de] = v
                    ins = s["emit"](eng)
                    ins.then_inc(sems[ename], s["weight"])
                if ename == "sync":
                    for e in ENGINES:
                        if totals[e] > 0 and waited.get(e, 0) < totals[e]:
                            eng.wait_ge(sems[e], totals[e])
                elif ename == "gpsimd" and totals[ename] > 0:
                    if waited.get(ename, 0) < totals[ename]:
                        eng.wait_ge(sems[ename], totals[ename])
            return body

        for e in ENGINES:
            getattr(block, e)(make(e))


class Ctx:
    def __init__(self, nc, stack):
        self.nc = nc
        self.stack = stack
        self.S = Sched()
        self.inputs = []
        self.sb = {}
        self.fmap = {}
        self.fvals = []
        self.aliases = {}
        self.abump = 0
        self.amax = 0
        self.use_arena = False
        self.GA_COLS = 0

    FC = 320

    def fconst(self, v):
        v = float(v)
        if v not in self.fmap:
            i = len(self.fvals)
            assert i < 64
            self.fvals.append(v)
            self.fmap[v] = i
            self.nc.const_aps.register(F32, v, self.CONST[:, self.FC + i:self.FC + i + 1])
        return ("CONST", (self.FC, self.FC + 64))

    GLOBAL_SBUF = ("CONST", "CONSTB", "lnb", "lnsm", "GA") + tuple(
        f"xres{k}" for k in range(8))

    def sbuf(self, name, shape, dtype):
        if name in self.GLOBAL_SBUF or not getattr(self, "use_arena", False):
            t = self.stack.enter_context(self.nc.sbuf_tensor(name, shape, dtype))
            self.sb[name] = t
            return t
        rows, cols = shape
        scale = {BF16: 1, F32: 2}[dtype]
        ga_cols = cols * scale
        off = (self.abump + 1) // 2 * 2
        self.abump = off + ga_cols
        self.amax = max(self.amax, self.abump)
        assert self.abump <= self.GA_COLS, (name, self.abump)
        self.aliases[name] = (off, scale, cols * scale)
        ap = self.GA[:, off:off + ga_cols]
        if dtype == F32:
            ap = ap.bitcast(F32)
        if rows != 128:
            ap = ap[0:rows, :]
        self.sb[name] = ap
        return ap

    def arena_reset(self):
        self.abump = 0

    def _translate(self, buf, reg):
        al = self.aliases.get(buf)
        if al is None:
            return (buf, reg)
        off, scale, size = al
        if reg is None:
            return ("GA", (off, off + size))
        return ("GA", (off + scale * int(reg[0]), off + scale * int(reg[1])))

    def psum(self, name, shape, dtype):
        return self.stack.enter_context(self.nc.psum_tensor(name, shape, dtype))

    def din(self, name, shape, dtype):
        if not hasattr(self, "_din_cache"):
            self._din_cache = {}
        if name in self._din_cache:
            return self._din_cache[name]
        self.inputs.append(name)
        ap = self.nc.dram_tensor(name, list(shape), dtype, kind="ExternalInput").ap()
        self._din_cache[name] = ap
        return ap

    def dout(self, name, shape, dtype):
        return self.nc.dram_tensor(name, list(shape), dtype, kind="ExternalOutput").ap()

    def dscratch(self, name, shape, dtype):
        return self.nc.dram_tensor(name, list(shape), dtype, kind="Internal").ap()

    debug = False

    def tap(self, name, ap, reads):
        if not self.debug:
            return
        shape = [int(x) for x in ap.shape]
        out = self.dout("tap_" + name, shape, ap.dtype)
        self.S.add("sync", lambda e: e.dma_start(out=out, in_=ap),
                   reads=list(reads), tag="tap_" + name)



def _mla_masks():
    a = np.zeros((128, 4 * 512), dtype=np.float32)
    ti = np.arange(128)[:, None]
    sl = np.arange(512)[None, :]
    for v, off in enumerate((0, 128, 256, 384)):
        a[:, v * 512:(v + 1) * 512] = np.where(sl <= ti + off, 0.0, -1e30)
    return a.astype(NPBF)


def _half2():
    a = np.zeros((128, 2), dtype=NPBF)
    a[0:64, 0] = 1
    a[64:128, 1] = 1
    return a


def _bc2():
    a = np.zeros((2, 128), dtype=NPBF)
    a[0, 0:64] = 1
    a[1, 64:128] = 1
    return a


def _l0f_masks():
    a = np.zeros((128, 8 * 512), dtype=np.float32)
    qi = np.arange(128)[:, None]
    kj = np.arange(512)[None, :]
    for qs in range(4):
        qw = qs * 128 + qi
        a[:, qs * 512:(qs + 1) * 512] = np.where(kj >= qw, 0.0, -1e30)
        a[:, (4 + qs) * 512:(5 + qs) * 512] = np.where(kj <= qw, 0.0, -1e30)
    return a.astype(NPBF)


_L0F_ROT_CACHE = []


def _l0f_rot():
    if _L0F_ROT_CACHE:
        return _L0F_ROT_CACHE[0]
    DH, W = 64, 512
    half = np.arange(0, DH, 2, dtype=np.float64)
    inv_freq = 1.0 / 10000.0 ** (half / DH)
    t = np.arange(T, dtype=np.float64)
    fr = t[:, None] * inv_freq[None, :]
    fr = np.concatenate([fr, fr], axis=1)
    base = (half + 0.4 * DH) / (1.4 * DH)
    power = (t - T // 2) / float(W // 2)
    sc = base[None, :] ** power[:, None]
    sc = np.concatenate([sc, sc], axis=1)
    cos, sin = np.cos(fr), np.sin(fr)
    def two(x):
        return np.ascontiguousarray(np.tile(x.T.astype(np.float32), (2, 1))).astype(NPBF)
    out = (two(cos * sc * 8.0), two(sin * sc * 8.0), two(cos / sc), two(sin / sc))
    _L0F_ROT_CACHE.append(out)
    return out


def _sel16():
    a = np.zeros((16, 16 * 128), dtype=NPBF)
    for r in range(16):
        a[r, r * 128:(r + 1) * 128] = 1
    return a


def _chunk16(v):
    w = v.shape[1]
    return np.ascontiguousarray(v.reshape(16, 128, w).transpose(1, 0, 2).reshape(128, 16 * w))


def _l3a_layer_preps():
    d = {}
    for i in range(2):
        d[f"L3A_inwT{i}"] = (lambda ins, i=i: np.ascontiguousarray(ins["L3_attn_in_proj_w"][i].T))
        d[f"L3A_cw{i}"] = (lambda ins, i=i: _chunk16(ins["L3_attn_conv_w"][i]))
        d[f"L3A_cb{i}"] = (lambda ins, i=i: _colchunks(ins["L3_attn_conv_b"][i]))
        d[f"L3A_xpwT{i}"] = (lambda ins, i=i: np.ascontiguousarray(ins["L3_attn_x_proj_w"][i].T))
        d[f"L3A_dtwT{i}"] = (lambda ins, i=i: np.ascontiguousarray(ins["L3_attn_dt_proj_w"][i].T))
        d[f"L3A_dtb{i}"] = (lambda ins, i=i: _colchunks(ins["L3_attn_dt_proj_b"][i]))
        d[f"L3A_alog{i}"] = (lambda ins, i=i: _chunk16(ins["L3_attn_A_log"][i]))
        d[f"L3A_D{i}"] = (lambda ins, i=i: _colchunks(ins["L3_attn_D"][i]))
        d[f"L3A_owT{i}"] = (lambda ins, i=i: np.ascontiguousarray(ins["L3_attn_out_proj_w"][i].T))
        d[f"L3A_rln{i}"] = (lambda ins, i=i: np.concatenate(
            [_colchunks(ins["L3_attn_rms_w"][i]), _colchunks(ins["L3_attn_ln_w"][i]),
             _colchunks(ins["L3_attn_ln_b"][i])], axis=1))
    return d


def _expbc():
    a = np.zeros((16, 16 * 128), dtype=NPBF)
    for e in range(16):
        a[e, e * 128:(e + 1) * 128] = 1
    return a


def _colchunks(v):
    F = v.shape[0]
    return np.ascontiguousarray(v.reshape(F // 128, 128).T)


PREP = {
    "c_ones_col_bf": lambda ins: np.ones((128, 1), dtype=NPBF),
    "c_ones_row_f": lambda ins: np.ones((1, 128), dtype=np.float32),
    "c_ones_col_f": lambda ins: np.ones((128, 1), dtype=np.float32),
    "c_ident_bf": lambda ins: np.eye(128, dtype=NPBF),
    "c_ident_f": lambda ins: np.eye(128, dtype=np.float32),
    "c_ones_row_bf": lambda ins: np.ones((1, 128), dtype=NPBF),
    "c_expbc": lambda ins: _expbc(),
    "L1f_W1T": lambda ins: np.ascontiguousarray(ins["L1_ffn_W1"].T),
    "L1f_W2T": lambda ins: np.ascontiguousarray(ins["L1_ffn_W2"].T),
    "L1f_b1c": lambda ins: _colchunks(ins["L1_ffn_b1"]),
    "L1f_b2c": lambda ins: _colchunks(ins["L1_ffn_b2"]),
    "L2f_K": lambda ins: np.ascontiguousarray(ins["L2_ffn_kernel"]),
    "L2f_a": lambda ins: np.ascontiguousarray(ins["L2_ffn_a"]),
    "L2f_b": lambda ins: np.ascontiguousarray(ins["L2_ffn_b"]),
    "L2f_bias": lambda ins: _colchunks(ins["L2_ffn_bias"]),
    "L3f_selT": lambda ins: np.ascontiguousarray(ins["L3_ffn_expert_sel"].T),
    "L3f_keys": lambda ins: np.ascontiguousarray(
        ins["L3_ffn_keys"].transpose(1, 0, 2).reshape(1024, 4096)),
    "L3f_vals": lambda ins: np.ascontiguousarray(
        ins["L3_ffn_values"].reshape(4096, 1024)),
    "L3f_rs": lambda ins: np.ascontiguousarray(
        ins["L3_ffn_route_scale"].reshape(1, 1)),
    "L0a_Wqkv": lambda ins: np.ascontiguousarray(ins["L0_attn_Wqkv"]),
    "L0a_bqkv": lambda ins: _colchunks(ins["L0_attn_bqkv"]),
    "L0a_Wout": lambda ins: np.ascontiguousarray(ins["L0_attn_Wout"]),
    "L0a_bout": lambda ins: _colchunks(ins["L0_attn_bout"]),
    "c_half2": lambda ins: _half2(),
    "c_bc2": lambda ins: _bc2(),
    "c_l0f_masks": lambda ins: _l0f_masks(),
    "c_l0f_csq": lambda ins: _l0f_rot()[0],
    "c_l0f_ssq": lambda ins: _l0f_rot()[1],
    "c_l0f_csk": lambda ins: _l0f_rot()[2],
    "c_l0f_ssk": lambda ins: _l0f_rot()[3],
    "c_sel16": lambda ins: _sel16(),
    "L3A_fw1T": lambda ins: np.ascontiguousarray(ins["L3_attn_ff_w1"].T),
    "L3A_fw2T": lambda ins: np.ascontiguousarray(ins["L3_attn_ff_w2"].T),
    "L3A_fb1": lambda ins: _colchunks(ins["L3_attn_ff_b1"]),
    "L3A_fb2": lambda ins: _colchunks(ins["L3_attn_ff_b2"]),
    "c_mla_masks": lambda ins: _mla_masks(),
    "L1a_WdqT": lambda ins: np.ascontiguousarray(ins["L1_attn_W_dq"].T),
    "L1a_Wdq": lambda ins: np.ascontiguousarray(ins["L1_attn_W_dq"]),
    "L1a_Wuq": lambda ins: np.ascontiguousarray(ins["L1_attn_W_uq"]),
    "L1a_Wuk": lambda ins: np.ascontiguousarray(ins["L1_attn_W_uk"]),
    "L1a_WdkvT": lambda ins: np.ascontiguousarray(ins["L1_attn_W_dkv"].T),
    "L1a_Wuv": lambda ins: np.ascontiguousarray(ins["L1_attn_W_uv"]),
    "L1a_WoT": lambda ins: np.ascontiguousarray(ins["L1_attn_W_o"].T),
    "L2a_WdqT": lambda ins: np.ascontiguousarray(ins["L2_attn_W_dq"].T),
    "L2a_Wdq": lambda ins: np.ascontiguousarray(ins["L2_attn_W_dq"]),
    "L2a_Wuq": lambda ins: np.ascontiguousarray(ins["L2_attn_W_uq"]),
    "L2a_Wuk": lambda ins: np.ascontiguousarray(ins["L2_attn_W_uk"]),
    "L2a_WdkvT": lambda ins: np.ascontiguousarray(ins["L2_attn_W_dkv"].T),
    "L2a_Wuv": lambda ins: np.ascontiguousarray(ins["L2_attn_W_uv"]),
    "L2a_WoT": lambda ins: np.ascontiguousarray(ins["L2_attn_W_o"].T),
}
PREP.update(_l3a_layer_preps())


def _l0f_layer_preps():
    d = {}
    for l in range(2):
        d[f"L0f_wqkv{l}"] = (lambda ins, l=l: np.ascontiguousarray(ins["L0_ffn_wqkv"][l]))
        d[f"L0f_wo{l}"] = (lambda ins, l=l: np.ascontiguousarray(ins["L0_ffn_wo"][l]))
        d[f"L0f_w1a{l}"] = (lambda ins, l=l: np.ascontiguousarray(ins["L0_ffn_w1"][l][:, :2730]))
        d[f"L0f_w1g{l}"] = (lambda ins, l=l: np.ascontiguousarray(ins["L0_ffn_w1"][l][:, 2730:]))
        d[f"L0f_w2{l}"] = (lambda ins, l=l: np.ascontiguousarray(ins["L0_ffn_w2"][l]))
        d[f"L0f_qs{l}"] = (lambda ins, l=l: np.ascontiguousarray(
            np.tile(ins["L0_ffn_q_scale"][l], 2).reshape(128, 1)))
        d[f"L0f_ks{l}"] = (lambda ins, l=l: np.ascontiguousarray(
            np.tile(ins["L0_ffn_k_scale"][l], 2).reshape(128, 1)))
        d[f"L0f_lg{l}"] = (lambda ins, l=l: _colchunks(ins["L0_ffn_ln_g"][l]))
        d[f"L0f_lb{l}"] = (lambda ins, l=l: _colchunks(ins["L0_ffn_ln_b"][l]))
        d[f"L0f_b1a{l}"] = (lambda ins, l=l: _colchunks(
            np.concatenate([ins["L0_ffn_b1"][l][:2730],
                            np.zeros(86, np.float32)]).astype(np.float32)))
        d[f"L0f_b1g{l}"] = (lambda ins, l=l: _colchunks(
            np.concatenate([ins["L0_ffn_b1"][l][2730:],
                            np.zeros(86, np.float32)]).astype(np.float32)))
        d[f"L0f_b2{l}"] = (lambda ins, l=l: _colchunks(ins["L0_ffn_b2"][l]))
    return d


PREP.update(_l0f_layer_preps())



def bank(C, b, width=512):
    return C.PS[:, b * 512: b * 512 + width]


def setup_common(C, first):
    S = C.S
    C.xres = [C.sbuf(f"xres{k}", [128, PAD + T], BF16) for k in range(DC)]
    C.PS = C.psum("PS", [128, 8 * 512], F32)
    C.CONST = C.sbuf("CONST", [128, 384], F32)
    Ctx.CONST = property(lambda self: self.sb["CONST"]) if False else None
    C.CONST = C.sb["CONST"]
    C.CONSTB = C.sbuf("CONSTB", [128, 512], BF16)
    C.lnb = C.sbuf("lnb", [128, 2 * TTW], BF16)
    C.lnsm = C.sbuf("lnsm", [128, 3 * TTW], F32)
    C.ones_row_f = C.CONST[0:1, 0:128]
    C.ones_col_f = C.CONST[:, 128:129]
    C.ident_f = C.CONST[:, 129:257]
    C.ones_col_bf = C.CONSTB[:, 0:1]
    C.ident_bf = C.CONSTB[:, 128:256]
    C.ones_row_bf = C.CONSTB[0:1, 256:384]

    d_or = C.din("c_ones_row_f", [1, 128], F32)
    d_ocf = C.din("c_ones_col_f", [128, 1], F32)
    d_oc = C.din("c_ones_col_bf", [128, 1], BF16)
    d_id = C.din("c_ident_bf", [128, 128], BF16)
    d_idf = C.din("c_ident_f", [128, 128], F32)
    d_fv = C.din("c_fvals", [128, 64], F32)
    d_orb = C.din("c_ones_row_bf", [1, 128], BF16)
    S.add("sync", lambda e: e.dma_start(out=C.CONST[0:1, 0:128], in_=d_or),
          writes=[("CONST", (0, 128))], tag="ldc1")
    S.add("sync", lambda e: e.dma_start(out=C.CONST[:, 128:129], in_=d_ocf),
          writes=[("CONST", (128, 129))], tag="ldc2")
    S.add("sync", lambda e: e.dma_start(out=C.CONSTB[:, 0:1], in_=d_oc),
          writes=[("CONSTB", (0, 1))], tag="ldc3")
    S.add("sync", lambda e: e.dma_start(out=C.CONSTB[:, 128:256], in_=d_id),
          writes=[("CONSTB", (128, 256))], tag="ldc4")
    S.add("sync", lambda e: e.dma_start(out=C.CONST[:, 129:257], in_=d_idf),
          writes=[("CONST", (129, 257))], tag="ldc5")
    S.add("sync", lambda e: e.dma_start(out=C.CONST[:, Ctx.FC:Ctx.FC + 64], in_=d_fv),
          writes=[("CONST", (Ctx.FC, Ctx.FC + 64))], tag="ldc6")
    S.add("sync", lambda e: e.dma_start(out=C.CONSTB[0:1, 256:384], in_=d_orb),
          writes=[("CONSTB", (256, 384))], tag="ldc7")

    for k in range(DC):
        S.add("vector", lambda e, k=k: e.memset(C.xres[k][:, 0:PAD], 0.0),
              writes=[(f"xres{k}", (0, PAD))], tag="padz")

    xin = C.din("x_in", [1024, T], F32 if first else BF16)
    eng = "gpsimd" if first else "sync"
    for k in range(DC):
        S.add(eng,
              lambda e, k=k: e.dma_start(out=C.xres[k][:, PAD:PAD + T],
                                         in_=xin[k * 128:(k + 1) * 128, :]),
              writes=[(f"xres{k}", (PAD, PAD + T))], weight=16, tag="ldx")


def finish_common(C, last):
    S = C.S
    xout = C.dout("x_out", [1024, T], F32 if last else BF16)
    eng = "gpsimd" if last else "sync"
    for k in range(DC):
        S.add(eng,
              lambda e, k=k: e.dma_start(out=xout[k * 128:(k + 1) * 128, :],
                                         in_=C.xres[k][:, PAD:PAD + T]),
              reads=[(f"xres{k}", (PAD, PAD + T))], weight=16, tag="stx")


def emit_ln_core(C, src, out_fn, gamma=None, beta=None, rms=False, gb_reads=(), tag=""):
    S = C.S
    ts = TTW
    lnb, sm = C.lnb, C.lnsm
    stat1 = C.PS[0:1, 6 * 512: 7 * 512]
    stat2 = C.PS[0:1, 7 * 512: 8 * 512]
    fk1 = C.fconst(1e-5)
    fk2 = C.fconst(-0.5)
    for k in range(DC):
        ap, rds = src(k)
        if not rms:
            S.add("vector", lambda e, ap=ap: e.tensor_copy(lnb[:, 0:ts], ap),
                  reads=rds, writes=[("lnb", (0, ts))], tag=f"{tag}lncast{k}")
        S.add("scalar", lambda e, ap=ap: e.activation(lnb[:, ts:2 * ts], ap, AF.Square),
              reads=rds, writes=[("lnb", (ts, 2 * ts))], tag=f"{tag}lnsq{k}")
        if not rms:
            S.add("tensor", lambda e, k=k: e.matmul(stat1, C.ones_col_bf, lnb[:, 0:ts],
                                                    start=(k == 0), stop=(k == DC - 1)),
                  reads=[("lnb", (0, ts)), ("CONSTB", (0, 1))],
                  writes=[("PS", (6 * 512, 7 * 512))], tag=f"{tag}lnmm1_{k}")
        S.add("tensor", lambda e, k=k: e.matmul(stat2, C.ones_col_bf, lnb[:, ts:2 * ts],
                                                start=(k == 0), stop=(k == DC - 1)),
              reads=[("lnb", (ts, 2 * ts)), ("CONSTB", (0, 1))],
              writes=[("PS", (7 * 512, 8 * 512))], tag=f"{tag}lnmm2_{k}")
    mean = sm[0:1, 0:ts]
    rstd = sm[0:1, ts:2 * ts]
    msq = sm[0:1, 2 * ts:3 * ts]
    if rms:
        S.add("vector", lambda e: e.tensor_scalar_mul(rstd, stat2, 1.0 / 1024.0),
              reads=[("PS", (7 * 512, 8 * 512))], writes=[("lnsm", (ts, 2 * ts))],
              tag=tag + "lnms")
    else:
        S.add("vector", lambda e: e.tensor_scalar_mul(mean, stat1, 1.0 / 1024.0),
              reads=[("PS", (6 * 512, 7 * 512))], writes=[("lnsm", (0, ts))], tag=tag + "lnmean")
        S.add("scalar", lambda e: e.activation(msq, mean, AF.Square),
              reads=[("lnsm", (0, ts))], writes=[("lnsm", (2 * ts, 3 * ts))], tag=tag + "lnmsq")
        S.add("vector", lambda e: e.scalar_tensor_tensor(rstd, stat2, 1.0 / 1024.0, msq,
                                                         ALU.mult, ALU.subtract),
              reads=[("PS", (7 * 512, 8 * 512)), ("lnsm", (2 * ts, 3 * ts))],
              writes=[("lnsm", (ts, 2 * ts))], tag=tag + "lnvar")
    S.add("scalar", lambda e: e.activation(rstd, rstd, AF.Ln, bias=1e-5),
          reads=[("lnsm", (ts, 2 * ts)), fk1], writes=[("lnsm", (ts, 2 * ts))], tag=tag + "lnln")
    S.add("scalar", lambda e: e.activation(rstd, rstd, AF.Exp, scale=-0.5),
          reads=[("lnsm", (ts, 2 * ts)), fk2], writes=[("lnsm", (ts, 2 * ts))], tag=tag + "lnexp")
    if not rms:
        S.add("tensor", lambda e: e.matmul(bank(C, 4), C.ones_row_f, mean, start=True, stop=True),
              reads=[("lnsm", (0, ts)), ("CONST", (0, 128))],
              writes=[("PS", (4 * 512, 5 * 512))], tag=tag + "lnbcm")
    S.add("tensor", lambda e: e.matmul(bank(C, 5), C.ones_row_f, rstd, start=True, stop=True),
          reads=[("lnsm", (ts, 2 * ts)), ("CONST", (0, 128))],
          writes=[("PS", (5 * 512, 6 * 512))], tag=tag + "lnbcr")
    stg = sm[:, 2 * ts:3 * ts]
    for k in range(DC):
        ap, rds = src(k)
        oap, owr = out_fn(k)
        if rms:
            if gamma is not None:
                S.add("vector", lambda e, ap=ap, oap=oap, k=k: e.scalar_tensor_tensor(
                    oap, ap, gamma[k], bank(C, 5), ALU.mult, ALU.mult),
                    reads=rds + [("PS", (5 * 512, 6 * 512))] + list(gb_reads),
                    writes=[owr], tag=f"{tag}lnout{k}")
            else:
                S.add("vector", lambda e, ap=ap, oap=oap: e.tensor_tensor(
                    oap, ap, bank(C, 5), ALU.mult),
                    reads=rds + [("PS", (5 * 512, 6 * 512))], writes=[owr],
                    tag=f"{tag}lnout{k}")
        else:
            S.add("vector", lambda e, ap=ap: e.tensor_tensor(stg, ap, bank(C, 4), ALU.subtract),
                  reads=rds + [("PS", (4 * 512, 5 * 512))],
                  writes=[("lnsm", (2 * ts, 3 * ts))], tag=f"{tag}lnsub{k}")
            if gamma is None:
                S.add("vector", lambda e, oap=oap: e.tensor_tensor(oap, stg, bank(C, 5), ALU.mult),
                      reads=[("lnsm", (2 * ts, 3 * ts)), ("PS", (5 * 512, 6 * 512))],
                      writes=[owr], tag=f"{tag}lnmul{k}")
            else:
                S.add("vector", lambda e: e.tensor_tensor(stg, stg, bank(C, 5), ALU.mult),
                      reads=[("lnsm", (2 * ts, 3 * ts)), ("PS", (5 * 512, 6 * 512))],
                      writes=[("lnsm", (2 * ts, 3 * ts))], tag=f"{tag}lnmul{k}")
                S.add("vector", lambda e, oap=oap, k=k: e.tensor_scalar(
                    oap, stg, gamma[k], beta[k], ALU.mult, ALU.add),
                    reads=[("lnsm", (2 * ts, 3 * ts))] + list(gb_reads),
                    writes=[owr], tag=f"{tag}lngb{k}")


def emit_postln_tile(C, n, rbuf, rkey):
    ts = TTW
    tsl = (PAD + n * ts, PAD + (n + 1) * ts)

    def src(k):
        return rbuf[:, k * ts:(k + 1) * ts], [(rkey, (k * ts, (k + 1) * ts))]

    def out_fn(k):
        return (C.xres[k][:, PAD + n * ts:PAD + (n + 1) * ts], (f"xres{k}", tsl))

    emit_ln_core(C, src, out_fn, tag=f"p{n}_")



def emit_abs_stat(C, wap, rkey_reg, sv, svkey, idx, op):
    C.S.add("vector", lambda e: e.tensor_reduce(sv[:, idx:idx + 1], wap, AX.X, op,
                                                apply_absolute_value=True),
            reads=[rkey_reg], writes=[(svkey, (idx, idx + 1))], tag=f"astat{idx}")


def emit_reduce_scale(C, sv, svkey, sl, out_idx, op, post):
    S = C.S
    sm = C.lnsm
    lo, hi = sl
    S.add("vector", lambda e: e.tensor_reduce(sm[0:128, 4:5], sv[:, lo:hi], AX.X, op,
                                              apply_absolute_value=False),
          reads=[(svkey, sl)], writes=[("lnsm", (4, 5))], tag="rs_red")
    if op == ALU.add:
        S.add("tensor", lambda e: e.matmul(C.PS[0:1, 6 * 512:6 * 512 + 1], sm[0:128, 4:5],
                                           C.ones_col_f, start=True, stop=True),
              reads=[("lnsm", (4, 5)), ("CONST", (128, 129))],
              writes=[("PS", (6 * 512, 6 * 512 + 1))], tag="rs_mm")
        src = C.PS[0:1, 6 * 512:6 * 512 + 1]
        src_reads = [("PS", (6 * 512, 6 * 512 + 1))]
    else:
        S.add("tensor", lambda e: e.matmul(C.PS[0:1, 6 * 512:6 * 512 + 128], sm[0:128, 4:5],
                                           C.ident_f, start=True, stop=True),
              reads=[("lnsm", (4, 5)), ("CONST", (129, 257))],
              writes=[("PS", (6 * 512, 6 * 512 + 128))], tag="rs_tr")
        S.add("vector", lambda e: e.tensor_reduce(sm[0:1, 5:6], C.PS[0:1, 6 * 512:6 * 512 + 128],
                                                  AX.X, ALU.max),
              reads=[("PS", (6 * 512, 6 * 512 + 128))], writes=[("lnsm", (5, 6))], tag="rs_max")
        src = sm[0:1, 5:6]
        src_reads = [("lnsm", (5, 6))]
    cur = src
    cur_reads = src_reads
    for i, stage in enumerate(post):
        dst = sm[0:1, 6 + i:7 + i]
        if stage == ("recip",):
            S.add("vector", lambda e, cur=cur, dst=dst: e.reciprocal(dst, cur),
                  reads=cur_reads, writes=[("lnsm", (6 + i, 7 + i))], tag=f"rs_post{i}")
        elif len(stage) == 2:
            s1, op0 = stage
            S.add("vector", lambda e, cur=cur, dst=dst, s1=s1, op0=op0:
                  e.tensor_single_scalar(dst, cur, s1, op0),
                  reads=cur_reads, writes=[("lnsm", (6 + i, 7 + i))], tag=f"rs_post{i}")
        else:
            s1, s2, op0, op1 = stage
            S.add("vector", lambda e, cur=cur, dst=dst, s1=s1, s2=s2, op0=op0, op1=op1:
                  e.tensor_scalar(dst, cur, s1, s2, op0, op1),
                  reads=cur_reads, writes=[("lnsm", (6 + i, 7 + i))], tag=f"rs_post{i}")
        cur = dst
        cur_reads = [("lnsm", (6 + i, 7 + i))]
    S.add("tensor", lambda e, cur=cur: e.matmul(C.PS[:, 6 * 512:6 * 512 + 1], C.ones_row_f,
                                                cur, start=True, stop=True),
          reads=cur_reads + [("CONST", (0, 128))],
          writes=[("PS", (6 * 512, 6 * 512 + 1))], tag="rs_bc")
    S.add("vector", lambda e: e.tensor_copy(sv[:, out_idx:out_idx + 1],
                                            C.PS[:, 6 * 512:6 * 512 + 1]),
          reads=[("PS", (6 * 512, 6 * 512 + 1))],
          writes=[(svkey, (out_idx, out_idx + 1))], tag="rs_cp")


def emit_ternarize(C, wap, rkey_reg, out_ap, wkey_reg, thr_ap, thr_read, stg, stg_reg):
    S = C.S
    S.add("vector", lambda e: e.tensor_scalar(stg, wap, -1.0, thr_ap, ALU.mult, ALU.is_ge),
          reads=[rkey_reg, thr_read], writes=[stg_reg], tag="tern_a")
    S.add("vector", lambda e: e.scalar_tensor_tensor(out_ap, wap, thr_ap, stg,
                                                     ALU.is_ge, ALU.subtract),
          reads=[rkey_reg, stg_reg, thr_read], writes=[wkey_reg], tag="tern_b")



def emit_l1ffn(C):
    S = C.S
    W1T = C.din("L1f_W1T", [1024, 4096], F32)
    W2T = C.din("L1f_W2T", [4096, 1024], F32)
    b1c = C.din("L1f_b1c", [128, 32], F32)
    b2c = C.din("L1f_b2c", [128, 8], F32)
    FQ = 4096
    NFC = FQ // 128

    w1t = C.sbuf("l1f_w1t", [128, DC * FQ], BF16)
    wq = C.sbuf("l1f_wq", [128, FQ // 2], F32)
    w2s = C.sbuf("l1f_w2s", [128, FQ], BF16)
    hsb = C.sbuf("l1f_h", [128, NFC * TTW], BF16)
    rb = C.sbuf("l1f_r", [128, DC * TTW], F32)
    sv = C.sbuf("l1f_sv", [128, 52], F32)
    bia = C.sbuf("l1f_bia", [128, 40], F32)

    S.add("sync", lambda e: e.dma_start(out=bia[:, 0:32], in_=b1c),
          writes=[("l1f_bia", (0, 32))], tag="ldb1")
    S.add("sync", lambda e: e.dma_start(out=bia[:, 32:40], in_=b2c),
          writes=[("l1f_bia", (32, 40))], tag="ldb2")

    w2scr = C.dscratch("l1f_w2scr", [4096, 1024], BF16)

    def load_group(dst_ap, dkey, dsl, src, r0, rn):
        if rn == 1:
            S.add("sync", lambda e: e.dma_start(out=dst_ap, in_=src[r0 * 128:(r0 + 1) * 128, :]),
                  writes=[(dkey, dsl)], tag=f"ldw{r0}")
        else:
            S.add("sync", lambda e: e.dma_start(
                out=dst_ap.rearrange("p (c f) -> p c f", c=rn),
                in_=src[r0 * 128:(r0 + rn) * 128, :].rearrange("(c p) f -> p c f", p=128)),
                writes=[(dkey, dsl)], tag=f"ldw{r0}")

    scale_post = [(1.0 / (4096.0 * 1024.0), 1e-8, ALU.mult, ALU.max), (0.5, ALU.mult)]

    H = FQ // 2
    def w1half(e, ch, hf):
        return e.dma_start(out=wq[:, 0:H],
                           in_=W1T[ch * 128:(ch + 1) * 128, hf * H:(hf + 1) * H])
    for ch in range(8):
        for hf in range(2):
            S.add("sync", lambda e, ch=ch, hf=hf: w1half(e, ch, hf),
                  writes=[("l1f_wq", (0, H))], tag=f"ldw1_{ch}_{hf}")
            emit_abs_stat(C, wq[:, 0:H], ("l1f_wq", (0, H)), sv, "l1f_sv",
                          2 * ch + hf, ALU.add)
    emit_reduce_scale(C, sv, "l1f_sv", (0, 16), 48, ALU.add, scale_post)
    for ch in range(8):
        for hf in range(2):
            S.add("sync", lambda e, ch=ch, hf=hf: w1half(e, ch, hf),
                  writes=[("l1f_wq", (0, H))], tag=f"ldw1b_{ch}_{hf}")
            osl = (ch * FQ + hf * H, ch * FQ + (hf + 1) * H)
            emit_ternarize(C, wq[:, 0:H], ("l1f_wq", (0, H)),
                           w1t[:, osl[0]:osl[1]], ("l1f_w1t", osl),
                           sv[:, 48:49], ("l1f_sv", (48, 49)),
                           rb[:, 0:H], ("l1f_r", (0, H)))
    for g in range(16):
        load_group(wq[:, 0:H], "l1f_wq", (0, H), W2T, g * 2, 2)
        for c in range(2):
            ch = g * 2 + c
            csl = (c * 1024, (c + 1) * 1024)
            emit_abs_stat(C, wq[:, csl[0]:csl[1]], ("l1f_wq", csl), sv, "l1f_sv", 16 + ch, ALU.add)
    emit_reduce_scale(C, sv, "l1f_sv", (16, 48), 49, ALU.add, scale_post)
    for g in range(16):
        load_group(wq[:, 0:H], "l1f_wq", (0, H), W2T, g * 2, 2)
        for c in range(2):
            ch = g * 2 + c
            csl = (c * 1024, (c + 1) * 1024)
            ssl = (0, 1024)
            emit_ternarize(C, wq[:, csl[0]:csl[1]], ("l1f_wq", csl),
                           w2s[:, ssl[0]:ssl[1]], ("l1f_w2s", ssl),
                           sv[:, 49:50], ("l1f_sv", (49, 50)),
                           rb[:, 0:1024], ("l1f_r", (0, 1024)))
            S.add("gpsimd", lambda e, ch=ch, ssl=ssl: e.dma_start(
                out=w2scr[ch * 128:(ch + 1) * 128, :], in_=w2s[:, ssl[0]:ssl[1]]),
                reads=[("l1f_w2s", ssl)], writes=[("l1f_w2scr", (ch, ch + 1))],
                weight=16, tag=f"stw2_{ch}")

    for n in range(NT):
        tsl = (PAD + n * TTW, PAD + (n + 1) * TTW)
        for m in range(NFC):
            bnk = m % 2
            def mmh(e, m=m, bnk=bnk, n=n):
                ins = None
                for k in range(DC):
                    ins = e.matmul(bank(C, bnk),
                                   w1t[:, k * FQ + m * 128: k * FQ + (m + 1) * 128],
                                   C.xres[k][:, PAD + n * TTW:PAD + (n + 1) * TTW],
                                   start=(k == 0), stop=(k == DC - 1))
                return ins
            S.add("tensor", mmh,
                  reads=[("l1f_w1t", None)] + [(f"xres{k}", tsl) for k in range(DC)],
                  writes=[("PS", (bnk * 512, bnk * 512 + 512))], tag=f"mmh{n}_{m}")
            S.add("scalar", lambda e, m=m, bnk=bnk: e.activation(
                hsb[:, m * TTW:(m + 1) * TTW], bank(C, bnk), AF.Gelu, bias=bia[:, m:m + 1]),
                reads=[("PS", (bnk * 512, bnk * 512 + 512)), ("l1f_bia", (m, m + 1))],
                writes=[("l1f_h", (m * TTW, (m + 1) * TTW))], tag=f"gelu{n}_{m}")
        for g in range(8):
            dsl = (0, FQ)
            S.add("sync", lambda e, g=g, dsl=dsl: e.dma_start(
                out=w2s[:, dsl[0]:dsl[1]].rearrange("p (c f) -> p c f", c=4),
                in_=w2scr[g * 4 * 128:(g + 1) * 4 * 128, :].rearrange("(c p) f -> p c f", p=128)),
                reads=[("l1f_w2scr", (g * 4, (g + 1) * 4))],
                writes=[("l1f_w2s", dsl)], tag=f"ldw2_{n}_{g}")
            for mo in range(DC):
                def mmo(e, g=g, mo=mo, dsl=dsl):
                    ins = None
                    for c in range(4):
                        ins = e.matmul(bank(C, mo),
                                       w2s[:, dsl[0] + c * 1024 + mo * 128:
                                            dsl[0] + c * 1024 + (mo + 1) * 128],
                                       hsb[:, (g * 4 + c) * TTW:(g * 4 + c + 1) * TTW],
                                       start=(g == 0 and c == 0), stop=(g == 7 and c == 3))
                    return ins
                S.add("tensor", mmo,
                      reads=[("l1f_w2s", dsl)] +
                            [("l1f_h", ((g * 4 + c) * TTW, (g * 4 + c + 1) * TTW))
                             for c in range(4)],
                      writes=[("PS", (mo * 512, mo * 512 + 512))], tag=f"mmo{n}_{g}_{mo}")
        for mo in range(DC):
            S.add("vector", lambda e, mo=mo, n=n: e.scalar_tensor_tensor(
                rb[:, mo * TTW:(mo + 1) * TTW], bank(C, mo), bia[:, 32 + mo:33 + mo],
                C.xres[mo][:, PAD + n * TTW:PAD + (n + 1) * TTW], ALU.add, ALU.add),
                reads=[("PS", (mo * 512, mo * 512 + 512)), ("l1f_bia", (32 + mo, 33 + mo)),
                       (f"xres{mo}", tsl)],
                writes=[("l1f_r", (mo * TTW, (mo + 1) * TTW))], tag=f"res{n}_{mo}")
        if n == 0:
            C.tap("h0", hsb[:, :], [("l1f_h", None)])
            C.tap("r0", rb[:, :], [("l1f_r", None)])
        emit_postln_tile(C, n, rb, "l1f_r")
        if n == 0:
            C.tap("x0n", C.xres[0][:, :], [("xres0", None)])
    C.tap("w1t", w1t[:, :], [("l1f_w1t", None)])
    C.tap("sv", sv[:, :], [("l1f_sv", None)])





def emit_l2ffn(C):
    S = C.S
    KER = C.din("L2f_K", [1024, 1024], F32)
    A = C.din("L2f_a", [1024, 16], F32)
    B = C.din("L2f_b", [16, 1024], F32)
    bic = C.din("L2f_bias", [128, 8], F32)
    BIGC = float(3 << 22)

    ksb = C.sbuf("l2f_k", [128, 8 * 1024], BF16)
    wq = C.sbuf("l2f_wq", [128, 2048], F32)
    stg1 = C.sbuf("l2f_s1", [128, 1024], F32)
    stg2 = C.sbuf("l2f_s2", [128, 1024], F32)
    asb = C.sbuf("l2f_a", [128, 8 * 16], BF16)
    bsb = C.sbuf("l2f_b", [16, 1024], BF16)
    zsb = C.sbuf("l2f_z", [16, T], BF16)
    rb = C.sbuf("l2f_r", [128, DC * TTW], F32)
    sv = C.sbuf("l2f_sv", [128, 20], F32)
    bia = C.sbuf("l2f_bia", [128, 8], F32)

    S.add("sync", lambda e: e.dma_start(out=bia[:, 0:8], in_=bic),
          writes=[("l2f_bia", (0, 8))], tag="ldbias")
    S.add("gpsimd", lambda e: e.dma_start(
        out=asb[:, :].rearrange("p (c f) -> p c f", c=8),
        in_=A.rearrange("(c p) f -> p c f", p=128)),
        writes=[("l2f_a", None)], weight=16, tag="lda")
    S.add("gpsimd", lambda e: e.dma_start(out=bsb[:, :], in_=B),
        writes=[("l2f_b", None)], weight=16, tag="ldb")

    for g in range(4):
        S.add("sync", lambda e, g=g: e.dma_start(
            out=wq[:, :].rearrange("p (c f) -> p c f", c=2),
            in_=KER[g * 256:(g + 1) * 256, :].rearrange("(c p) f -> p c f", p=128)),
            writes=[("l2f_wq", (0, 2048))], tag=f"ldk{g}")
        for c in range(2):
            ch = g * 2 + c
            csl = (c * 1024, (c + 1) * 1024)
            emit_abs_stat(C, wq[:, csl[0]:csl[1]], ("l2f_wq", csl), sv, "l2f_sv", ch, ALU.max)
    emit_reduce_scale(C, sv, "l2f_sv", (0, 8), 16, ALU.max, [(1.0 / 7.0, ALU.mult)])
    emit_reduce_scale(C, sv, "l2f_sv", (0, 8), 17, ALU.max, [("recip",), (7.0, ALU.mult)])
    for g in range(4):
        S.add("sync", lambda e, g=g: e.dma_start(
            out=wq[:, :].rearrange("p (c f) -> p c f", c=2),
            in_=KER[g * 256:(g + 1) * 256, :].rearrange("(c p) f -> p c f", p=128)),
            writes=[("l2f_wq", (0, 2048))], tag=f"ldk2{g}")
        for c in range(2):
            ch = g * 2 + c
            csl = (c * 1024, (c + 1) * 1024)
            S.add("vector", lambda e, csl=csl: e.tensor_scalar(
                stg1[:, :], wq[:, csl[0]:csl[1]], sv[:, 17:18], BIGC, ALU.mult, ALU.add),
                reads=[("l2f_wq", csl), ("l2f_sv", (17, 18))],
                writes=[("l2f_s1", None)], tag=f"q1_{ch}")
            S.add("vector", lambda e: e.tensor_scalar(
                stg2[:, :], stg1[:, :], BIGC, 7.0, ALU.subtract, ALU.min),
                reads=[("l2f_s1", None)], writes=[("l2f_s2", None)], tag=f"q2_{ch}")
            S.add("vector", lambda e, ch=ch: e.tensor_scalar(
                ksb[:, ch * 1024:(ch + 1) * 1024], stg2[:, :], -8.0, sv[:, 16:17],
                ALU.max, ALU.mult),
                reads=[("l2f_s2", None), ("l2f_sv", (16, 17))],
                writes=[("l2f_k", (ch * 1024, (ch + 1) * 1024))], tag=f"q3_{ch}")

    for n in range(NT):
        tsl = (PAD + n * TTW, PAD + (n + 1) * TTW)
        zsl = (n * TTW, (n + 1) * TTW)
        def mmz(e, n=n):
            ins = None
            for k in range(DC):
                ins = e.matmul(C.PS[0:16, 2 * 512:3 * 512],
                               asb[:, k * 16:(k + 1) * 16],
                               C.xres[k][:, PAD + n * TTW:PAD + (n + 1) * TTW],
                               start=(k == 0), stop=(k == DC - 1))
            return ins
        S.add("tensor", mmz,
              reads=[("l2f_a", None)] + [(f"xres{k}", tsl) for k in range(DC)],
              writes=[("PS", (2 * 512, 3 * 512))], tag=f"mmz{n}")
        S.add("vector", lambda e, zsl=zsl: e.tensor_copy(zsb[0:16, zsl[0]:zsl[1]],
                                                         C.PS[0:16, 2 * 512:3 * 512]),
              reads=[("PS", (2 * 512, 3 * 512))], writes=[("l2f_z", zsl)], tag=f"zcp{n}")
        for mo in range(DC):
            def mmo(e, mo=mo, n=n, zsl=zsl):
                for k in range(DC):
                    e.matmul(bank(C, mo),
                             ksb[:, k * 1024 + mo * 128: k * 1024 + (mo + 1) * 128],
                             C.xres[k][:, PAD + n * TTW:PAD + (n + 1) * TTW],
                             start=(k == 0), stop=False)
                return e.matmul(bank(C, mo), bsb[:, mo * 128:(mo + 1) * 128],
                                zsb[0:16, zsl[0]:zsl[1]], start=False, stop=True)
            S.add("tensor", mmo,
                  reads=[("l2f_k", None), ("l2f_b", None), ("l2f_z", zsl)] +
                        [(f"xres{k}", tsl) for k in range(DC)],
                  writes=[("PS", (mo * 512, mo * 512 + 512))], tag=f"l2mmo{n}_{mo}")
        for mo in range(DC):
            S.add("vector", lambda e, mo=mo, n=n: e.scalar_tensor_tensor(
                rb[:, mo * TTW:(mo + 1) * TTW], bank(C, mo), bia[:, mo:mo + 1],
                C.xres[mo][:, PAD + n * TTW:PAD + (n + 1) * TTW], ALU.add, ALU.add),
                reads=[("PS", (mo * 512, mo * 512 + 512)), ("l2f_bia", (mo, mo + 1)),
                       (f"xres{mo}", tsl)],
                writes=[("l2f_r", (mo * TTW, (mo + 1) * TTW))], tag=f"l2res{n}_{mo}")
        emit_postln_tile(C, n, rb, "l2f_r")





def emit_l3ffn(C):
    S = C.S
    SELT = C.din("L3f_selT", [1024, 16], F32)
    KEYS = C.din("L3f_keys", [1024, 4096], F32)
    VALS = C.din("L3f_vals", [4096, 1024], F32)
    RS = C.din("L3f_rs", [1, 1], F32)

    keysb = C.sbuf("l3f_keys", [128, 8 * 4096], BF16)
    selb = C.sbuf("l3f_sel", [128, 8 * 16], BF16)
    usb = C.sbuf("l3f_u", [128, 32 * TTW], BF16)
    vsb = C.sbuf("l3f_v", [128, 2048], BF16)
    rb = C.sbuf("l3f_r", [128, DC * TTW], F32)
    ptm = C.sbuf("l3f_p", [128, 64], F32)
    mosc = C.sbuf("l3f_m", [128, 304], F32)
    wtm = C.sbuf("l3f_wtm", [128, 32 * 16], BF16)
    wfm = C.sbuf("l3f_wfm", [16, T], BF16)
    brcsb = C.sbuf("l3f_brc", [128, TTW], BF16)
    rssb = C.sbuf("l3f_rs", [1, 4], F32)
    expbc = C.sbuf("l3f_expbc", [16, 16 * 128], BF16)
    EXPBC = C.din("c_expbc", [16, 16 * 128], BF16)
    S.add("sync", lambda e: e.dma_start(out=expbc[:, :], in_=EXPBC),
          writes=[("l3f_expbc", None)], tag="ldexpbc")

    S.add("gpsimd", lambda e: e.dma_start(
        out=selb[:, :].rearrange("p (c f) -> p c f", c=8),
        in_=SELT.rearrange("(c p) f -> p c f", p=128)),
        writes=[("l3f_sel", None)], weight=16, tag="ldsel")
    for g in range(2):
        S.add("gpsimd", lambda e, g=g: e.dma_start(
            out=keysb[:, g * 4 * 4096:(g + 1) * 4 * 4096].rearrange("p (c f) -> p c f", c=4),
            in_=KEYS[g * 4 * 128:(g + 1) * 4 * 128, :].rearrange("(c p) f -> p c f", p=128)),
            writes=[("l3f_keys", (g * 4 * 4096, (g + 1) * 4 * 4096))], weight=16, tag=f"ldkeys{g}")
    S.add("sync", lambda e: e.dma_start(out=rssb[0:1, 0:1], in_=RS),
          writes=[("l3f_rs", (0, 1))], tag="ldrs")
    S.add("tensor", lambda e: e.matmul(C.PS[:, 2 * 512:2 * 512 + 1], C.ones_row_f,
                                       rssb[0:1, 0:1], start=True, stop=True),
          reads=[("l3f_rs", (0, 1)), ("CONST", (0, 128))],
          writes=[("PS", (2 * 512, 2 * 512 + 1))], tag="rsbc")
    S.add("vector", lambda e: e.tensor_copy(mosc[:, 291:292], C.PS[:, 2 * 512:2 * 512 + 1]),
          reads=[("PS", (2 * 512, 2 * 512 + 1))], writes=[("l3f_m", (291, 292))], tag="rscp")

    for tc in range(32):
        xsl = (PAD + tc * 128, PAD + (tc + 1) * 128)
        psl = ((tc % 4) * 16, (tc % 4) * 16 + 16)
        def mml(e, tc=tc):
            ins = None
            for k in range(DC):
                ins = e.matmul(C.PS[:, 2 * 512:2 * 512 + 16],
                               C.xres[k][:, PAD + tc * 128:PAD + (tc + 1) * 128],
                               selb[:, k * 16:(k + 1) * 16],
                               start=(k == 0), stop=(k == DC - 1))
            return ins
        S.add("tensor", mml,
              reads=[("l3f_sel", None)] + [(f"xres{k}", xsl) for k in range(DC)],
              writes=[("PS", (2 * 512, 2 * 512 + 16))], tag=f"mml{tc}")
        S.add("scalar", lambda e, psl=psl: e.activation(ptm[:, psl[0]:psl[1]],
                                                        C.PS[:, 2 * 512:2 * 512 + 16], AF.Sigmoid),
              reads=[("PS", (2 * 512, 2 * 512 + 16))], writes=[("l3f_p", psl)], tag=f"sig{tc}")
        p16 = ptm[:, psl[0]:psl[1]]
        in0 = p16.unsqueeze(2).to_broadcast((128, 16, 16))
        in1 = p16.unsqueeze(1).to_broadcast((128, 16, 16))
        S.add("vector", lambda e, in0=in0, in1=in1: e.tensor_tensor(
            mosc[:, 0:256].rearrange("p (a b) -> p a b", a=16), in0, in1, ALU.is_gt),
            reads=[("l3f_p", psl)], writes=[("l3f_m", (0, 256))], tag=f"cmp{tc}")
        S.add("vector", lambda e: e.tensor_reduce(
            mosc[:, 256:272], mosc[:, 0:256].rearrange("p (a b) -> p a b", a=16),
            AX.X, ALU.add),
            reads=[("l3f_m", (0, 256))], writes=[("l3f_m", (256, 272))], tag=f"cnt{tc}")
        S.add("vector", lambda e: e.tensor_single_scalar(mosc[:, 256:272], mosc[:, 256:272],
                                                         12.0, ALU.is_ge),
              reads=[("l3f_m", (256, 272))], writes=[("l3f_m", (256, 272))], tag=f"msk{tc}")
        S.add("vector", lambda e, p16=p16: e.tensor_tensor(mosc[:, 272:288], p16,
                                                           mosc[:, 256:272], ALU.mult),
              reads=[("l3f_p", psl), ("l3f_m", (256, 272))],
              writes=[("l3f_m", (272, 288))], tag=f"pm{tc}")
        S.add("vector", lambda e: e.tensor_reduce(mosc[:, 288:289], mosc[:, 272:288],
                                                  AX.X, ALU.add),
              reads=[("l3f_m", (272, 288))], writes=[("l3f_m", (288, 289))], tag=f"zz{tc}")
        S.add("vector", lambda e: e.tensor_single_scalar(mosc[:, 289:290], mosc[:, 288:289],
                                                         1e-9, ALU.max),
              reads=[("l3f_m", (288, 289))], writes=[("l3f_m", (289, 290))], tag=f"zc{tc}")
        S.add("vector", lambda e: e.reciprocal(mosc[:, 290:291], mosc[:, 289:290]),
              reads=[("l3f_m", (289, 290))], writes=[("l3f_m", (290, 291))], tag=f"rec{tc}")
        S.add("vector", lambda e, tc=tc: e.tensor_scalar(
            wtm[:, tc * 16:(tc + 1) * 16], mosc[:, 272:288], mosc[:, 290:291],
            mosc[:, 291:292], ALU.mult, ALU.mult),
            reads=[("l3f_m", (272, 292))],
            writes=[("l3f_wtm", (tc * 16, (tc + 1) * 16))], tag=f"wgt{tc}")
        psb = C.PS[:, 3 * 512:4 * 512].bitcast(BF16)
        S.add("tensor", lambda e, tc=tc, psb=psb: e.transpose(
            psb[0:16, 0:128], wtm[:, tc * 16:(tc + 1) * 16], C.ident_bf),
            reads=[("l3f_wtm", (tc * 16, (tc + 1) * 16)), ("CONSTB", (128, 256))],
            writes=[("PS", (3 * 512, 4 * 512))], tag=f"wtr{tc}")
        S.add("vector", lambda e, tc=tc, psb=psb: e.tensor_copy(
            wfm[0:16, tc * 128:(tc + 1) * 128], psb[0:16, 0:128]),
            reads=[("PS", (3 * 512, 4 * 512))],
            writes=[("l3f_wfm", (tc * 128, (tc + 1) * 128))], tag=f"wcp{tc}")

    for n in range(NT):
        tsl = (PAD + n * TTW, PAD + (n + 1) * TTW)
        for e_i in range(16):
            S.add("tensor", lambda e, e_i=e_i, n=n: e.matmul(
                C.PS[:, 2 * 512:3 * 512], expbc[:, e_i * 128:(e_i + 1) * 128],
                wfm[0:16, n * TTW:(n + 1) * TTW], start=True, stop=True),
                reads=[("l3f_wfm", (n * TTW, (n + 1) * TTW)), ("l3f_expbc", None)],
                writes=[("PS", (2 * 512, 3 * 512))], tag=f"brc{n}_{e_i}")
            S.add("vector", lambda e: e.tensor_copy(brcsb[:, :], C.PS[:, 2 * 512:3 * 512]),
                  reads=[("PS", (2 * 512, 3 * 512))], writes=[("l3f_brc", None)],
                  tag=f"brcp{n}_{e_i}")
            for mh in range(2):
                uc = e_i * 2 + mh
                bnk = uc % 2
                def mms(e, e_i=e_i, mh=mh, bnk=bnk, n=n):
                    ins = None
                    for k in range(DC):
                        ins = e.matmul(bank(C, bnk),
                                       keysb[:, k * 4096 + e_i * 256 + mh * 128:
                                             k * 4096 + e_i * 256 + (mh + 1) * 128],
                                       C.xres[k][:, PAD + n * TTW:PAD + (n + 1) * TTW],
                                       start=(k == 0), stop=(k == DC - 1))
                    return ins
                S.add("tensor", mms,
                      reads=[("l3f_keys", None)] + [(f"xres{k}", tsl) for k in range(DC)],
                      writes=[("PS", (bnk * 512, bnk * 512 + 512))], tag=f"mms{n}_{uc}")
                S.add("vector", lambda e, uc=uc, bnk=bnk: e.scalar_tensor_tensor(
                    usb[:, uc * TTW:(uc + 1) * TTW], bank(C, bnk), 0.0, brcsb[:, :],
                    ALU.max, ALU.mult),
                    reads=[("PS", (bnk * 512, bnk * 512 + 512)), ("l3f_brc", None)],
                    writes=[("l3f_u", (uc * TTW, (uc + 1) * TTW))], tag=f"relu{n}_{uc}")
        for g in range(16):
            S.add("gpsimd", lambda e, g=g: e.dma_start(
                out=vsb[:, :].rearrange("p (c f) -> p c f", c=2),
                in_=VALS[g * 256:(g + 1) * 256, :].rearrange("(c p) f -> p c f", p=128)),
                writes=[("l3f_v", None)], weight=16, tag=f"ldv{n}_{g}")
            for mo in range(DC):
                def mmo(e, g=g, mo=mo):
                    ins = None
                    for c in range(2):
                        ins = e.matmul(bank(C, mo),
                                       vsb[:, c * 1024 + mo * 128: c * 1024 + (mo + 1) * 128],
                                       usb[:, (g * 2 + c) * TTW:(g * 2 + c + 1) * TTW],
                                       start=(g == 0 and c == 0), stop=(g == 15 and c == 1))
                    return ins
                S.add("tensor", mmo,
                      reads=[("l3f_v", None)] +
                            [("l3f_u", ((g * 2 + c) * TTW, (g * 2 + c + 1) * TTW))
                             for c in range(2)],
                      writes=[("PS", (mo * 512, mo * 512 + 512))], tag=f"l3mmo{n}_{g}_{mo}")
        for mo in range(DC):
            S.add("vector", lambda e, mo=mo, n=n: e.tensor_tensor(
                rb[:, mo * TTW:(mo + 1) * TTW], bank(C, mo),
                C.xres[mo][:, PAD + n * TTW:PAD + (n + 1) * TTW], ALU.add),
                reads=[("PS", (mo * 512, mo * 512 + 512)), (f"xres{mo}", tsl)],
                writes=[("l3f_r", (mo * TTW, (mo + 1) * TTW))], tag=f"l3res{n}_{mo}")
        emit_postln_tile(C, n, rb, "l3f_r")





def emit_mla(C, pfx, LAT):
    S = C.S
    NL = LAT // 128
    H = 16
    WdqT = C.din(pfx + "_WdqT", [1024, LAT], F32)
    Wdq = C.din(pfx + "_Wdq", [LAT, 1024], F32)
    Wuq = C.din(pfx + "_Wuq", [1024, LAT], F32)
    Wuk = C.din(pfx + "_Wuk", [1024, LAT], F32)
    WdkvT = C.din(pfx + "_WdkvT", [1024, LAT], F32)
    Wuv = C.din(pfx + "_Wuv", [1024, LAT], F32)
    WoT = C.din(pfx + "_WoT", [1024, 1024], F32)
    MSK = C.din("c_mla_masks", [128, 4 * 512], BF16)

    p = pfx
    wdqT = C.sbuf(p + "wdqT", [128, DC * LAT], BF16)
    wuq = C.sbuf(p + "wuq", [128, DC * LAT], BF16)
    wdkvT = C.sbuf(p + "wdkvT", [128, DC * LAT], BF16)
    wuv = C.sbuf(p + "wuv", [128, DC * LAT], BF16)
    wdqn = C.sbuf(p + "wdqn", [128, NL * 1024], BF16)
    wstr = C.sbuf(p + "wstr", [128, 1024], BF16)
    m1 = C.sbuf(p + "m1", [128, NL * LAT], BF16)
    kef = C.sbuf(p + "kef", [128, DC * LAT], BF16)
    ves = C.sbuf(p + "ves", [128, NL * 1024], BF16)
    ckvf = C.sbuf(p + "ckvf", [128, NL * T], BF16)
    cktm = C.sbuf(p + "cktm", [128, 32 * LAT], BF16)
    cqb = C.sbuf(p + "cqb", [128, NL * 512], BF16)
    qkt = C.sbuf(p + "qkt", [128, NL * LAT], BF16)
    qlat = C.sbuf(p + "qlat", [128, NL * 512], BF16)
    pnb = C.sbuf(p + "pnb", [128, 4096], BF16)
    ptb = C.sbuf(p + "ptb", [128, 32 * 512], BF16)
    msks = C.sbuf(p + "msk", [128, 4 * 512], BF16)
    mst = C.sbuf(p + "mst", [128, 8], F32)
    ctsb = C.sbuf(p + "cts", [128, NL * 512], BF16)
    rb = C.sbuf(p + "rb", [128, DC * TTW], F32)

    kmsk = p + "msk"
    S.add("sync", lambda e: e.dma_start(out=msks[:, :], in_=MSK),
          writes=[(kmsk, None)], tag="ldmsk")

    def ldw(dst, dkey, src, F):
        for g in range(2):
            S.add("gpsimd", lambda e, g=g: e.dma_start(
                out=dst[:, g * 4 * F:(g + 1) * 4 * F].rearrange("p (c f) -> p c f", c=4),
                in_=src[g * 4 * 128:(g + 1) * 4 * 128, :].rearrange("(c p) f -> p c f", p=128)),
                writes=[(dkey, (g * 4 * F, (g + 1) * 4 * F))], weight=16, tag="ldw_" + dkey)

    ldw(wdqT, p + "wdqT", WdqT, LAT)
    ldw(wuq, p + "wuq", Wuq, LAT)
    ldw(wdkvT, p + "wdkvT", WdkvT, LAT)
    ldw(wuv, p + "wuv", Wuv, LAT)
    ldw(kef, p + "kef", Wuk, LAT)
    for g in range(NL):
        S.add("gpsimd", lambda e, g=g: e.dma_start(
            out=wdqn[:, g * 1024:(g + 1) * 1024], in_=Wdq[g * 128:(g + 1) * 128, :]),
            writes=[(p + "wdqn", (g * 1024, (g + 1) * 1024))], weight=16, tag=f"ldwdqn{g}")

    for mc in range(NL):
        def mm1(e, mc=mc):
            ins = None
            for k in range(DC):
                ins = e.matmul(C.PS[:, 0:LAT],
                               wuq[:, k * LAT + mc * 128:k * LAT + (mc + 1) * 128],
                               kef[:, k * LAT:(k + 1) * LAT],
                               start=(k == 0), stop=(k == DC - 1))
            return ins
        S.add("tensor", mm1, reads=[(p + "wuq", None), (p + "kef", None)],
              writes=[("PS", (0, LAT))], tag=f"m1_{mc}")
        S.add("vector", lambda e, mc=mc: e.tensor_copy(m1[:, mc * LAT:(mc + 1) * LAT],
                                                       C.PS[:, 0:LAT]),
              reads=[("PS", (0, LAT))], writes=[(p + "m1", (mc * LAT, (mc + 1) * LAT))],
              tag=f"m1cp{mc}")
    for ma in range(DC):
        def mmk(e, ma=ma):
            ins = None
            for lc in range(NL):
                ins = e.matmul(C.PS[:, 0:LAT],
                               wdqn[:, lc * 1024 + ma * 128:lc * 1024 + (ma + 1) * 128],
                               m1[:, lc * LAT:(lc + 1) * LAT],
                               start=(lc == 0), stop=(lc == NL - 1))
            return ins
        S.add("tensor", mmk, reads=[(p + "wdqn", None), (p + "m1", None)],
              writes=[("PS", (0, LAT))], tag=f"kef{ma}")
        S.add("vector", lambda e, ma=ma: e.tensor_scalar_mul(
            kef[:, ma * LAT:(ma + 1) * LAT], C.PS[:, 0:LAT], 0.125),
            reads=[("PS", (0, LAT))], writes=[(p + "kef", (ma * LAT, (ma + 1) * LAT))],
            tag=f"kefcp{ma}")
    for ch in range(DC):
        S.add("gpsimd", lambda e, ch=ch: e.dma_start(
            out=wstr[:, :], in_=WoT[ch * 128:(ch + 1) * 128, :]),
            writes=[(p + "wstr", None)], weight=16, tag=f"ldwo{ch}")
        for mc in range(NL):
            for nn in range(2):
                bk = mc * 2 + nn
                S.add("tensor", lambda e, ch=ch, mc=mc, nn=nn, bk=bk: e.matmul(
                    bank(C, bk),
                    wuv[:, ch * LAT + mc * 128:ch * LAT + (mc + 1) * 128],
                    wstr[:, nn * 512:(nn + 1) * 512],
                    start=(ch == 0), stop=(ch == DC - 1)),
                    reads=[(p + "wuv", None), (p + "wstr", None)],
                    writes=[("PS", (bk * 512, (bk + 1) * 512))], tag=f"ve{ch}_{bk}")
    for mc in range(NL):
        for nn in range(2):
            bk = mc * 2 + nn
            S.add("vector", lambda e, mc=mc, nn=nn, bk=bk: e.tensor_copy(
                ves[:, mc * 1024 + nn * 512:mc * 1024 + (nn + 1) * 512], bank(C, bk)),
                reads=[("PS", (bk * 512, (bk + 1) * 512))],
                writes=[(p + "ves", (mc * 1024 + nn * 512, mc * 1024 + (nn + 1) * 512))],
                tag=f"vecp{bk}")
    for lc in range(NL):
        for n in range(NT):
            def mmc(e, lc=lc, n=n):
                ins = None
                for k in range(DC):
                    ins = e.matmul(C.PS[:, 4 * 512:5 * 512],
                                   wdkvT[:, k * LAT + lc * 128:k * LAT + (lc + 1) * 128],
                                   C.xres[k][:, PAD + n * TTW:PAD + (n + 1) * TTW],
                                   start=(k == 0), stop=(k == DC - 1))
                return ins
            S.add("tensor", mmc,
                  reads=[(p + "wdkvT", None)] +
                        [(f"xres{k}", (PAD + n * TTW, PAD + (n + 1) * TTW)) for k in range(DC)],
                  writes=[("PS", (4 * 512, 5 * 512))], tag=f"ckv{lc}_{n}")
            S.add("vector", lambda e, lc=lc, n=n: e.tensor_copy(
                ckvf[:, lc * T + n * TTW:lc * T + (n + 1) * TTW], C.PS[:, 4 * 512:5 * 512]),
                reads=[("PS", (4 * 512, 5 * 512))],
                writes=[(p + "ckvf", (lc * T + n * TTW, lc * T + (n + 1) * TTW))],
                tag=f"ckvcp{lc}_{n}")
    for tc in range(32):
        def mmt(e, tc=tc):
            ins = None
            for k in range(DC):
                ins = e.matmul(C.PS[:, 5 * 512:5 * 512 + LAT],
                               C.xres[k][:, PAD + tc * 128:PAD + (tc + 1) * 128],
                               wdkvT[:, k * LAT:(k + 1) * LAT],
                               start=(k == 0), stop=(k == DC - 1))
            return ins
        S.add("tensor", mmt,
              reads=[(p + "wdkvT", None)] +
                    [(f"xres{k}", (PAD + tc * 128, PAD + (tc + 1) * 128)) for k in range(DC)],
              writes=[("PS", (5 * 512, 6 * 512))], tag=f"cktm{tc}")
        S.add("vector", lambda e, tc=tc: e.tensor_copy(
            cktm[:, tc * LAT:(tc + 1) * LAT], C.PS[:, 5 * 512:5 * 512 + LAT]),
            reads=[("PS", (5 * 512, 6 * 512))],
            writes=[(p + "cktm", (tc * LAT, (tc + 1) * LAT))], tag=f"cktmcp{tc}")

    for b in range(NT):
        bsl = (PAD + b * TTW, PAD + (b + 1) * TTW)
        for lc in range(NL):
            def mmq(e, lc=lc, b=b):
                ins = None
                for k in range(DC):
                    ins = e.matmul(C.PS[:, 4 * 512:5 * 512],
                                   wdqT[:, k * LAT + lc * 128:k * LAT + (lc + 1) * 128],
                                   C.xres[k][:, PAD + b * TTW:PAD + (b + 1) * TTW],
                                   start=(k == 0), stop=(k == DC - 1))
                return ins
            S.add("tensor", mmq,
                  reads=[(p + "wdqT", None)] + [(f"xres{k}", bsl) for k in range(DC)],
                  writes=[("PS", (4 * 512, 5 * 512))], tag=f"cq{b}_{lc}")
            S.add("vector", lambda e, lc=lc: e.tensor_copy(
                cqb[:, lc * 512:(lc + 1) * 512], C.PS[:, 4 * 512:5 * 512]),
                reads=[("PS", (4 * 512, 5 * 512))],
                writes=[(p + "cqb", (lc * 512, (lc + 1) * 512))], tag=f"cqcp{b}_{lc}")
        for h in range(H):
            hc, ho = h // 2, (h % 2) * 64
            for qc in range(NL):
                S.add("tensor", lambda e, qc=qc, hc=hc, ho=ho: e.matmul(
                    C.PS[:, 4 * 512:4 * 512 + LAT],
                    wuq[ho:ho + 64, hc * LAT + qc * 128:hc * LAT + (qc + 1) * 128],
                    kef[ho:ho + 64, hc * LAT:(hc + 1) * LAT],
                    start=True, stop=True),
                    reads=[(p + "wuq", None), (p + "kef", None)],
                    writes=[("PS", (4 * 512, 4 * 512 + LAT))], tag=f"qk{b}_{h}_{qc}")
                S.add("vector", lambda e, qc=qc: e.tensor_copy(
                    qkt[:, qc * LAT:(qc + 1) * LAT], C.PS[:, 4 * 512:4 * 512 + LAT]),
                    reads=[("PS", (4 * 512, 4 * 512 + LAT))],
                    writes=[(p + "qkt", (qc * LAT, (qc + 1) * LAT))], tag=f"qkcp{b}_{h}_{qc}")
            for mc in range(NL):
                def mmql(e, mc=mc):
                    ins = None
                    for qc in range(NL):
                        ins = e.matmul(C.PS[:, 4 * 512:5 * 512],
                                       qkt[:, qc * LAT + mc * 128:qc * LAT + (mc + 1) * 128],
                                       cqb[:, qc * 512:(qc + 1) * 512],
                                       start=(qc == 0), stop=(qc == NL - 1))
                    return ins
                S.add("tensor", mmql, reads=[(p + "qkt", None), (p + "cqb", None)],
                      writes=[("PS", (4 * 512, 5 * 512))], tag=f"ql{b}_{h}_{mc}")
                S.add("vector", lambda e, mc=mc: e.tensor_copy(
                    qlat[:, mc * 512:(mc + 1) * 512], C.PS[:, 4 * 512:5 * 512]),
                    reads=[("PS", (4 * 512, 5 * 512))],
                    writes=[(p + "qlat", (mc * 512, (mc + 1) * 512))], tag=f"qlcp{b}_{h}_{mc}")
            for qi in range(4):
                qt = b * 4 + qi
                nkb = qt // 4 + 1
                nsc = 4 * nkb
                def mms(e, qi=qi, nkb=nkb):
                    ins = None
                    for kb in range(nkb):
                        for lc in range(NL):
                            ins = e.matmul(bank(C, kb),
                                           qlat[:, lc * 512 + qi * 128:lc * 512 + (qi + 1) * 128],
                                           ckvf[:, lc * T + kb * 512:lc * T + (kb + 1) * 512],
                                           start=(lc == 0), stop=(lc == NL - 1))
                    return ins
                S.add("tensor", mms, reads=[(p + "qlat", None), (p + "ckvf", None)],
                      writes=[("PS", (0, nkb * 512))], tag=f"S{b}_{h}_{qi}")
                S.add("vector", lambda e, nkb=nkb, qi=qi: e.tensor_tensor(
                    bank(C, nkb - 1), bank(C, nkb - 1),
                    msks[:, qi * 512:(qi + 1) * 512], ALU.add),
                    reads=[("PS", ((nkb - 1) * 512, nkb * 512)), (kmsk, None)],
                    writes=[("PS", ((nkb - 1) * 512, nkb * 512))], tag=f"mka{b}_{h}_{qi}")
                S.add("vector", lambda e, nkb=nkb: e.reduce_max(
                    mst[:, 0:1], C.PS[:, 0:nkb * 512], axis=AX.X, negate=True),
                    reads=[("PS", (0, nkb * 512))], writes=[(p + "mst", (0, 1))],
                    tag=f"nm{b}_{h}_{qi}")
                S.add("scalar", lambda e, nkb=nkb: e.activation(
                    pnb[:, 0:nkb * 512], C.PS[:, 0:nkb * 512], AF.Exp,
                    bias=mst[:, 0:1], accum_out=mst[:, 1:2]),
                    reads=[("PS", (0, nkb * 512)), (p + "mst", (0, 1))],
                    writes=[(p + "pnb", (0, nkb * 512)), (p + "mst", (1, 2))],
                    tag=f"exp{b}_{h}_{qi}")
                S.add("vector", lambda e: e.reciprocal(mst[:, 2:3], mst[:, 1:2]),
                      reads=[(p + "mst", (1, 2))], writes=[(p + "mst", (2, 3))],
                      tag=f"rin{b}_{h}_{qi}")
                S.add("vector", lambda e, nkb=nkb: e.tensor_scalar_mul(
                    pnb[:, 0:nkb * 512], pnb[:, 0:nkb * 512], mst[:, 2:3]),
                    reads=[(p + "pnb", (0, nkb * 512)), (p + "mst", (2, 3))],
                    writes=[(p + "pnb", (0, nkb * 512))], tag=f"nrm{b}_{h}_{qi}")
                psb = C.PS[:, 5 * 512:6 * 512].bitcast(BF16)
                for tg in range((nsc + 3) // 4):
                    ns = min(4, nsc - tg * 4)
                    def ttr(e, tg=tg, ns=ns, psb=psb):
                        ins = None
                        for j in range(ns):
                            ins = e.transpose(psb[:, j * 128:(j + 1) * 128],
                                              pnb[:, (tg * 4 + j) * 128:(tg * 4 + j + 1) * 128],
                                              C.ident_bf)
                        return ins
                    S.add("tensor", ttr,
                          reads=[(p + "pnb", (tg * 512, tg * 512 + ns * 128)),
                                 ("CONSTB", (128, 256))],
                          writes=[("PS", (5 * 512, 6 * 512))], tag=f"tr{b}_{h}_{qi}_{tg}")
                    def tcp(e, tg=tg, ns=ns, qi=qi, psb=psb):
                        ins = None
                        for j in range(ns):
                            sc = tg * 4 + j
                            ins = e.tensor_copy(
                                ptb[:, sc * 512 + qi * 128: sc * 512 + (qi + 1) * 128],
                                psb[:, j * 128:(j + 1) * 128])
                        return ins
                    S.add("vector", tcp,
                          reads=[("PS", (5 * 512, 6 * 512))],
                          writes=[(p + "ptb", (tg * 4 * 512 + qi * 128,
                                               (tg * 4 + ns - 1) * 512 + (qi + 1) * 128))],
                          tag=f"tcp{b}_{h}_{qi}_{tg}")
            nscb = 4 * (b + 1)
            for mc in range(NL):
                def mmx(e, mc=mc, nscb=nscb, b=b):
                    ins = None
                    for sc in range(nscb):
                        toff = (max(4 * b, sc) - 4 * b) * 128
                        ins = e.matmul(C.PS[:, (2 + mc) * 512 + toff:(3 + mc) * 512],
                                       cktm[:, sc * LAT + mc * 128:sc * LAT + (mc + 1) * 128],
                                       ptb[:, sc * 512 + toff:(sc + 1) * 512],
                                       start=(sc == 0), stop=(sc == nscb - 1))
                    return ins
                S.add("tensor", mmx, reads=[(p + "cktm", None), (p + "ptb", None)],
                      writes=[("PS", ((2 + mc) * 512, (3 + mc) * 512))], tag=f"ctx{b}_{h}_{mc}")
                S.add("vector", lambda e, mc=mc: e.tensor_copy(
                    ctsb[:, mc * 512:(mc + 1) * 512], C.PS[:, (2 + mc) * 512:(3 + mc) * 512]),
                    reads=[("PS", ((2 + mc) * 512, (3 + mc) * 512))],
                    writes=[(p + "cts", (mc * 512, (mc + 1) * 512))], tag=f"ctxcp{b}_{h}_{mc}")
            def mmy(e, h=h):
                ins = None
                for mc in range(NL):
                    ins = e.matmul(C.PS[0:64, 4 * 512:5 * 512],
                                   ves[:, mc * 1024 + h * 64:mc * 1024 + h * 64 + 64],
                                   ctsb[:, mc * 512:(mc + 1) * 512],
                                   start=(mc == 0), stop=(mc == NL - 1))
                return ins
            S.add("tensor", mmy, reads=[(p + "ves", None), (p + "cts", None)],
                  writes=[("PS", (4 * 512, 5 * 512))], tag=f"y{b}_{h}")
            S.add("vector", lambda e, hc=hc, ho=ho, b=b: e.tensor_tensor(
                rb[ho:ho + 64, hc * TTW:(hc + 1) * TTW],
                C.PS[0:64, 4 * 512:5 * 512],
                C.xres[hc][ho:ho + 64, PAD + b * TTW:PAD + (b + 1) * TTW], ALU.add),
                reads=[("PS", (4 * 512, 5 * 512)), (f"xres{hc}", bsl)],
                writes=[(p + "rb", (hc * TTW, (hc + 1) * TTW))],
                tag=f"yres{b}_{h}")
        emit_postln_tile(C, b, rb, p + "rb")


def emit_l1attn(C):
    emit_mla(C, "L1a", 256)


def emit_l2attn(C):
    emit_mla(C, "L2a", 128)





def emit_l0attn(C):
    S = C.S
    H, M, DH = 16, 64, 64
    Wqkv = C.din("L0a_Wqkv", [1024, 3072], F32)
    bqkv = C.din("L0a_bqkv", [128, 24], F32)
    Wout = C.din("L0a_Wout", [1024, 1024], F32)
    bout = C.din("L0a_bout", [128, 8], F32)

    wcol = C.sbuf("l0a_wcol", [128, 2 * 1024], BF16)
    qstage = C.sbuf("l0a_qst", [128, T], BF16)
    qlf = C.sbuf("l0a_qlf", [128, 8 * 64], F32)
    klf = C.sbuf("l0a_klf", [128, 8 * 64], F32)
    bia = C.sbuf("l0a_bia", [128, 32], F32)
    qh = C.sbuf("l0a_qh", [64, T], BF16)
    kh = C.sbuf("l0a_kh", [64, T], BF16)
    vh = C.sbuf("l0a_vh", [64, T], BF16)
    vht = C.sbuf("l0a_vht", [128, 32 * 64], BF16)
    s3b = C.sbuf("l0a_s3b", [64, T], BF16)
    a3t = C.sbuf("l0a_a3t", [128, 32 * 64], BF16)
    aos = C.sbuf("l0a_aos", [64, T], BF16)
    psc = C.sbuf("l0a_psc", [64, 8 * 64], BF16)
    a1b = C.sbuf("l0a_a1b", [128, 64], BF16)
    a1ts = C.sbuf("l0a_a1ts", [64, 128], BF16)
    mst = C.sbuf("l0a_mst", [128, 8], F32)
    wosb = C.sbuf("l0a_wo", [128, 8 * 1024], BF16)
    aobuf = C.sbuf("l0a_ao", [128, 8 * 512], BF16)
    rb = C.sbuf("l0a_rb", [128, DC * TTW], F32)

    q_scr = C.dscratch("l0a_qscr", [1024, T], BF16)
    k_scr = C.dscratch("l0a_kscr", [1024, T], BF16)
    v_scr = C.dscratch("l0a_vscr", [1024, T], BF16)
    ao_scr = C.dscratch("l0a_aoscr", [1024, T], BF16)

    S.add("sync", lambda e: e.dma_start(out=bia[:, 0:24], in_=bqkv),
          writes=[("l0a_bia", (0, 24))], tag="ldbqkv")
    S.add("sync", lambda e: e.dma_start(out=bia[:, 24:32], in_=bout),
          writes=[("l0a_bia", (24, 32))], tag="ldbout")
    for g in range(2):
        S.add("gpsimd", lambda e, g=g: e.dma_start(
            out=wosb[:, g * 4 * 1024:(g + 1) * 4 * 1024].rearrange("p (c f) -> p c f", c=4),
            in_=Wout[g * 4 * 128:(g + 1) * 4 * 128, :].rearrange("(c p) f -> p c f", p=128)),
            writes=[("l0a_wo", (g * 4096, (g + 1) * 4096))], weight=16, tag=f"ldwo{g}")

    for fc in range(24):
        slot = (fc % 2) * 1024
        S.add("gpsimd", lambda e, fc=fc, slot=slot: e.dma_start(
            out=wcol[:, slot:slot + 1024].rearrange("p (c f) -> p c f", c=8),
            in_=Wqkv[:, fc * 128:(fc + 1) * 128].rearrange("(c p) f -> p c f", p=128)),
            writes=[("l0a_wcol", (slot, slot + 1024))], weight=16, tag=f"ldwq{fc}")
        for n in range(NT):
            bk = n % 2
            def mmp(e, fc=fc, slot=slot, n=n, bk=bk):
                ins = None
                for k in range(DC):
                    ins = e.matmul(bank(C, bk),
                                   wcol[:, slot + k * 128:slot + (k + 1) * 128],
                                   C.xres[k][:, PAD + n * TTW:PAD + (n + 1) * TTW],
                                   start=(k == 0), stop=(k == DC - 1))
                return ins
            S.add("tensor", mmp,
                  reads=[("l0a_wcol", (slot, slot + 1024))] +
                        [(f"xres{k}", (PAD + n * TTW, PAD + (n + 1) * TTW)) for k in range(DC)],
                  writes=[("PS", (bk * 512, (bk + 1) * 512))], tag=f"qkv{fc}_{n}")
            if fc < 8:
                S.add("vector", lambda e, fc=fc, n=n, bk=bk: e.tensor_scalar(
                    qstage[:, n * 512:(n + 1) * 512], bank(C, bk), bia[:, fc:fc + 1],
                    0.125, ALU.add, ALU.mult),
                    reads=[("PS", (bk * 512, (bk + 1) * 512)), ("l0a_bia", (fc, fc + 1))],
                    writes=[("l0a_qst", (n * 512, (n + 1) * 512))], tag=f"qe{fc}_{n}")
            else:
                S.add("vector", lambda e, fc=fc, n=n, bk=bk: e.tensor_scalar_add(
                    qstage[:, n * 512:(n + 1) * 512], bank(C, bk), bia[:, fc:fc + 1]),
                    reads=[("PS", (bk * 512, (bk + 1) * 512)), ("l0a_bia", (fc, fc + 1))],
                    writes=[("l0a_qst", (n * 512, (n + 1) * 512))], tag=f"qe{fc}_{n}")
            if fc < 16:
                dstlf = qlf if fc < 8 else klf
                dkey = "l0a_qlf" if fc < 8 else "l0a_klf"
                cc = fc % 8
                S.add("vector", lambda e, cc=cc, n=n, dstlf=dstlf: e.tensor_reduce(
                    dstlf[:, cc * 64 + n * 8:cc * 64 + (n + 1) * 8],
                    qstage[:, n * 512:(n + 1) * 512].rearrange("p (g l) -> p g l", g=8),
                    AX.X, ALU.add),
                    reads=[("l0a_qst", (n * 512, (n + 1) * 512))],
                    writes=[(dkey, (cc * 64 + n * 8, cc * 64 + (n + 1) * 8))],
                    tag=f"lm{fc}_{n}")
        scr = q_scr if fc < 8 else (k_scr if fc < 16 else v_scr)
        skey = "l0a_qscr" if fc < 8 else ("l0a_kscr" if fc < 16 else "l0a_vscr")
        cc = fc % 8
        S.add("sync", lambda e, cc=cc, scr=scr: e.dma_start(
            out=scr[cc * 128:(cc + 1) * 128, :], in_=qstage[:, :]),
            reads=[("l0a_qst", None)], writes=[(skey, (cc, cc + 1))], tag=f"st{fc}")
    qlb = C.sbuf("l0a_qlb", [64, 16 * 64], BF16)
    klb = C.sbuf("l0a_klb", [64, 16 * 64], BF16)
    for hh in range(16):
        hcc, hoo = hh // 2, (hh % 2) * 64
        S.add("vector", lambda e, hh=hh, hcc=hcc, hoo=hoo: e.tensor_scalar_mul(
            qlb[:, hh * 64:(hh + 1) * 64], qlf[hoo:hoo + 64, hcc * 64:(hcc + 1) * 64],
            1.0 / 64.0),
            reads=[("l0a_qlf", (hcc * 64, (hcc + 1) * 64))],
            writes=[("l0a_qlb", (hh * 64, (hh + 1) * 64))], tag=f"qlm{hh}")
        S.add("vector", lambda e, hh=hh, hcc=hcc, hoo=hoo: e.tensor_scalar_mul(
            klb[:, hh * 64:(hh + 1) * 64], klf[hoo:hoo + 64, hcc * 64:(hcc + 1) * 64],
            1.0 / 64.0),
            reads=[("l0a_klf", (hcc * 64, (hcc + 1) * 64))],
            writes=[("l0a_klb", (hh * 64, (hh + 1) * 64))], tag=f"klm{hh}")

    I64 = C.ident_bf[0:64, 0:64]
    psb = C.PS[:, 5 * 512:6 * 512].bitcast(BF16)

    for h in range(H):
        hc, ho = h // 2, (h % 2) * 64
        S.add("sync", lambda e, h=h: e.dma_start(out=qh[:, :], in_=q_scr[h * 64:(h + 1) * 64, :]),
              reads=[("l0a_qscr", (hc, hc + 1))], writes=[("l0a_qh", None)], tag=f"ldq{h}")
        S.add("sync", lambda e, h=h: e.dma_start(out=kh[:, :], in_=k_scr[h * 64:(h + 1) * 64, :]),
              reads=[("l0a_kscr", (hc, hc + 1))], writes=[("l0a_kh", None)], tag=f"ldk{h}")
        S.add("sync", lambda e, h=h: e.dma_start(out=vh[:, :], in_=v_scr[h * 64:(h + 1) * 64, :]),
              reads=[("l0a_vscr", (hc, hc + 1))], writes=[("l0a_vh", None)], tag=f"ldv{h}")
        for tg in range(8):
            def vtr(e, tg=tg):
                ins = None
                for j in range(4):
                    tc = tg * 4 + j
                    ins = e.transpose(psb[0:128, j * 64:(j + 1) * 64],
                                      vh[:, tc * 128:(tc + 1) * 128], I64)
                return ins
            S.add("tensor", vtr,
                  reads=[("l0a_vh", (tg * 512, (tg + 1) * 512)), ("CONSTB", (128, 256))],
                  writes=[("PS", (5 * 512, 6 * 512))], tag=f"vtr{h}_{tg}")
            def vcp(e, tg=tg):
                ins = None
                for j in range(4):
                    tc = tg * 4 + j
                    ins = e.tensor_copy(vht[:, tc * 64:(tc + 1) * 64],
                                        psb[0:128, j * 64:(j + 1) * 64])
                return ins
            S.add("vector", vcp, reads=[("PS", (5 * 512, 6 * 512))],
                  writes=[("l0a_vht", (tg * 256, (tg + 1) * 256))], tag=f"vcp{h}_{tg}")
        def mm3(e, h=h):
            ins = None
            for n in range(NT):
                ins = e.matmul(C.PS[0:64, n * 512:(n + 1) * 512],
                               qlb[:, h * 64:(h + 1) * 64],
                               kh[:, n * 512:(n + 1) * 512], start=True, stop=True)
            return ins
        S.add("tensor", mm3,
              reads=[("l0a_qlb", (h * 64, (h + 1) * 64)), ("l0a_kh", None)],
              writes=[("PS", (0, 4096))], tag=f"s3{h}")
        S.add("vector", lambda e: e.reduce_max(mst[0:64, 0:1], C.PS[0:64, :], axis=AX.X,
                                               negate=True),
              reads=[("PS", (0, 4096))], writes=[("l0a_mst", (0, 1))], tag=f"s3m{h}")
        S.add("scalar", lambda e: e.activation(s3b[:, :], C.PS[0:64, :], AF.Exp,
                                               bias=mst[0:64, 0:1], accum_out=mst[0:64, 1:2]),
              reads=[("PS", (0, 4096)), ("l0a_mst", (0, 1))],
              writes=[("l0a_s3b", None), ("l0a_mst", (1, 2))], tag=f"s3e{h}")
        S.add("vector", lambda e: e.reciprocal(mst[0:64, 2:3], mst[0:64, 1:2]),
              reads=[("l0a_mst", (1, 2))], writes=[("l0a_mst", (2, 3))], tag=f"s3r{h}")
        for tg in range(8):
            def atr(e, tg=tg):
                ins = None
                for j in range(4):
                    tc = tg * 4 + j
                    ins = e.transpose(psb[0:128, j * 64:(j + 1) * 64],
                                      s3b[:, tc * 128:(tc + 1) * 128], I64)
                return ins
            S.add("tensor", atr,
                  reads=[("l0a_s3b", (tg * 512, (tg + 1) * 512)), ("CONSTB", (128, 256))],
                  writes=[("PS", (5 * 512, 6 * 512))], tag=f"atr{h}_{tg}")
            def acp(e, tg=tg):
                ins = None
                for j in range(4):
                    tc = tg * 4 + j
                    ins = e.tensor_copy(a3t[:, tc * 64:(tc + 1) * 64],
                                       psb[0:128, j * 64:(j + 1) * 64])
                return ins
            S.add("vector", acp, reads=[("PS", (5 * 512, 6 * 512))],
                  writes=[("l0a_a3t", (tg * 256, (tg + 1) * 256))], tag=f"acp{h}_{tg}")
        def mmu(e):
            ins = None
            for sc in range(32):
                ins = e.matmul(C.PS[0:64, 4 * 512:4 * 512 + 64],
                               a3t[:, sc * 64:(sc + 1) * 64],
                               vht[:, sc * 64:(sc + 1) * 64],
                               start=(sc == 0), stop=(sc == 31))
            return ins
        S.add("tensor", mmu, reads=[("l0a_a3t", None), ("l0a_vht", None)],
              writes=[("PS", (4 * 512, 4 * 512 + 64))], tag=f"U{h}")
        S.add("vector", lambda e: e.tensor_scalar_mul(psc[:, 7 * 64:8 * 64],
                                                      C.PS[0:64, 4 * 512:4 * 512 + 64],
                                                      mst[0:64, 2:3]),
              reads=[("PS", (4 * 512, 4 * 512 + 64)), ("l0a_mst", (2, 3))],
              writes=[("l0a_psc", (7 * 64, 8 * 64))], tag=f"Ue{h}")
        S.add("tensor", lambda e, h=h: e.matmul(
            C.PS[0:64, 4 * 512:4 * 512 + 64], qlb[:, h * 64:(h + 1) * 64],
            klb[:, h * 64:(h + 1) * 64], start=True, stop=True),
            reads=[("l0a_qlb", (h * 64, (h + 1) * 64)), ("l0a_klb", (h * 64, (h + 1) * 64))],
            writes=[("PS", (4 * 512, 4 * 512 + 64))], tag=f"s2{h}")
        S.add("vector", lambda e: e.reduce_max(mst[0:64, 3:4], C.PS[0:64, 4 * 512:4 * 512 + 64],
                                               axis=AX.X, negate=True),
              reads=[("PS", (4 * 512, 4 * 512 + 64))], writes=[("l0a_mst", (3, 4))],
              tag=f"s2m{h}")
        S.add("scalar", lambda e: e.activation(psc[:, 0:64], C.PS[0:64, 4 * 512:4 * 512 + 64],
                                               AF.Exp, bias=mst[0:64, 3:4],
                                               accum_out=mst[0:64, 4:5]),
              reads=[("PS", (4 * 512, 4 * 512 + 64)), ("l0a_mst", (3, 4))],
              writes=[("l0a_psc", (0, 64)), ("l0a_mst", (4, 5))], tag=f"s2e{h}")
        S.add("vector", lambda e: e.reciprocal(mst[0:64, 5:6], mst[0:64, 4:5]),
              reads=[("l0a_mst", (4, 5))], writes=[("l0a_mst", (5, 6))], tag=f"s2r{h}")
        S.add("vector", lambda e: e.tensor_scalar_mul(psc[:, 0:64], psc[:, 0:64],
                                                      mst[0:64, 5:6]),
              reads=[("l0a_psc", (0, 64)), ("l0a_mst", (5, 6))],
              writes=[("l0a_psc", (0, 64))], tag=f"s2n{h}")
        S.add("vector", lambda e: e.tensor_copy(psc[:, 2 * 64:3 * 64], psc[:, 0:64]),
              reads=[("l0a_psc", (0, 64))], writes=[("l0a_psc", (2 * 64, 3 * 64))],
              tag=f"vi{h}")
        def tr64(src_sl, dst_sl, tg):
            S.add("tensor", lambda e: e.transpose(psb[0:64, 0:64],
                                                  psc[:, src_sl[0]:src_sl[1]], I64),
                  reads=[("l0a_psc", src_sl), ("CONSTB", (128, 256))],
                  writes=[("PS", (5 * 512, 6 * 512))], tag=f"t64a{tg}")
            S.add("vector", lambda e: e.tensor_copy(psc[:, dst_sl[0]:dst_sl[1]],
                                                    psb[0:64, 0:64]),
                  reads=[("PS", (5 * 512, 6 * 512))], writes=[("l0a_psc", dst_sl)],
                  tag=f"t64b{tg}")
        tr64((0, 64), (64, 128), f"a2T{h}")
        for it in range(6):
            S.add("tensor", lambda e: e.matmul(C.PS[0:64, 4 * 512:4 * 512 + 64],
                                               psc[:, 64:128], psc[:, 2 * 64:3 * 64],
                                               start=True, stop=True),
                  reads=[("l0a_psc", (64, 128)), ("l0a_psc", (2 * 64, 3 * 64))],
                  writes=[("PS", (4 * 512, 4 * 512 + 64))], tag=f"kv{h}_{it}")
            S.add("vector", lambda e: e.tensor_copy(psc[:, 4 * 64:5 * 64],
                                                    C.PS[0:64, 4 * 512:4 * 512 + 64]),
                  reads=[("PS", (4 * 512, 4 * 512 + 64))],
                  writes=[("l0a_psc", (4 * 64, 5 * 64))], tag=f"kvc{h}_{it}")
            tr64((4 * 64, 5 * 64), (5 * 64, 6 * 64), f"kvT{h}_{it}")
            S.add("vector", lambda e: e.scalar_tensor_tensor(
                psc[:, 6 * 64:7 * 64], I64, 7.0, psc[:, 4 * 64:5 * 64],
                ALU.mult, ALU.subtract),
                reads=[("CONSTB", (128, 256)), ("l0a_psc", (4 * 64, 5 * 64))],
                writes=[("l0a_psc", (6 * 64, 7 * 64))], tag=f"t7{h}_{it}")
            S.add("tensor", lambda e: e.matmul(C.PS[0:64, 4 * 512:4 * 512 + 64],
                                               psc[:, 5 * 64:6 * 64], psc[:, 6 * 64:7 * 64],
                                               start=True, stop=True),
                  reads=[("l0a_psc", (5 * 64, 6 * 64)), ("l0a_psc", (6 * 64, 7 * 64))],
                  writes=[("PS", (4 * 512, 4 * 512 + 64))], tag=f"kt{h}_{it}")
            S.add("vector", lambda e: e.scalar_tensor_tensor(
                psc[:, 6 * 64:7 * 64], I64, 15.0, C.PS[0:64, 4 * 512:4 * 512 + 64],
                ALU.mult, ALU.subtract),
                reads=[("CONSTB", (128, 256)), ("PS", (4 * 512, 4 * 512 + 64))],
                writes=[("l0a_psc", (6 * 64, 7 * 64))], tag=f"t15{h}_{it}")
            S.add("tensor", lambda e: e.matmul(C.PS[0:64, 4 * 512:4 * 512 + 64],
                                               psc[:, 5 * 64:6 * 64], psc[:, 6 * 64:7 * 64],
                                               start=True, stop=True),
                  reads=[("l0a_psc", (5 * 64, 6 * 64)), ("l0a_psc", (6 * 64, 7 * 64))],
                  writes=[("PS", (4 * 512, 4 * 512 + 64))], tag=f"kt2{h}_{it}")
            S.add("vector", lambda e: e.scalar_tensor_tensor(
                psc[:, 6 * 64:7 * 64], I64, 13.0, C.PS[0:64, 4 * 512:4 * 512 + 64],
                ALU.mult, ALU.subtract),
                reads=[("CONSTB", (128, 256)), ("PS", (4 * 512, 4 * 512 + 64))],
                writes=[("l0a_psc", (6 * 64, 7 * 64))], tag=f"t13{h}_{it}")
            tr64((2 * 64, 3 * 64), (3 * 64, 4 * 64), f"vT{h}_{it}")
            S.add("tensor", lambda e: e.matmul(C.PS[0:64, 4 * 512:4 * 512 + 64],
                                               psc[:, 3 * 64:4 * 64], psc[:, 6 * 64:7 * 64],
                                               start=True, stop=True),
                  reads=[("l0a_psc", (3 * 64, 4 * 64)), ("l0a_psc", (6 * 64, 7 * 64))],
                  writes=[("PS", (4 * 512, 4 * 512 + 64))], tag=f"vn{h}_{it}")
            S.add("vector", lambda e: e.tensor_scalar_mul(psc[:, 2 * 64:3 * 64],
                                                          C.PS[0:64, 4 * 512:4 * 512 + 64],
                                                          0.25),
                  reads=[("PS", (4 * 512, 4 * 512 + 64))],
                  writes=[("l0a_psc", (2 * 64, 3 * 64))], tag=f"vne{h}_{it}")
        tr64((2 * 64, 3 * 64), (3 * 64, 4 * 64), f"vTf{h}")
        S.add("tensor", lambda e: e.matmul(C.PS[0:64, 4 * 512:4 * 512 + 64],
                                           psc[:, 3 * 64:4 * 64], psc[:, 7 * 64:8 * 64],
                                           start=True, stop=True),
              reads=[("l0a_psc", (3 * 64, 4 * 64)), ("l0a_psc", (7 * 64, 8 * 64))],
              writes=[("PS", (4 * 512, 4 * 512 + 64))], tag=f"wm{h}")
        S.add("vector", lambda e: e.tensor_copy(psc[:, 6 * 64:7 * 64],
                                                C.PS[0:64, 4 * 512:4 * 512 + 64]),
              reads=[("PS", (4 * 512, 4 * 512 + 64))],
              writes=[("l0a_psc", (6 * 64, 7 * 64))], tag=f"wmc{h}")
        for tc in range(32):
            S.add("tensor", lambda e, tc=tc, h=h: e.matmul(
                C.PS[:, 4 * 512 + 128:4 * 512 + 192], qh[:, tc * 128:(tc + 1) * 128],
                klb[:, h * 64:(h + 1) * 64], start=True, stop=True),
                reads=[("l0a_qh", (tc * 128, (tc + 1) * 128)),
                       ("l0a_klb", (h * 64, (h + 1) * 64))],
                writes=[("PS", (4 * 512 + 128, 4 * 512 + 192))], tag=f"s1{h}_{tc}")
            S.add("vector", lambda e: e.reduce_max(mst[:, 6:7],
                                                   C.PS[:, 4 * 512 + 128:4 * 512 + 192],
                                                   axis=AX.X, negate=True),
                  reads=[("PS", (4 * 512 + 128, 4 * 512 + 192))],
                  writes=[("l0a_mst", (6, 7))], tag=f"s1m{h}_{tc}")
            S.add("scalar", lambda e: e.activation(a1b[:, :],
                                                   C.PS[:, 4 * 512 + 128:4 * 512 + 192],
                                                   AF.Exp, bias=mst[:, 6:7],
                                                   accum_out=mst[:, 7:8]),
                  reads=[("PS", (4 * 512 + 128, 4 * 512 + 192)), ("l0a_mst", (6, 7))],
                  writes=[("l0a_a1b", None), ("l0a_mst", (7, 8))], tag=f"s1e{h}_{tc}")
            S.add("vector", lambda e: e.reciprocal(mst[:, 6:7], mst[:, 7:8]),
                  reads=[("l0a_mst", (7, 8))], writes=[("l0a_mst", (6, 7))],
                  tag=f"s1r{h}_{tc}")
            S.add("vector", lambda e: e.tensor_scalar_mul(a1b[:, :], a1b[:, :], mst[:, 6:7]),
                  reads=[("l0a_a1b", None), ("l0a_mst", (6, 7))],
                  writes=[("l0a_a1b", None)], tag=f"s1n{h}_{tc}")
            S.add("tensor", lambda e: e.transpose(psb[0:64, 0:128], a1b[:, :], C.ident_bf),
                  reads=[("l0a_a1b", None), ("CONSTB", (128, 256))],
                  writes=[("PS", (5 * 512, 6 * 512))], tag=f"a1t{h}_{tc}")
            S.add("vector", lambda e: e.tensor_copy(a1ts[:, :], psb[0:64, 0:128]),
                  reads=[("PS", (5 * 512, 6 * 512))],
                  writes=[("l0a_a1ts", None)], tag=f"a1c{h}_{tc}")
            S.add("tensor", lambda e: e.matmul(C.PS[0:64, 4 * 512 + 256:4 * 512 + 384],
                                               psc[:, 6 * 64:7 * 64], a1ts[:, :],
                                               start=True, stop=True),
                  reads=[("l0a_psc", (6 * 64, 7 * 64)), ("l0a_a1ts", None)],
                  writes=[("PS", (4 * 512 + 256, 4 * 512 + 384))], tag=f"oh{h}_{tc}")
            S.add("vector", lambda e, tc=tc: e.tensor_copy(
                aos[:, tc * 128:(tc + 1) * 128], C.PS[0:64, 4 * 512 + 256:4 * 512 + 384]),
                reads=[("PS", (4 * 512 + 256, 4 * 512 + 384))],
                writes=[("l0a_aos", (tc * 128, (tc + 1) * 128))], tag=f"ohc{h}_{tc}")
        S.add("sync", lambda e, h=h: e.dma_start(out=ao_scr[h * 64:(h + 1) * 64, :],
                                                 in_=aos[:, :]),
              reads=[("l0a_aos", None)], writes=[("l0a_aoscr", (hc, hc + 1))], tag=f"stao{h}")

    for n in range(NT):
        tsl = (PAD + n * TTW, PAD + (n + 1) * TTW)
        S.add("sync", lambda e, n=n: e.dma_start(
            out=aobuf[:, :].rearrange("p (c f) -> p c f", c=8),
            in_=ao_scr[:, n * TTW:(n + 1) * TTW].rearrange("(c p) f -> p c f", p=128)),
            reads=[("l0a_aoscr", (0, 8))], writes=[("l0a_ao", None)], tag=f"ldao{n}")
        for mo in range(DC):
            def mmo(e, mo=mo):
                ins = None
                for k in range(DC):
                    ins = e.matmul(bank(C, mo),
                                   wosb[:, k * 1024 + mo * 128:k * 1024 + (mo + 1) * 128],
                                   aobuf[:, k * 512:(k + 1) * 512],
                                   start=(k == 0), stop=(k == DC - 1))
                return ins
            S.add("tensor", mmo, reads=[("l0a_wo", None), ("l0a_ao", None)],
                  writes=[("PS", (mo * 512, (mo + 1) * 512))], tag=f"l0mmo{n}_{mo}")
        for mo in range(DC):
            S.add("vector", lambda e, mo=mo, n=n: e.scalar_tensor_tensor(
                rb[:, mo * TTW:(mo + 1) * TTW], bank(C, mo), bia[:, 24 + mo:25 + mo],
                C.xres[mo][:, PAD + n * TTW:PAD + (n + 1) * TTW], ALU.add, ALU.add),
                reads=[("PS", (mo * 512, (mo + 1) * 512)), ("l0a_bia", (24 + mo, 25 + mo)),
                       (f"xres{mo}", tsl)],
                writes=[("l0a_rb", (mo * TTW, (mo + 1) * TTW))], tag=f"l0res{n}_{mo}")
        emit_postln_tile(C, n, rb, "l0a_rb")





def emit_l3attn(C):
    S = C.S
    EC = 16
    NST = 16
    GA_COLS = 50176

    ga = C.sbuf("l3ga", [128, GA_COLS], BF16)
    K_GA = "l3ga"
    O_XIN = 0
    O_YST = 0
    O_ZST = 8240
    O_XCS = 16432
    O_DLS = 24624
    O_NRM = 24624
    O_DUS = 32816
    O_H0T = 32816
    O_WCL = 41008
    O_SEG = 43056
    O_BCB = 45104
    O_WO2 = 43056
    O_HOS = 41008
    O_H1R = 0
    O_FWC = 32768
    O_AST = 33792
    O_GST = 37888
    O_ALD = 41984
    O_VST = 46080
    O_FW2 = 0
    O_VGR = 8192

    def gap(a, b):
        return ga[:, a:b]

    def grg(a, b):
        return (K_GA, (a, b))

    sm = C.sbuf("l3a_sm", [128, 216], F32)
    acol = C.sbuf("l3a_ac", [128, 256], F32)
    hcar = C.sbuf("l3a_hc", [128, 256], F32)
    dtsb = C.sbuf("l3a_dtw", [64, 2048], BF16)
    xpsb = C.sbuf("l3a_xpw", [128, 16 * 96], BF16)
    sel16 = C.sbuf("l3a_sel", [16, 16 * 128], BF16)
    dts = C.sbuf("l3a_dts", [64, 512], BF16)
    bcur = C.sbuf("l3a_bcu", [16, 512], BF16)
    ccur = C.sbuf("l3a_ccu", [16, 512], BF16)
    xcar = C.sbuf("l3a_xcar", [128, 48], BF16)
    rb = C.sbuf("l3a_rb", [128, 4096], F32)
    K_RB = "l3a_rb"

    SEL = C.din("c_sel16", [16, 16 * 128], BF16)
    S.add("sync", lambda e: e.dma_start(out=sel16[:, :], in_=SEL),
          writes=[("l3a_sel", None)], tag="ldsel16")

    FW1 = C.din("L3A_fw1T", [1024, 8192], F32)
    FW2 = C.din("L3A_fw2T", [4096, 1024], F32)
    FB1 = C.din("L3A_fb1", [128, 64], F32)
    FB2 = C.din("L3A_fb2", [128, 8], F32)

    h_scr = [C.dscratch(f"l3a_h{i}", [1024, T], BF16) for i in range(2)]
    au_scr = C.dscratch("l3a_au", [4096, T], BF16)
    v_scr = C.dscratch("l3a_v", [4096, T], BF16)
    inw_scr = [C.dscratch(f"l3a_inw{i}", [1024, 4096], BF16) for i in range(2)]
    ow_scr = [C.dscratch(f"l3a_ow{i}", [2048, 1024], BF16) for i in range(2)]
    fw2_scr = C.dscratch("l3a_fw2s", [4096, 1024], BF16)

    fc1 = C.fconst(1.0)
    fcm1 = C.fconst(-1.0)

    def precast(src, scr, skey, rows, F, tagp):
        for c in range(rows // 128):
            S.add("gpsimd", lambda e, c=c: e.dma_start(
                out=gap(O_XIN, O_XIN + F), in_=src[c * 128:(c + 1) * 128, :]),
                writes=[grg(O_XIN, O_XIN + F)], weight=16, tag=f"{tagp}c{c}")
            S.add("sync", lambda e, c=c: e.dma_start(
                out=scr[c * 128:(c + 1) * 128, :], in_=gap(O_XIN, O_XIN + F)),
                reads=[grg(O_XIN, O_XIN + F)], writes=[(skey, (c, c + 1))],
                tag=f"{tagp}s{c}")

    for i in range(2):
        INW = C.din(f"L3A_inwT{i}", [1024, 4096], F32)
        OW = C.din(f"L3A_owT{i}", [2048, 1024], F32)
        precast(INW, inw_scr[i], f"l3a_inwscr{i}", 1024, 4096, f"pci{i}")
        precast(OW, ow_scr[i], f"l3a_owscr{i}", 2048, 1024, f"pco{i}")
    precast(FW2, fw2_scr, "l3a_fw2scr", 4096, 1024, "pcf")

    for i in range(2):
        CW = C.din(f"L3A_cw{i}", [128, 64], F32)
        CB = C.din(f"L3A_cb{i}", [128, 16], F32)
        DTB = C.din(f"L3A_dtb{i}", [128, 16], F32)
        DD = C.din(f"L3A_D{i}", [128, 16], F32)
        RLN = C.din(f"L3A_rln{i}", [128, 24], F32)
        ALOG = C.din(f"L3A_alog{i}", [128, 256], F32)
        XPW = C.din(f"L3A_xpwT{i}", [2048, 96], F32)
        DTW = C.din(f"L3A_dtwT{i}", [64, 2048], F32)

        S.add("sync", lambda e, CW=CW: e.dma_start(out=sm[:, 0:64], in_=CW),
              writes=[("l3a_sm", (0, 64))], tag=f"ldcw{i}")
        S.add("sync", lambda e, CB=CB: e.dma_start(out=sm[:, 64:80], in_=CB),
              writes=[("l3a_sm", (64, 80))], tag=f"ldcb{i}")
        S.add("sync", lambda e, DTB=DTB: e.dma_start(out=sm[:, 80:96], in_=DTB),
              writes=[("l3a_sm", (80, 96))], tag=f"lddtb{i}")
        S.add("sync", lambda e, DD=DD: e.dma_start(out=sm[:, 96:112], in_=DD),
              writes=[("l3a_sm", (96, 112))], tag=f"ldD{i}")
        S.add("sync", lambda e, RLN=RLN: e.dma_start(out=sm[:, 112:136], in_=RLN),
              writes=[("l3a_sm", (112, 136))], tag=f"ldrln{i}")
        S.add("sync", lambda e, ALOG=ALOG: e.dma_start(out=acol[:, :], in_=ALOG),
              writes=[("l3a_ac", None)], tag=f"ldal{i}")
        S.add("gpsimd", lambda e, XPW=XPW: e.dma_start(
            out=xpsb[:, :].rearrange("p (c f) -> p c f", c=16),
            in_=XPW.rearrange("(c p) f -> p c f", p=128)),
            writes=[("l3a_xpw", None)], weight=16, tag=f"ldxpw{i}")
        S.add("gpsimd", lambda e, DTW=DTW: e.dma_start(out=dtsb[:, :], in_=DTW),
              writes=[("l3a_dtw", None)], weight=16, tag=f"lddtw{i}")
        S.add("scalar", lambda e: e.activation(acol[:, :], acol[:, :], AF.Exp),
              reads=[("l3a_ac", None)], writes=[("l3a_ac", None)], tag=f"aexp{i}")
        S.add("vector", lambda e: e.tensor_scalar_mul(acol[:, :], acol[:, :], -1.0),
              reads=[("l3a_ac", None)], writes=[("l3a_ac", None)], tag=f"aneg{i}")
        S.add("vector", lambda e: e.memset(hcar[:, :], 0.0),
              writes=[("l3a_hc", None)], tag=f"hc0{i}")
        S.add("vector", lambda e: e.memset(xcar[:, :], 0.0),
              writes=[("l3a_xcar", None)], tag=f"xc0{i}")

        rmsw = [sm[:, 112 + k:113 + k] for k in range(8)]
        lnw = [sm[:, 120 + k:121 + k] for k in range(8)]
        lnbv = [sm[:, 128 + k:129 + k] for k in range(8)]

        for n in range(NT):
            tsl = (PAD + n * TTW, PAD + (n + 1) * TTW)
            if i == 0:
                def src(k, n=n):
                    return (C.xres[k][:, PAD + n * TTW:PAD + (n + 1) * TTW],
                            [(f"xres{k}", tsl)])
            else:
                S.add("sync", lambda e, n=n: e.dma_start(
                    out=gap(O_H0T, O_H0T + 4096).rearrange("p (c f) -> p c f", c=8),
                    in_=h_scr[0][:, n * TTW:(n + 1) * TTW].rearrange("(c p) f -> p c f", p=128)),
                    reads=[("l3a_hscr0", (0, 8))],
                    writes=[grg(O_H0T, O_H0T + 4096)], tag=f"ldh0_{n}")
                for k in range(DC):
                    S.add("vector", lambda e, k=k, n=n: e.tensor_tensor(
                        rb[:, k * 512:(k + 1) * 512],
                        C.xres[k][:, PAD + n * TTW:PAD + (n + 1) * TTW],
                        gap(O_H0T + k * 512, O_H0T + (k + 1) * 512), ALU.add),
                        reads=[(f"xres{k}", tsl), grg(O_H0T + k * 512, O_H0T + (k + 1) * 512)],
                        writes=[(K_RB, (k * 512, (k + 1) * 512))], tag=f"res1_{n}_{k}")
                def src(k, n=n):
                    return (rb[:, k * 512:(k + 1) * 512], [(K_RB, (k * 512, (k + 1) * 512))])

            def nout(k):
                return (gap(O_NRM + k * 512, O_NRM + (k + 1) * 512),
                        grg(O_NRM + k * 512, O_NRM + (k + 1) * 512))
            emit_ln_core(C, src, nout, gamma=rmsw, rms=True,
                         gb_reads=[("l3a_sm", (112, 120))], tag=f"m{i}n{n}_")
            for ep in range(16):
                S.add("sync", lambda e, ep=ep, i=i: e.dma_start(
                    out=gap(O_WCL, O_WCL + 2048).rearrange("p (c f) -> p c f", c=8),
                    in_=inw_scr[i][:, ep * 256:(ep + 1) * 256].rearrange("(c p) f -> p c f", p=128)),
                    reads=[(f"l3a_inwscr{i}", (0, 8))],
                    writes=[grg(O_WCL, O_WCL + 2048)], tag=f"ldinw{i}_{n}_{ep}")
                for c2 in range(2):
                    eci = ep * 2 + c2
                    bk = eci % 2
                    def mmi(e, c2=c2, bk=bk):
                        ins = None
                        for k in range(DC):
                            ins = e.matmul(bank(C, bk),
                                           gap(O_WCL + k * 256 + c2 * 128,
                                               O_WCL + k * 256 + (c2 + 1) * 128),
                                           gap(O_NRM + k * 512, O_NRM + (k + 1) * 512),
                                           start=(k == 0), stop=(k == DC - 1))
                        return ins
                    S.add("tensor", mmi,
                          reads=[grg(O_WCL, O_WCL + 2048), grg(O_NRM, O_NRM + 4096)],
                          writes=[("PS", (bk * 512, (bk + 1) * 512))],
                          tag=f"mmi{i}_{n}_{eci}")
                    if eci < 16:
                        dst = (O_XIN + eci * 515 + 3, O_XIN + eci * 515 + 515)
                    else:
                        dst = (O_ZST + (eci - 16) * 512, O_ZST + (eci - 15) * 512)
                    S.add("vector", lambda e, dst=dst, bk=bk: e.tensor_copy(
                        gap(dst[0], dst[1]), bank(C, bk)),
                        reads=[("PS", (bk * 512, (bk + 1) * 512))],
                        writes=[grg(dst[0], dst[1])], tag=f"ev{i}_{n}_{eci}")
            xin3 = gap(O_XIN, O_XIN + 16 * 515).rearrange("p (c w) -> p c w", c=16)
            S.add("vector", lambda e: e.tensor_copy(xin3[:, :, 0:3],
                                                    xcar[:, :].rearrange("p (c w) -> p c w", c=16)),
                  reads=[("l3a_xcar", None)], writes=[grg(O_XIN, O_XIN + 16 * 515)],
                  tag=f"cin{i}_{n}")
            for ec in range(EC):
                base = O_XIN + ec * 515
                ct = gap(O_SEG, O_SEG + 512)
                S.add("vector", lambda e, base=base, ec=ec: e.tensor_single_scalar(
                    ct, gap(base + 0, base + 512), sm[:, ec * 4:ec * 4 + 1], ALU.mult),
                    reads=[grg(base, base + 515), ("l3a_sm", (0, 64))],
                    writes=[grg(O_SEG, O_SEG + 512)], tag=f"cv{i}_{n}_{ec}_0")
                for kk in range(1, 4):
                    S.add("vector", lambda e, base=base, kk=kk, ec=ec: e.scalar_tensor_tensor(
                        ct, gap(base + kk, base + kk + 512), sm[:, ec * 4 + kk:ec * 4 + kk + 1],
                        ct, ALU.mult, ALU.add),
                        reads=[grg(base, base + 515), ("l3a_sm", (0, 64)),
                               grg(O_SEG, O_SEG + 512)],
                        writes=[grg(O_SEG, O_SEG + 512)], tag=f"cv{i}_{n}_{ec}_{kk}")
                S.add("scalar", lambda e, ec=ec: e.activation(
                    gap(O_XCS + ec * 512, O_XCS + (ec + 1) * 512), ct, AF.Silu,
                    bias=sm[:, 64 + ec:65 + ec]),
                    reads=[grg(O_SEG, O_SEG + 512), ("l3a_sm", (64, 80))],
                    writes=[grg(O_XCS + ec * 512, O_XCS + (ec + 1) * 512)],
                    tag=f"cs{i}_{n}_{ec}")
            S.add("vector", lambda e: e.tensor_copy(xcar[:, :].rearrange("p (c w) -> p c w", c=16),
                                                    xin3[:, :, 512:515]),
                  reads=[grg(O_XIN, O_XIN + 16 * 515)], writes=[("l3a_xcar", None)],
                  tag=f"cout{i}_{n}")
            for ec in range(EC):
                def mmxd(e, ec=ec):
                    e.matmul(C.PS[0:64, 2 * 512:3 * 512], xpsb[:, ec * 96:ec * 96 + 64],
                             gap(O_XCS + ec * 512, O_XCS + (ec + 1) * 512),
                             start=(ec == 0), stop=(ec == EC - 1))
                    e.matmul(C.PS[0:16, 3 * 512:4 * 512], xpsb[:, ec * 96 + 64:ec * 96 + 80],
                             gap(O_XCS + ec * 512, O_XCS + (ec + 1) * 512),
                             start=(ec == 0), stop=(ec == EC - 1))
                    return e.matmul(C.PS[0:16, 4 * 512:5 * 512],
                                    xpsb[:, ec * 96 + 80:ec * 96 + 96],
                                    gap(O_XCS + ec * 512, O_XCS + (ec + 1) * 512),
                                    start=(ec == 0), stop=(ec == EC - 1))
                S.add("tensor", mmxd,
                    reads=[("l3a_xpw", None), grg(O_XCS + ec * 512, O_XCS + (ec + 1) * 512)],
                    writes=[("PS", (2 * 512, 5 * 512))], tag=f"xd{i}_{n}_{ec}")
            S.add("vector", lambda e: e.tensor_copy(dts[:, :], C.PS[0:64, 2 * 512:3 * 512]),
                  reads=[("PS", (2 * 512, 3 * 512))], writes=[("l3a_dts", None)],
                  tag=f"dtc{i}_{n}")
            S.add("vector", lambda e: e.tensor_copy(bcur[:, :], C.PS[0:16, 3 * 512:4 * 512]),
                  reads=[("PS", (3 * 512, 4 * 512))], writes=[("l3a_bcu", None)],
                  tag=f"bcc{i}_{n}")
            S.add("vector", lambda e: e.tensor_copy(ccur[:, :], C.PS[0:16, 4 * 512:5 * 512]),
                  reads=[("PS", (4 * 512, 5 * 512))], writes=[("l3a_ccu", None)],
                  tag=f"ccc{i}_{n}")
            for ec in range(EC):
                S.add("tensor", lambda e, ec=ec: e.matmul(
                    C.PS[:, 3 * 512:4 * 512], dtsb[:, ec * 128:(ec + 1) * 128],
                    dts[:, :], start=True, stop=True),
                    reads=[("l3a_dtw", None), ("l3a_dts", None)],
                    writes=[("PS", (3 * 512, 4 * 512))], tag=f"dmm{i}_{n}_{ec}")
                t1 = rb[:, 0:512]
                t2 = rb[:, 512:1024]
                S.add("scalar", lambda e, ec=ec: e.activation(
                    t2, C.PS[:, 3 * 512:4 * 512], AF.Abs, bias=sm[:, 80 + ec:81 + ec]),
                    reads=[("PS", (3 * 512, 4 * 512)), ("l3a_sm", (80, 96))],
                    writes=[(K_RB, (512, 1024))], tag=f"sp1{i}_{n}_{ec}")
                S.add("scalar", lambda e: e.activation(t2, t2, AF.Exp, scale=-1.0),
                      reads=[(K_RB, (512, 1024)), fcm1], writes=[(K_RB, (512, 1024))],
                      tag=f"sp2{i}_{n}_{ec}")
                S.add("scalar", lambda e: e.activation(t2, t2, AF.Ln, bias=1.0),
                      reads=[(K_RB, (512, 1024)), fc1], writes=[(K_RB, (512, 1024))],
                      tag=f"sp3{i}_{n}_{ec}")
                S.add("scalar", lambda e, ec=ec: e.activation(
                    t1, C.PS[:, 3 * 512:4 * 512], AF.Relu, bias=sm[:, 80 + ec:81 + ec]),
                    reads=[("PS", (3 * 512, 4 * 512)), ("l3a_sm", (80, 96))],
                    writes=[(K_RB, (0, 512))], tag=f"sp4{i}_{n}_{ec}")
                S.add("vector", lambda e, ec=ec: e.tensor_tensor(
                    gap(O_DLS + ec * 512, O_DLS + (ec + 1) * 512), t1, t2, ALU.add),
                    reads=[(K_RB, (0, 1024))],
                    writes=[grg(O_DLS + ec * 512, O_DLS + (ec + 1) * 512)],
                    tag=f"sp5{i}_{n}_{ec}")
            S.add("vector", lambda e: e.tensor_tensor(
                gap(O_DUS, O_DUS + 8192), gap(O_DLS, O_DLS + 8192),
                gap(O_XCS, O_XCS + 8192), ALU.mult),
                reads=[grg(O_DLS, O_DLS + 8192), grg(O_XCS, O_XCS + 8192)],
                writes=[grg(O_DUS, O_DUS + 8192)], tag=f"du{i}_{n}")
            S.add("scalar", lambda e: e.activation(gap(O_ZST, O_ZST + 8192),
                                                   gap(O_ZST, O_ZST + 8192), AF.Silu),
                  reads=[grg(O_ZST, O_ZST + 8192)], writes=[grg(O_ZST, O_ZST + 8192)],
                  tag=f"zs{i}_{n}")
            for nst in range(NST):
                S.add("tensor", lambda e, nst=nst: e.matmul(
                    C.PS[:, 2 * 512:3 * 512], sel16[:, nst * 128:(nst + 1) * 128],
                    bcur[:, :], start=True, stop=True),
                    reads=[("l3a_sel", None), ("l3a_bcu", None)],
                    writes=[("PS", (2 * 512, 3 * 512))], tag=f"bb{i}_{n}_{nst}")
                S.add("vector", lambda e: e.tensor_copy(gap(O_BCB, O_BCB + 512),
                                                        C.PS[:, 2 * 512:3 * 512]),
                      reads=[("PS", (2 * 512, 3 * 512))], writes=[grg(O_BCB, O_BCB + 512)],
                      tag=f"bbc{i}_{n}_{nst}")
                S.add("tensor", lambda e, nst=nst: e.matmul(
                    C.PS[:, 2 * 512:3 * 512], sel16[:, nst * 128:(nst + 1) * 128],
                    ccur[:, :], start=True, stop=True),
                    reads=[("l3a_sel", None), ("l3a_ccu", None)],
                    writes=[("PS", (2 * 512, 3 * 512))], tag=f"cb{i}_{n}_{nst}")
                S.add("vector", lambda e: e.tensor_copy(gap(O_BCB + 512, O_BCB + 1024),
                                                        C.PS[:, 2 * 512:3 * 512]),
                      reads=[("PS", (2 * 512, 3 * 512))],
                      writes=[grg(O_BCB + 512, O_BCB + 1024)], tag=f"cbc{i}_{n}_{nst}")
                for ec in range(EC):
                    cidx = ec * 16 + nst
                    dA = gap(O_SEG, O_SEG + 512)
                    dB = gap(O_SEG + 512, O_SEG + 1024)
                    hh = gap(O_SEG + 1024, O_SEG + 1536)
                    hC = gap(O_SEG + 1536, O_SEG + 2048)
                    S.add("scalar", lambda e, ec=ec, cidx=cidx: e.activation(
                        dA, gap(O_DLS + ec * 512, O_DLS + (ec + 1) * 512), AF.Exp,
                        scale=acol[:, cidx:cidx + 1]),
                        reads=[grg(O_DLS + ec * 512, O_DLS + (ec + 1) * 512),
                               ("l3a_ac", (cidx, cidx + 1))],
                        writes=[grg(O_SEG, O_SEG + 512)], tag=f"dA{i}_{n}_{nst}_{ec}")
                    S.add("vector", lambda e, ec=ec: e.tensor_tensor(
                        dB, gap(O_DUS + ec * 512, O_DUS + (ec + 1) * 512),
                        gap(O_BCB, O_BCB + 512), ALU.mult),
                        reads=[grg(O_DUS + ec * 512, O_DUS + (ec + 1) * 512),
                               grg(O_BCB, O_BCB + 512)],
                        writes=[grg(O_SEG + 512, O_SEG + 1024)], tag=f"dB{i}_{n}_{nst}_{ec}")
                    S.add("vector", lambda e, cidx=cidx: e.tensor_tensor_scan(
                        hh, dA, dB, hcar[:, cidx:cidx + 1], ALU.mult, ALU.add),
                        reads=[grg(O_SEG, O_SEG + 1024), ("l3a_hc", (cidx, cidx + 1))],
                        writes=[grg(O_SEG + 1024, O_SEG + 1536)],
                        tag=f"sc{i}_{n}_{nst}_{ec}")
                    S.add("vector", lambda e, cidx=cidx: e.tensor_copy(
                        hcar[:, cidx:cidx + 1], hh[:, 511:512]),
                        reads=[grg(O_SEG + 1024, O_SEG + 1536)],
                        writes=[("l3a_hc", (cidx, cidx + 1))], tag=f"hc{i}_{n}_{nst}_{ec}")
                    S.add("vector", lambda e: e.tensor_tensor(
                        hC, hh, gap(O_BCB + 512, O_BCB + 1024), ALU.mult),
                        reads=[grg(O_SEG + 1024, O_SEG + 1536), grg(O_BCB + 512, O_BCB + 1024)],
                        writes=[grg(O_SEG + 1536, O_SEG + 2048)], tag=f"hC{i}_{n}_{nst}_{ec}")
                    if nst == 0:
                        S.add("vector", lambda e, ec=ec: e.tensor_copy(
                            gap(O_YST + ec * 512, O_YST + (ec + 1) * 512), hC),
                            reads=[grg(O_SEG + 1536, O_SEG + 2048)],
                            writes=[grg(O_YST + ec * 512, O_YST + (ec + 1) * 512)],
                            tag=f"y0{i}_{n}_{nst}_{ec}")
                    else:
                        S.add("vector", lambda e, ec=ec: e.tensor_tensor(
                            gap(O_YST + ec * 512, O_YST + (ec + 1) * 512),
                            gap(O_YST + ec * 512, O_YST + (ec + 1) * 512), hC, ALU.add),
                            reads=[grg(O_YST + ec * 512, O_YST + (ec + 1) * 512),
                                   grg(O_SEG + 1536, O_SEG + 2048)],
                            writes=[grg(O_YST + ec * 512, O_YST + (ec + 1) * 512)],
                            tag=f"ya{i}_{n}_{nst}_{ec}")
            for ec in range(EC):
                S.add("vector", lambda e, ec=ec: e.scalar_tensor_tensor(
                    gap(O_YST + ec * 512, O_YST + (ec + 1) * 512),
                    gap(O_XCS + ec * 512, O_XCS + (ec + 1) * 512),
                    sm[:, 96 + ec:97 + ec],
                    gap(O_YST + ec * 512, O_YST + (ec + 1) * 512), ALU.mult, ALU.add),
                    reads=[grg(O_XCS + ec * 512, O_XCS + (ec + 1) * 512),
                           ("l3a_sm", (96, 112)),
                           grg(O_YST + ec * 512, O_YST + (ec + 1) * 512)],
                    writes=[grg(O_YST + ec * 512, O_YST + (ec + 1) * 512)],
                    tag=f"y2{i}_{n}_{ec}")
                S.add("vector", lambda e, ec=ec: e.tensor_tensor(
                    gap(O_YST + ec * 512, O_YST + (ec + 1) * 512),
                    gap(O_YST + ec * 512, O_YST + (ec + 1) * 512),
                    gap(O_ZST + ec * 512, O_ZST + (ec + 1) * 512), ALU.mult),
                    reads=[grg(O_YST + ec * 512, O_YST + (ec + 1) * 512),
                           grg(O_ZST + ec * 512, O_ZST + (ec + 1) * 512)],
                    writes=[grg(O_YST + ec * 512, O_YST + (ec + 1) * 512)],
                    tag=f"yg{i}_{n}_{ec}")
            for g in range(4):
                S.add("sync", lambda e, g=g, i=i: e.dma_start(
                    out=gap(O_WO2, O_WO2 + 4096).rearrange("p (c f) -> p c f", c=4),
                    in_=ow_scr[i][g * 512:(g + 1) * 512, :].rearrange("(c p) f -> p c f", p=128)),
                    reads=[(f"l3a_owscr{i}", (0, 16))],
                    writes=[grg(O_WO2, O_WO2 + 4096)], tag=f"ldow{i}_{n}_{g}")
                for mo in range(DC):
                    def mmo(e, g=g, mo=mo):
                        ins = None
                        for c in range(4):
                            ins = e.matmul(bank(C, mo),
                                           gap(O_WO2 + c * 1024 + mo * 128,
                                               O_WO2 + c * 1024 + (mo + 1) * 128),
                                           gap(O_YST + (g * 4 + c) * 512,
                                               O_YST + (g * 4 + c + 1) * 512),
                                           start=(g == 0 and c == 0),
                                           stop=(g == 3 and c == 3))
                        return ins
                    S.add("tensor", mmo,
                          reads=[grg(O_WO2, O_WO2 + 4096), grg(O_YST, O_YST + 8192)],
                          writes=[("PS", (mo * 512, (mo + 1) * 512))],
                          tag=f"op{i}_{n}_{g}_{mo}")
            for mo in range(DC):
                S.add("vector", lambda e, mo=mo: e.tensor_copy(
                    rb[:, mo * 512:(mo + 1) * 512], bank(C, mo)),
                    reads=[("PS", (mo * 512, (mo + 1) * 512))],
                    writes=[(K_RB, (mo * 512, (mo + 1) * 512))], tag=f"oe{i}_{n}_{mo}")
            def src2(k):
                return (rb[:, k * 512:(k + 1) * 512], [(K_RB, (k * 512, (k + 1) * 512))])
            def hout(k):
                return (gap(O_HOS + k * 512, O_HOS + (k + 1) * 512),
                        grg(O_HOS + k * 512, O_HOS + (k + 1) * 512))
            emit_ln_core(C, src2, hout, gamma=lnw, beta=lnbv,
                         gb_reads=[("l3a_sm", (120, 136))], tag=f"o{i}n{n}_")
            S.add("sync", lambda e, n=n, i=i: e.dma_start(
                out=h_scr[i][:, n * TTW:(n + 1) * TTW].rearrange("(c p) f -> p c f", p=128),
                in_=gap(O_HOS, O_HOS + 4096).rearrange("p (c f) -> p c f", c=8)),
                reads=[grg(O_HOS, O_HOS + 4096)], writes=[(f"l3a_hscr{i}", (0, 8))],
                tag=f"sth{i}_{n}")

    S.add("sync", lambda e: e.dma_start(out=sm[:, 136:200], in_=FB1),
          writes=[("l3a_sm", (136, 200))], tag="ldfb1")
    S.add("sync", lambda e: e.dma_start(out=sm[:, 200:208], in_=FB2),
          writes=[("l3a_sm", (200, 208))], tag="ldfb2")
    for g in range(2):
        S.add("sync", lambda e, g=g: e.dma_start(
            out=gap(O_H1R + g * 16384, O_H1R + (g + 1) * 16384).rearrange("p (c f) -> p c f", c=4),
            in_=h_scr[1][g * 512:(g + 1) * 512, :].rearrange("(c p) f -> p c f", p=128)),
            reads=[("l3a_hscr1", (0, 8))],
            writes=[grg(O_H1R + g * 16384, O_H1R + (g + 1) * 16384)], tag=f"ldh1_{g}")
    for fc in range(64):
        S.add("gpsimd", lambda e, fc=fc: e.dma_start(
            out=gap(O_FWC, O_FWC + 1024).rearrange("p (c f) -> p c f", c=8),
            in_=FW1[:, fc * 128:(fc + 1) * 128].rearrange("(c p) f -> p c f", p=128)),
            writes=[grg(O_FWC, O_FWC + 1024)], weight=16, tag=f"ldfw1_{fc}")
        for n in range(NT):
            bk = n % 2
            def mmf(e, fc=fc, n=n, bk=bk):
                ins = None
                for k in range(DC):
                    ins = e.matmul(bank(C, bk),
                                   gap(O_FWC + k * 128, O_FWC + (k + 1) * 128),
                                   gap(O_H1R + k * 4096 + n * 512, O_H1R + k * 4096 + (n + 1) * 512),
                                   start=(k == 0), stop=(k == DC - 1))
                return ins
            S.add("tensor", mmf,
                  reads=[grg(O_FWC, O_FWC + 1024), grg(O_H1R, O_H1R + 32768)],
                  writes=[("PS", (bk * 512, (bk + 1) * 512))], tag=f"ff1_{fc}_{n}")
            if fc < 32:
                S.add("vector", lambda e, fc=fc, n=n, bk=bk: e.tensor_scalar_add(
                    gap(O_AST + n * 512, O_AST + (n + 1) * 512), bank(C, bk),
                    sm[:, 136 + fc:137 + fc]),
                    reads=[("PS", (bk * 512, (bk + 1) * 512)), ("l3a_sm", (136, 200))],
                    writes=[grg(O_AST + n * 512, O_AST + (n + 1) * 512)],
                    tag=f"ae{fc}_{n}")
            else:
                S.add("scalar", lambda e, fc=fc, n=n, bk=bk: e.activation(
                    gap(O_GST + n * 512, O_GST + (n + 1) * 512), bank(C, bk), AF.Gelu,
                    bias=sm[:, 136 + fc:137 + fc]),
                    reads=[("PS", (bk * 512, (bk + 1) * 512)), ("l3a_sm", (136, 200))],
                    writes=[grg(O_GST + n * 512, O_GST + (n + 1) * 512)],
                    tag=f"ge{fc}_{n}")
        if fc < 32:
            S.add("sync", lambda e, fc=fc: e.dma_start(
                out=au_scr[fc * 128:(fc + 1) * 128, :], in_=gap(O_AST, O_AST + 4096)),
                reads=[grg(O_AST, O_AST + 4096)], writes=[("l3a_auscr", (fc, fc + 1))],
                tag=f"stau{fc}")
        else:
            S.add("sync", lambda e, fc=fc: e.dma_start(
                out=gap(O_ALD, O_ALD + 4096), in_=au_scr[(fc - 32) * 128:(fc - 31) * 128, :]),
                reads=[("l3a_auscr", (fc - 32, fc - 31))],
                writes=[grg(O_ALD, O_ALD + 4096)], tag=f"ldau{fc}")
            S.add("vector", lambda e: e.tensor_tensor(
                gap(O_VST, O_VST + 4096), gap(O_ALD, O_ALD + 4096),
                gap(O_GST, O_GST + 4096), ALU.mult),
                reads=[grg(O_ALD, O_ALD + 4096), grg(O_GST, O_GST + 4096)],
                writes=[grg(O_VST, O_VST + 4096)], tag=f"vg{fc}")
            S.add("sync", lambda e, fc=fc: e.dma_start(
                out=v_scr[(fc - 32) * 128:(fc - 31) * 128, :], in_=gap(O_VST, O_VST + 4096)),
                reads=[grg(O_VST, O_VST + 4096)], writes=[("l3a_vscr", (fc - 32, fc - 31))],
                tag=f"stv{fc}")
    for n in range(NT):
        tsl = (PAD + n * TTW, PAD + (n + 1) * TTW)
        for g in range(4):
            S.add("sync", lambda e, g=g: e.dma_start(
                out=gap(O_FW2, O_FW2 + 8192).rearrange("p (c f) -> p c f", c=8),
                in_=fw2_scr[g * 1024:(g + 1) * 1024, :].rearrange("(c p) f -> p c f", p=128)),
                reads=[("l3a_fw2scr", (0, 32))],
                writes=[grg(O_FW2, O_FW2 + 8192)], tag=f"ldfw2_{n}_{g}")
            S.add("sync", lambda e, g=g, n=n: e.dma_start(
                out=gap(O_VGR, O_VGR + 4096).rearrange("p (c f) -> p c f", c=8),
                in_=v_scr[g * 1024:(g + 1) * 1024, n * 512:(n + 1) * 512].rearrange(
                    "(c p) f -> p c f", p=128)),
                reads=[("l3a_vscr", (g * 8, (g + 1) * 8))],
                writes=[grg(O_VGR, O_VGR + 4096)], tag=f"ldv_{n}_{g}")
            for mo in range(DC):
                def mmo2(e, g=g, mo=mo):
                    ins = None
                    for c in range(8):
                        ins = e.matmul(bank(C, mo),
                                       gap(O_FW2 + c * 1024 + mo * 128,
                                           O_FW2 + c * 1024 + (mo + 1) * 128),
                                       gap(O_VGR + c * 512, O_VGR + (c + 1) * 512),
                                       start=(g == 0 and c == 0), stop=(g == 3 and c == 7))
                    return ins
                S.add("tensor", mmo2,
                      reads=[grg(O_FW2, O_FW2 + 8192), grg(O_VGR, O_VGR + 4096)],
                      writes=[("PS", (mo * 512, (mo + 1) * 512))], tag=f"ff2_{n}_{g}_{mo}")
        for mo in range(DC):
            S.add("vector", lambda e, mo=mo, n=n: e.scalar_tensor_tensor(
                rb[:, mo * 512:(mo + 1) * 512], bank(C, mo), sm[:, 200 + mo:201 + mo],
                C.xres[mo][:, PAD + n * TTW:PAD + (n + 1) * TTW], ALU.add, ALU.add),
                reads=[("PS", (mo * 512, (mo + 1) * 512)), ("l3a_sm", (200, 208)),
                       (f"xres{mo}", tsl)],
                writes=[(K_RB, (mo * 512, (mo + 1) * 512))], tag=f"ffr{n}_{mo}")
        emit_postln_tile(C, n, rb, K_RB)





L0F_LIMIT = None


def emit_l0ffn(C):
    S = C.S
    H = 8

    ga = C.sbuf("l0ga", [128, 30720], BF16)
    K_GA = "l0ga"

    def gap(a, b):
        return ga[:, a:b]

    def grg(a, b):
        return (K_GA, (a, b))

    Q_HST = 0
    Q_ROT = 4104
    Q_STG = 6152
    Q_TMP = 12296
    Q_WCL = 14344
    Q_OUT = 15368
    A_XI = 0
    A_QW = 4104
    A_KW = 6152
    A_VW = 10248
    A_VT = 14344
    A_PN = 14856
    A_PTB = 15880
    A_OFM = 16904
    A_NST = 18952
    A_AST = 23048
    A_VOG = 23560
    A_VIG = 25608
    A_W2C = 27656
    A_WCL = 28680
    A_WOS = 23048

    msks = C.sbuf("l0f_msk", [128, 8 * 512], BF16)
    sml = C.sbuf("l0f_sm", [128, 72], F32)
    half2 = C.sbuf("l0f_h2", [128, 2], BF16)
    bc2 = C.sbuf("l0f_bc2", [2, 128], BF16)
    rr = C.sbuf("l0f_rr", [2, 512], BF16)
    rrf = C.sbuf("l0f_rrf", [2, 512], F32)
    mst = C.sbuf("l0f_mst", [128, 8], F32)
    xc2 = C.sbuf("l0f_xc2", [128, 4], BF16)
    rb = C.sbuf("l0f_rb", [128, 4096], F32)
    K_RB = "l0f_rb"

    MSK = C.din("c_l0f_masks", [128, 8 * 512], BF16)
    H2 = C.din("c_half2", [128, 2], BF16)
    BC2 = C.din("c_bc2", [2, 128], BF16)
    CSQ = C.din("c_l0f_csq", [128, T], BF16)
    SSQ = C.din("c_l0f_ssq", [128, T], BF16)
    CSK = C.din("c_l0f_csk", [128, T], BF16)
    SSK = C.din("c_l0f_ssk", [128, T], BF16)
    QS = [C.din(f"L0f_qs{l}", [128, 1], F32) for l in range(2)]
    KS = [C.din(f"L0f_ks{l}", [128, 1], F32) for l in range(2)]
    LG = [C.din(f"L0f_lg{l}", [128, 8], F32) for l in range(2)]
    LB = [C.din(f"L0f_lb{l}", [128, 8], F32) for l in range(2)]
    B1A = [C.din(f"L0f_b1a{l}", [128, 22], F32) for l in range(2)]
    B1G = [C.din(f"L0f_b1g{l}", [128, 22], F32) for l in range(2)]
    B2C = [C.din(f"L0f_b2{l}", [128, 8], F32) for l in range(2)]

    S.add("sync", lambda e: e.dma_start(out=msks[:, :], in_=MSK),
          writes=[("l0f_msk", None)], tag="ldmsk0")
    S.add("sync", lambda e: e.dma_start(out=half2[:, :], in_=H2),
          writes=[("l0f_h2", None)], tag="ldh2")
    S.add("sync", lambda e: e.dma_start(out=bc2[:, :], in_=BC2),
          writes=[("l0f_bc2", None)], tag="ldbc2")

    xint = C.dscratch("l0f_xint", [1024, 1 + T], BF16)
    q_scr = C.dscratch("l0f_q", [512, T], BF16)
    k_scr = C.dscratch("l0f_k", [512, T], BF16)
    v_scr = C.dscratch("l0f_v", [512, T], BF16)
    vf_scr = C.dscratch("l0f_vf", [2730, T], BF16)
    wqkv_scr = [C.dscratch(f"l0f_wqkv{l}", [1024, 1536], BF16) for l in range(2)]
    wo_scr = [C.dscratch(f"l0f_wo{l}", [512, 1024], BF16) for l in range(2)]
    w1a_scr = [C.dscratch(f"l0f_w1a{l}", [1024, 2730], BF16) for l in range(2)]
    w1g_scr = [C.dscratch(f"l0f_w1g{l}", [1024, 2730], BF16) for l in range(2)]
    w2_scr = [C.dscratch(f"l0f_w2{l}", [2730, 1024], BF16) for l in range(2)]

    def precast(src, scr, skey, rows, F, tagp):
        nch = (rows + 127) // 128
        for c in range(nch):
            r0 = c * 128
            r1 = min(rows, r0 + 128)
            S.add("gpsimd", lambda e, r0=r0, r1=r1: e.dma_start(
                out=gap(0, F)[0:r1 - r0, :], in_=src[r0:r1, :]),
                writes=[grg(0, F)], weight=16, tag=f"{tagp}c{c}")
            S.add("sync", lambda e, r0=r0, r1=r1: e.dma_start(
                out=scr[r0:r1, :], in_=gap(0, F)[0:r1 - r0, :]),
                reads=[grg(0, F)], writes=[(skey, (c, c + 1))], tag=f"{tagp}s{c}")

    for l in range(2):
        WQKV = C.din(f"L0f_wqkv{l}", [1024, 1536], F32)
        WO = C.din(f"L0f_wo{l}", [512, 1024], F32)
        W1A = C.din(f"L0f_w1a{l}", [1024, 2730], F32)
        W1G = C.din(f"L0f_w1g{l}", [1024, 2730], F32)
        W2 = C.din(f"L0f_w2{l}", [2730, 1024], F32)
        precast(WQKV, wqkv_scr[l], f"l0f_wqkvscr{l}", 1024, 1536, f"pq{l}")
        precast(WO, wo_scr[l], f"l0f_woscr{l}", 512, 1024, f"pw{l}")
        precast(W1A, w1a_scr[l], f"l0f_w1ascr{l}", 1024, 2730, f"pa{l}")
        precast(W1G, w1g_scr[l], f"l0f_w1gscr{l}", 1024, 2730, f"pg{l}")
        precast(W2, w2_scr[l], f"l0f_w2scr{l}", 2730, 1024, f"p2{l}")

    S.add("vector", lambda e: e.memset(xc2[:, 0:1], 0.0), writes=[("l0f_xc2", (0, 1))],
          tag="z0")
    S.add("sync", lambda e: e.dma_start(
        out=xint[:, 0:1].rearrange("(c p) f -> p c f", p=128),
        in_=xc2[:, 0:1].to_broadcast((128, 8, 1)) if False else xc2[:, 0:1]),
        reads=[("l0f_xc2", (0, 1))], writes=[("l0f_xintc", (0, 1))], tag="z1") if False else None
    for k in range(DC):
        S.add("sync", lambda e, k=k: e.dma_start(out=xint[k * 128:(k + 1) * 128, 0:1],
                                                 in_=xc2[:, 0:1],
                                                 allow_slow_non_contiguous=True),
              reads=[("l0f_xc2", (0, 1))], writes=[("l0f_xintz", (k, k + 1))], tag=f"z{k}")
    for k in range(DC):
        S.add("sync", lambda e, k=k: e.dma_start(out=xint[k * 128:(k + 1) * 128, 1:1 + T],
                                                 in_=C.xres[k][:, PAD:PAD + T]),
              reads=[(f"xres{k}", (PAD, PAD + T))], writes=[("l0f_xints", (k, k + 1))],
              tag=f"xi{k}")

    for l in range(1 if L0F_LIMIT in ("q0", "a0") else 2):
        S.add("vector", lambda e: e.memset(xc2[:, 0:4], 0.0),
              writes=[("l0f_xc2", (0, 4))], tag=f"xc2z{l}")
        S.add("sync", lambda e, l=l: e.dma_start(out=sml[:, 0:1], in_=QS[l]),
              writes=[("l0f_sm", (0, 1))], tag=f"ldqs{l}")
        S.add("sync", lambda e, l=l: e.dma_start(out=sml[:, 1:2], in_=KS[l]),
              writes=[("l0f_sm", (1, 2))], tag=f"ldks{l}")
        S.add("sync", lambda e, l=l: e.dma_start(out=sml[:, 2:10], in_=LG[l]),
              writes=[("l0f_sm", (2, 10))], tag=f"ldlg{l}")
        S.add("sync", lambda e, l=l: e.dma_start(out=sml[:, 10:18], in_=LB[l]),
              writes=[("l0f_sm", (10, 18))], tag=f"ldlb{l}")
        S.add("sync", lambda e, l=l: e.dma_start(out=sml[:, 18:40], in_=B1A[l]),
              writes=[("l0f_sm", (18, 40))], tag=f"ldb1a{l}")
        S.add("sync", lambda e, l=l: e.dma_start(out=sml[:, 40:62], in_=B1G[l]),
              writes=[("l0f_sm", (40, 62))], tag=f"ldb1g{l}")
        S.add("sync", lambda e, l=l: e.dma_start(out=sml[:, 62:70], in_=B2C[l]),
              writes=[("l0f_sm", (62, 70))], tag=f"ldb2{l}")

        for n in range(NT):
            S.add("sync", lambda e, n=n: e.dma_start(
                out=gap(Q_HST, Q_HST + 2048).rearrange("p (c f) -> p c f", c=4),
                in_=xint[0:512, 1 + n * 512:1 + (n + 1) * 512].rearrange("(c p) f -> p c f", p=128)),
                reads=[("l0f_xints", (0, 4)), ("l0f_xintz", (0, 4))],
                writes=[grg(Q_HST, Q_HST + 2048)], tag=f"hs{l}_{n}a")
            S.add("sync", lambda e, n=n: e.dma_start(
                out=gap(Q_HST + 2048, Q_HST + 4096).rearrange("p (c f) -> p c f", c=4),
                in_=xint[512:1024, n * 512:(n + 1) * 512].rearrange("(c p) f -> p c f", p=128)),
                reads=[("l0f_xints", (4, 8)), ("l0f_xintz", (4, 8))],
                writes=[grg(Q_HST + 2048, Q_HST + 4096)], tag=f"hs{l}_{n}b")
            for (tb, off) in ((CSQ, 0), (SSQ, 512), (CSK, 1024), (SSK, 1536)):
                S.add("sync", lambda e, tb=tb, off=off, n=n: e.dma_start(
                    out=gap(Q_ROT + off, Q_ROT + off + 512),
                    in_=tb[:, n * 512:(n + 1) * 512]),
                    writes=[grg(Q_ROT + off, Q_ROT + off + 512)], tag=f"rt{l}_{n}_{off}")
            for fc in range(12):
                S.add("sync", lambda e, fc=fc, l=l: e.dma_start(
                    out=gap(Q_WCL, Q_WCL + 1024).rearrange("p (c f) -> p c f", c=8),
                    in_=wqkv_scr[l][:, fc * 128:(fc + 1) * 128].rearrange("(c p) f -> p c f", p=128)),
                    reads=[(f"l0f_wqkvscr{l}", (0, 8))],
                    writes=[grg(Q_WCL, Q_WCL + 1024)], tag=f"qw{l}_{n}_{fc}")
                bk = fc % 2
                def mmq(e, bk=bk):
                    ins = None
                    for k in range(DC):
                        ins = e.matmul(bank(C, bk), gap(Q_WCL + k * 128, Q_WCL + (k + 1) * 128),
                                       gap(Q_HST + k * 512, Q_HST + (k + 1) * 512),
                                       start=(k == 0), stop=(k == DC - 1))
                    return ins
                S.add("tensor", mmq,
                      reads=[grg(Q_WCL, Q_WCL + 1024), grg(Q_HST, Q_HST + 4096)],
                      writes=[("PS", (bk * 512, (bk + 1) * 512))], tag=f"mq{l}_{n}_{fc}")
                st = Q_STG + (fc % 4) * 512
                S.add("vector", lambda e, st=st, bk=bk: e.tensor_copy(
                    gap(st, st + 512), bank(C, bk)),
                    reads=[("PS", (bk * 512, (bk + 1) * 512))],
                    writes=[grg(st, st + 512)], tag=f"qe{l}_{n}_{fc}")
                if fc >= 8:
                    S.add("sync", lambda e, fc=fc, n=n, st=st: e.dma_start(
                        out=v_scr[(fc - 8) * 128:(fc - 7) * 128, n * 512:(n + 1) * 512],
                        in_=gap(st, st + 512)),
                        reads=[grg(st, st + 512)], writes=[(f"l0f_vscr", ((fc - 8) * 8 + n, (fc - 8) * 8 + n + 1))],
                        tag=f"vs{l}_{n}_{fc}")
                    continue
                isq = fc < 4
                scol = sml[:, 0:1] if isq else sml[:, 1:2]
                srd = ("l0f_sm", (0, 1)) if isq else ("l0f_sm", (1, 2))
                cso = Q_ROT + (0 if isq else 1024)
                sso = Q_ROT + (512 if isq else 1536)
                S.add("scalar", lambda e, st=st: e.activation(gap(Q_TMP, Q_TMP + 512),
                                                              gap(st, st + 512), AF.Square),
                      reads=[grg(st, st + 512)], writes=[grg(Q_TMP, Q_TMP + 512)],
                      tag=f"sq{l}_{n}_{fc}")
                S.add("tensor", lambda e: e.matmul(C.PS[0:2, 2 * 512:3 * 512], half2[:, :],
                                                   gap(Q_TMP, Q_TMP + 512), start=True, stop=True),
                      reads=[grg(Q_TMP, Q_TMP + 512), ("l0f_h2", None)],
                      writes=[("PS", (2 * 512, 3 * 512))], tag=f"ss{l}_{n}_{fc}")
                S.add("scalar", lambda e: e.activation(rrf[:, :], C.PS[0:2, 2 * 512:3 * 512],
                                                       AF.Sqrt),
                      reads=[("PS", (2 * 512, 3 * 512))], writes=[("l0f_rrf", None)],
                      tag=f"sr{l}_{n}_{fc}")
                S.add("vector", lambda e: e.tensor_scalar_max(rrf[:, :], rrf[:, :], 1e-12),
                      reads=[("l0f_rrf", None)], writes=[("l0f_rrf", None)],
                      tag=f"sm{l}_{n}_{fc}")
                S.add("vector", lambda e: e.reciprocal(rrf[:, :], rrf[:, :]),
                      reads=[("l0f_rrf", None)], writes=[("l0f_rrf", None)],
                      tag=f"sc{l}_{n}_{fc}")
                S.add("vector", lambda e: e.tensor_copy(rr[:, :], rrf[:, :]),
                      reads=[("l0f_rrf", None)], writes=[("l0f_rr", None)],
                      tag=f"sc2{l}_{n}_{fc}")
                S.add("tensor", lambda e: e.matmul(C.PS[:, 2 * 512:3 * 512], bc2[:, :],
                                                   rr[:, :], start=True, stop=True),
                      reads=[("l0f_rr", None), ("l0f_bc2", None)],
                      writes=[("PS", (2 * 512, 3 * 512))], tag=f"sb{l}_{n}_{fc}")
                S.add("vector", lambda e, st=st, scol=scol, srd=srd: e.scalar_tensor_tensor(
                    gap(Q_TMP, Q_TMP + 512), gap(st, st + 512), scol,
                    C.PS[:, 2 * 512:3 * 512], ALU.mult, ALU.mult),
                    reads=[grg(st, st + 512), srd, ("PS", (2 * 512, 3 * 512))],
                    writes=[grg(Q_TMP, Q_TMP + 512)], tag=f"qn{l}_{n}_{fc}")
                for (o1, o2, sgn) in ((0, 32, -1.0), (32, 0, 1.0), (64, 96, -1.0), (96, 64, 1.0)):
                    S.add("vector", lambda e, o1=o1, o2=o2, sgn=sgn: e.tensor_scalar_mul(
                        gap(Q_TMP + 512, Q_TMP + 1024)[o1:o1 + 32, :],
                        gap(Q_TMP, Q_TMP + 512)[o2:o2 + 32, :], sgn),
                        reads=[grg(Q_TMP, Q_TMP + 512)],
                        writes=[grg(Q_TMP + 512, Q_TMP + 1024)], tag=f"rh{l}_{n}_{fc}_{o1}")
                S.add("vector", lambda e, cso=cso: e.tensor_tensor(
                    gap(Q_TMP, Q_TMP + 512), gap(Q_TMP, Q_TMP + 512),
                    gap(cso, cso + 512), ALU.mult),
                    reads=[grg(Q_TMP, Q_TMP + 512), grg(cso, cso + 512)],
                    writes=[grg(Q_TMP, Q_TMP + 512)], tag=f"rc{l}_{n}_{fc}")
                S.add("vector", lambda e, sso=sso: e.tensor_tensor(
                    gap(Q_TMP + 512, Q_TMP + 1024), gap(Q_TMP + 512, Q_TMP + 1024),
                    gap(sso, sso + 512), ALU.mult),
                    reads=[grg(Q_TMP + 512, Q_TMP + 1024), grg(sso, sso + 512)],
                    writes=[grg(Q_TMP + 512, Q_TMP + 1024)], tag=f"rs{l}_{n}_{fc}")
                oslot = Q_OUT + (fc % 4) * 512
                S.add("vector", lambda e, oslot=oslot: e.tensor_tensor(
                    gap(oslot, oslot + 512), gap(Q_TMP, Q_TMP + 512),
                    gap(Q_TMP + 512, Q_TMP + 1024), ALU.add),
                    reads=[grg(Q_TMP, Q_TMP + 1024)],
                    writes=[grg(oslot, oslot + 512)], tag=f"ro{l}_{n}_{fc}")
                scr = q_scr if isq else k_scr
                skey = "l0f_qscr" if isq else "l0f_kscr"
                S.add("sync", lambda e, fc=fc, n=n, oslot=oslot, scr=scr: e.dma_start(
                    out=scr[(fc % 4) * 128:(fc % 4 + 1) * 128, n * 512:(n + 1) * 512],
                    in_=gap(oslot, oslot + 512)),
                    reads=[grg(oslot, oslot + 512)],
                    writes=[(skey, ((fc % 4) * 8 + n, (fc % 4) * 8 + n + 1))],
                    tag=f"qs{l}_{n}_{fc}")

        if L0F_LIMIT == "q0" and l == 0:
            break
        for n in range(NT):
            w = n
            nkb = 2 if w > 0 else 1
            kv0 = (w - 1) * 512 if w > 0 else 0
            kvw = nkb * 512
            S.add("sync", lambda e, n=n: e.dma_start(
                out=gap(A_XI, A_XI + 4096).rearrange("p (c f) -> p c f", c=8),
                in_=xint[:, 1 + n * 512:1 + (n + 1) * 512].rearrange("(c p) f -> p c f", p=128)),
                reads=[("l0f_xints", (0, 8)), ("l0f_xintw", (0, 100))],
                writes=[grg(A_XI, A_XI + 4096)], tag=f"xl{l}_{n}")
            S.add("sync", lambda e, n=n: e.dma_start(
                out=gap(A_QW, A_QW + 2048).rearrange("p (c f) -> p c f", c=4),
                in_=q_scr[:, n * 512:(n + 1) * 512].rearrange("(c p) f -> p c f", p=128)),
                reads=[("l0f_qscr", None)], writes=[grg(A_QW, A_QW + 2048)], tag=f"ql{l}_{n}")
            S.add("sync", lambda e, kv0=kv0, kvw=kvw: e.dma_start(
                out=gap(A_KW, A_KW + 4 * kvw).rearrange("p (c f) -> p c f", c=4),
                in_=k_scr[:, kv0:kv0 + kvw].rearrange("(c p) f -> p c f", p=128)),
                reads=[("l0f_kscr", None)], writes=[grg(A_KW, A_KW + 4096)], tag=f"kl{l}_{n}")
            S.add("sync", lambda e, kv0=kv0, kvw=kvw: e.dma_start(
                out=gap(A_VW, A_VW + 4 * kvw).rearrange("p (c f) -> p c f", c=4),
                in_=v_scr[:, kv0:kv0 + kvw].rearrange("(c p) f -> p c f", p=128)),
                reads=[("l0f_vscr", None)], writes=[grg(A_VW, A_VW + 4096)], tag=f"vl{l}_{n}")
            if C.debug and l == 0 and n == 0:
                C.tap("qw0", gap(A_QW, A_QW + 2048), [grg(A_QW, A_QW + 2048)])
                C.tap("kw0", gap(A_KW, A_KW + 4096), [grg(A_KW, A_KW + 4096)])
                C.tap("vw0", gap(A_VW, A_VW + 4096), [grg(A_VW, A_VW + 4096)])
            psb = C.PS[:, 5 * 512:6 * 512].bitcast(BF16)
            for h in range(H):
                hc, ho = h // 2, (h % 2) * 64
                nsc = kvw // 128
                for tg in range((nsc + 3) // 4):
                    ns = min(4, nsc - tg * 4)
                    def vtr(e, tg=tg, ns=ns, hc=hc, ho=ho, kvw=kvw):
                        ins = None
                        for j in range(ns):
                            sc = tg * 4 + j
                            ins = e.transpose(psb[0:128, j * 64:(j + 1) * 64],
                                              gap(A_VW + hc * kvw + sc * 128,
                                                  A_VW + hc * kvw + (sc + 1) * 128)[ho:ho + 64, :],
                                              C.ident_bf[ho:ho + 64, ho:ho + 64])
                        return ins
                    S.add("tensor", vtr, reads=[grg(A_VW, A_VW + 4096), ("CONSTB", (128, 256))],
                          writes=[("PS", (5 * 512, 6 * 512))], tag=f"vt{l}_{n}_{h}_{tg}")
                    def vcp(e, tg=tg, ns=ns):
                        ins = None
                        for j in range(ns):
                            sc = tg * 4 + j
                            ins = e.tensor_copy(gap(A_VT + sc * 64, A_VT + (sc + 1) * 64),
                                                psb[0:128, j * 64:(j + 1) * 64])
                        return ins
                    S.add("vector", vcp, reads=[("PS", (5 * 512, 6 * 512))],
                          writes=[grg(A_VT + tg * 256, A_VT + (tg * 4 + ns) * 64)],
                          tag=f"vc{l}_{n}_{h}_{tg}")
                for qs in range(4):
                    def mms(e, qs=qs, hc=hc, ho=ho, nkb=nkb, kvw=kvw):
                        ins = None
                        for kb in range(nkb):
                            ins = e.matmul(bank(C, kb),
                                           gap(A_QW + hc * 512 + qs * 128,
                                               A_QW + hc * 512 + (qs + 1) * 128)[ho:ho + 64, :],
                                           gap(A_KW + hc * kvw + kb * 512,
                                               A_KW + hc * kvw + (kb + 1) * 512)[ho:ho + 64, :],
                                           start=True, stop=True)
                        return ins
                    S.add("tensor", mms,
                          reads=[grg(A_QW, A_QW + 2048), grg(A_KW, A_KW + 4096)],
                          writes=[("PS", (0, nkb * 512))], tag=f"S{l}_{n}_{h}_{qs}")
                    if w > 0:
                        S.add("vector", lambda e, qs=qs: e.tensor_tensor(
                            bank(C, 0), bank(C, 0), msks[:, qs * 512:(qs + 1) * 512], ALU.add),
                            reads=[("PS", (0, 512)), ("l0f_msk", None)],
                            writes=[("PS", (0, 512))], tag=f"mk0{l}_{n}_{h}_{qs}")
                    S.add("vector", lambda e, qs=qs, nkb=nkb: e.tensor_tensor(
                        bank(C, nkb - 1), bank(C, nkb - 1),
                        msks[:, (4 + qs) * 512:(5 + qs) * 512], ALU.add),
                        reads=[("PS", ((nkb - 1) * 512, nkb * 512)), ("l0f_msk", None)],
                        writes=[("PS", ((nkb - 1) * 512, nkb * 512))], tag=f"mk1{l}_{n}_{h}_{qs}")
                    S.add("vector", lambda e, nkb=nkb: e.reduce_max(
                        mst[:, 0:1], C.PS[:, 0:nkb * 512], axis=AX.X, negate=True),
                        reads=[("PS", (0, nkb * 512))], writes=[("l0f_mst", (0, 1))],
                        tag=f"nm{l}_{n}_{h}_{qs}")
                    S.add("scalar", lambda e, nkb=nkb: e.activation(
                        gap(A_PN, A_PN + nkb * 512), C.PS[:, 0:nkb * 512], AF.Exp,
                        bias=mst[:, 0:1], accum_out=mst[:, 1:2]),
                        reads=[("PS", (0, nkb * 512)), ("l0f_mst", (0, 1))],
                        writes=[grg(A_PN, A_PN + 1024), ("l0f_mst", (1, 2))],
                        tag=f"ex{l}_{n}_{h}_{qs}")
                    S.add("vector", lambda e: e.reciprocal(mst[:, 2:3], mst[:, 1:2]),
                          reads=[("l0f_mst", (1, 2))], writes=[("l0f_mst", (2, 3))],
                          tag=f"ri{l}_{n}_{h}_{qs}")
                    S.add("vector", lambda e, nkb=nkb: e.tensor_scalar_mul(
                        gap(A_PN, A_PN + nkb * 512), gap(A_PN, A_PN + nkb * 512), mst[:, 2:3]),
                        reads=[grg(A_PN, A_PN + 1024), ("l0f_mst", (2, 3))],
                        writes=[grg(A_PN, A_PN + 1024)], tag=f"nr{l}_{n}_{h}_{qs}")
                    if C.debug and l == 0 and n == 0 and h == 2 and qs == 0:
                        C.tap("pnb2", gap(A_PN, A_PN + 512), [grg(A_PN, A_PN + 1024)])
                    if C.debug and l == 0 and n == 0 and h == 2 and qs == 0:
                        C.tap("mst2", mst[:, 0:4], [("l0f_mst", (0, 4))])
                    def ptr(e, nkb=nkb):
                        ins = None
                        for sc in range(nkb * 4):
                            ins = e.transpose(psb[:, 512 + sc * 0:512] if False else
                                              psb[:, sc * 128 - sc * 128:128] if False else
                                              psb[:, 0:128] if False else
                                              psb[:, (sc % 4) * 128:(sc % 4 + 1) * 128],
                                              gap(A_PN + sc * 128, A_PN + (sc + 1) * 128),
                                              C.ident_bf)
                        return ins
                    for tg in range((nkb * 4 + 3) // 4):
                        ns = min(4, nkb * 4 - tg * 4)
                        def ptr2(e, tg=tg, ns=ns):
                            ins = None
                            for j in range(ns):
                                sc = tg * 4 + j
                                ins = e.transpose(psb[:, j * 128:(j + 1) * 128],
                                                  gap(A_PN + sc * 128, A_PN + (sc + 1) * 128),
                                                  C.ident_bf)
                            return ins
                        S.add("tensor", ptr2,
                              reads=[grg(A_PN, A_PN + 1024), ("CONSTB", (128, 256))],
                              writes=[("PS", (5 * 512, 6 * 512))], tag=f"pt{l}_{n}_{h}_{qs}_{tg}")
                        def pcp(e, tg=tg, ns=ns):
                            ins = None
                            for j in range(ns):
                                sc = tg * 4 + j
                                ins = e.tensor_copy(gap(A_PTB + sc * 128, A_PTB + (sc + 1) * 128),
                                                    psb[:, j * 128:(j + 1) * 128])
                            return ins
                        S.add("vector", pcp, reads=[("PS", (5 * 512, 6 * 512))],
                              writes=[grg(A_PTB + tg * 512, A_PTB + (tg * 4 + ns) * 128)],
                              tag=f"pc{l}_{n}_{h}_{qs}_{tg}")
                    def mmv(e, nkb=nkb):
                        ins = None
                        for sc in range(nkb * 4):
                            ins = e.matmul(C.PS[0:64, 4 * 512:4 * 512 + 128],
                                           gap(A_VT + sc * 64, A_VT + (sc + 1) * 64),
                                           gap(A_PTB + sc * 128, A_PTB + (sc + 1) * 128),
                                           start=(sc == 0), stop=(sc == nkb * 4 - 1))
                        return ins
                    S.add("tensor", mmv,
                          reads=[grg(A_VT, A_VT + 512), grg(A_PTB, A_PTB + 1024)],
                          writes=[("PS", (4 * 512, 5 * 512))], tag=f"pv{l}_{n}_{h}_{qs}")
                    S.add("vector", lambda e, qs=qs, hc=hc, ho=ho: e.tensor_copy(
                        gap(A_OFM + hc * 512 + qs * 128,
                            A_OFM + hc * 512 + (qs + 1) * 128)[ho:ho + 64, :],
                        C.PS[0:64, 4 * 512:4 * 512 + 128]),
                        reads=[("PS", (4 * 512, 5 * 512))],
                        writes=[grg(A_OFM + hc * 512, A_OFM + (hc + 1) * 512)],
                        tag=f"oe{l}_{n}_{h}_{qs}")
            S.add("sync", lambda e, l=l: e.dma_start(
                out=gap(A_WOS, A_WOS + 4096).rearrange("p (c f) -> p c f", c=4),
                in_=wo_scr[l].rearrange("(c p) f -> p c f", p=128)),
                reads=[(f"l0f_woscr{l}", (0, 4))], writes=[grg(A_WOS, A_WOS + 4096)],
                tag=f"wol{l}_{n}")
            for mo in range(DC):
                def mmw(e, mo=mo):
                    ins = None
                    for c in range(4):
                        ins = e.matmul(bank(C, mo),
                                       gap(A_WOS + c * 1024 + mo * 128,
                                           A_WOS + c * 1024 + (mo + 1) * 128),
                                       gap(A_OFM + c * 512, A_OFM + (c + 1) * 512),
                                       start=(c == 0), stop=(c == 3))
                    return ins
                S.add("tensor", mmw,
                      reads=[grg(A_WOS, A_WOS + 4096), grg(A_OFM, A_OFM + 2048)],
                      writes=[("PS", (mo * 512, (mo + 1) * 512))], tag=f"wo{l}_{n}_{mo}")
            for mo in range(DC):
                S.add("vector", lambda e, mo=mo: e.tensor_tensor(
                    gap(A_XI + mo * 512, A_XI + (mo + 1) * 512),
                    gap(A_XI + mo * 512, A_XI + (mo + 1) * 512), bank(C, mo), ALU.add),
                    reads=[grg(A_XI + mo * 512, A_XI + (mo + 1) * 512),
                           ("PS", (mo * 512, (mo + 1) * 512))],
                    writes=[grg(A_XI + mo * 512, A_XI + (mo + 1) * 512)],
                    tag=f"xr{l}_{n}_{mo}")
            if C.debug and l == 0 and n == 0:
                C.tap("xiattn0", gap(A_XI, A_XI + 4096), [grg(A_XI, A_XI + 4096)])
                C.tap("ofm0", gap(A_OFM, A_OFM + 2048), [grg(A_OFM, A_OFM + 2048)])
            for mo in range(DC):
                if mo < 4:
                    S.add("vector", lambda e, mo=mo: e.tensor_copy(
                        gap(A_NST + mo * 512, A_NST + (mo + 1) * 512),
                        gap(A_XI + mo * 512, A_XI + (mo + 1) * 512)),
                        reads=[grg(A_XI + mo * 512, A_XI + (mo + 1) * 512)],
                        writes=[grg(A_NST + mo * 512, A_NST + (mo + 1) * 512)],
                        tag=f"sh{l}_{n}_{mo}")
                else:
                    S.add("vector", lambda e, mo=mo: e.tensor_copy(
                        gap(A_NST + mo * 512 + 1, A_NST + (mo + 1) * 512),
                        gap(A_XI + mo * 512, A_XI + (mo + 1) * 512 - 1)),
                        reads=[grg(A_XI + mo * 512, A_XI + (mo + 1) * 512)],
                        writes=[grg(A_NST + mo * 512, A_NST + (mo + 1) * 512)],
                        tag=f"sh{l}_{n}_{mo}")
                    S.add("vector", lambda e, mo=mo: e.tensor_copy(
                        gap(A_NST + mo * 512, A_NST + mo * 512 + 1),
                        xc2[:, mo - 4:mo - 3]),
                        reads=[("l0f_xc2", (mo - 4, mo - 3))],
                        writes=[grg(A_NST + mo * 512, A_NST + mo * 512 + 1)],
                        tag=f"shc{l}_{n}_{mo}")
            for mo in range(4, 8):
                S.add("vector", lambda e, mo=mo: e.tensor_copy(
                    xc2[:, mo - 4:mo - 3], gap(A_XI + (mo + 1) * 512 - 1, A_XI + (mo + 1) * 512)),
                    reads=[grg(A_XI + (mo + 1) * 512 - 1, A_XI + (mo + 1) * 512)],
                    writes=[("l0f_xc2", (mo - 4, mo - 3))], tag=f"cs{l}_{n}_{mo}")
            lg = [sml[:, 2 + k:3 + k] for k in range(8)]
            lb = [sml[:, 10 + k:11 + k] for k in range(8)]
            def src(k):
                return (gap(A_NST + k * 512, A_NST + (k + 1) * 512),
                        [grg(A_NST + k * 512, A_NST + (k + 1) * 512)])
            def nout(k):
                return (gap(A_NST + k * 512, A_NST + (k + 1) * 512),
                        grg(A_NST + k * 512, A_NST + (k + 1) * 512))
            emit_ln_core(C, src, nout, gamma=lg, beta=lb,
                         gb_reads=[("l0f_sm", (2, 18))], tag=f"f{l}n{n}_")
            if C.debug and l == 0 and n == 0:
                C.tap("nst0", gap(A_NST, A_NST + 4096), [grg(A_NST, A_NST + 4096)])
            for j in range(22):
                jw = 128 if j < 21 else 42
                S.add("sync", lambda e, j=j, l=l, jw=jw: e.dma_start(
                    out=gap(A_WCL, A_WCL + 8 * jw).rearrange("p (c f) -> p c f", c=8),
                    in_=w1a_scr[l][:, j * 128:j * 128 + jw].rearrange("(c p) f -> p c f", p=128)),
                    reads=[(f"l0f_w1ascr{l}", (0, 8))], writes=[grg(A_WCL, A_WCL + 1024)],
                    tag=f"wa{l}_{n}_{j}")
                def mma(e, jw=jw):
                    ins = None
                    for k in range(DC):
                        ins = e.matmul(C.PS[0:jw, 0:512], gap(A_WCL + k * jw, A_WCL + (k + 1) * jw),
                                       gap(A_NST + k * 512, A_NST + (k + 1) * 512),
                                       start=(k == 0), stop=(k == DC - 1))
                    return ins
                S.add("tensor", mma, reads=[grg(A_WCL, A_WCL + 1024), grg(A_NST, A_NST + 4096)],
                      writes=[("PS", (0, 512))], tag=f"ma{l}_{n}_{j}")
                S.add("vector", lambda e, jw=jw, j=j: e.tensor_scalar_add(
                    gap(A_AST, A_AST + 512)[0:jw, :], C.PS[0:jw, 0:512],
                    sml[0:jw, 18 + j:19 + j]),
                      reads=[("PS", (0, 512)), ("l0f_sm", (18, 40))],
                      writes=[grg(A_AST, A_AST + 512)],
                      tag=f"ae{l}_{n}_{j}")
                S.add("sync", lambda e, j=j, l=l, jw=jw: e.dma_start(
                    out=gap(A_WCL, A_WCL + 8 * jw).rearrange("p (c f) -> p c f", c=8),
                    in_=w1g_scr[l][:, j * 128:j * 128 + jw].rearrange("(c p) f -> p c f", p=128)),
                    reads=[(f"l0f_w1gscr{l}", (0, 8))], writes=[grg(A_WCL, A_WCL + 1024)],
                    tag=f"wg{l}_{n}_{j}")
                S.add("tensor", mma, reads=[grg(A_WCL, A_WCL + 1024), grg(A_NST, A_NST + 4096)],
                      writes=[("PS", (0, 512))], tag=f"mg{l}_{n}_{j}")
                vslot = A_VOG + (j % 4) * 512
                S.add("scalar", lambda e, jw=jw, vslot=vslot, j=j: e.activation(
                    gap(vslot, vslot + 512)[0:jw, :], C.PS[0:jw, 0:512], AF.Gelu,
                    bias=sml[0:jw, 40 + j:41 + j]),
                    reads=[("PS", (0, 512)), ("l0f_sm", (40, 62))],
                    writes=[grg(vslot, vslot + 512)],
                    tag=f"gg{l}_{n}_{j}")
                S.add("vector", lambda e, jw=jw, vslot=vslot: e.tensor_tensor(
                    gap(vslot, vslot + 512)[0:jw, :], gap(A_AST, A_AST + 512)[0:jw, :],
                    gap(vslot, vslot + 512)[0:jw, :], ALU.mult),
                    reads=[grg(A_AST, A_AST + 512), grg(vslot, vslot + 512)],
                    writes=[grg(vslot, vslot + 512)], tag=f"vm{l}_{n}_{j}")
                if j % 4 == 3 and j < 20:
                    j0 = (j // 4) * 4
                    S.add("sync", lambda e, j0=j0, n=n: e.dma_start(
                        out=vf_scr[j0 * 128:(j0 + 4) * 128, n * 512:(n + 1) * 512].rearrange(
                            "(c p) f -> p c f", p=128),
                        in_=gap(A_VOG, A_VOG + 4 * 512).rearrange("p (c f) -> p c f", c=4)),
                        reads=[grg(A_VOG, A_VOG + 2048)],
                        writes=[("l0f_vfscr", (j0, j0 + 4))], tag=f"vf{l}_{n}_{j}")
                elif j == 21:
                    S.add("sync", lambda e, n=n: e.dma_start(
                        out=vf_scr[20 * 128:21 * 128, n * 512:(n + 1) * 512],
                        in_=gap(A_VOG, A_VOG + 512)),
                        reads=[grg(A_VOG, A_VOG + 2048)],
                        writes=[("l0f_vfscr", (20, 21))], tag=f"vf{l}_{n}_20")
                    S.add("sync", lambda e, n=n: e.dma_start(
                        out=vf_scr[21 * 128:21 * 128 + 42, n * 512:(n + 1) * 512],
                        in_=gap(A_VOG + 512, A_VOG + 1024)[0:42, :]),
                        reads=[grg(A_VOG, A_VOG + 2048)],
                        writes=[("l0f_vfscr", (21, 22))], tag=f"vf{l}_{n}_21")
            for j in range(22):
                jw = 128 if j < 21 else 42
                S.add("sync", lambda e, j=j, l=l, jw=jw: e.dma_start(
                    out=gap(A_W2C, A_W2C + 1024)[0:jw, :],
                    in_=w2_scr[l][j * 128:j * 128 + jw, :]),
                    reads=[(f"l0f_w2scr{l}", (j, j + 1))], writes=[grg(A_W2C, A_W2C + 1024)],
                    tag=f"w2l{l}_{n}_{j}")
                S.add("sync", lambda e, j=j, n=n, jw=jw: e.dma_start(
                    out=gap(A_VIG, A_VIG + 512)[0:jw, :],
                    in_=vf_scr[j * 128:j * 128 + jw, n * 512:(n + 1) * 512]),
                    reads=[("l0f_vfscr", (j, j + 1))], writes=[grg(A_VIG, A_VIG + 512)],
                    tag=f"v2l{l}_{n}_{j}")
                def mm2(e, j=j, jw=jw):
                    ins = None
                    for mo in range(DC):
                        ins = e.matmul(bank(C, mo),
                                       gap(A_W2C + mo * 128, A_W2C + (mo + 1) * 128)[0:jw, :],
                                       gap(A_VIG, A_VIG + 512)[0:jw, :],
                                       start=(j == 0), stop=(j == 21))
                    return ins
                S.add("tensor", mm2,
                      reads=[grg(A_W2C, A_W2C + 1024), grg(A_VIG, A_VIG + 512)],
                      writes=[("PS", (0, 4096))], tag=f"m2{l}_{n}_{j}")
            for mo in range(DC):
                S.add("vector", lambda e, mo=mo: e.scalar_tensor_tensor(
                    gap(A_XI + mo * 512, A_XI + (mo + 1) * 512), bank(C, mo),
                    sml[:, 62 + mo:63 + mo],
                    gap(A_XI + mo * 512, A_XI + (mo + 1) * 512), ALU.add, ALU.add),
                    reads=[grg(A_XI + mo * 512, A_XI + (mo + 1) * 512),
                           ("PS", (mo * 512, (mo + 1) * 512)), ("l0f_sm", (62, 70))],
                    writes=[grg(A_XI + mo * 512, A_XI + (mo + 1) * 512)],
                    tag=f"x2{l}_{n}_{mo}")
            S.add("sync", lambda e, n=n: e.dma_start(
                out=xint[:, 1 + n * 512:1 + (n + 1) * 512].rearrange("(c p) f -> p c f", p=128),
                in_=gap(A_XI, A_XI + 4096).rearrange("p (c f) -> p c f", c=8)),
                reads=[grg(A_XI, A_XI + 4096)], writes=[("l0f_xintw", (n, n + 1)),
                                                        ("l0f_xints", (0, 8))],
                tag=f"xw{l}_{n}")

    if L0F_LIMIT == "a0":
        pass
    C.tap("l0f_q", q_scr[:, :], [("l0f_qscr", None)])
    C.tap("l0f_k", k_scr[:, :], [("l0f_kscr", None)])
    C.tap("l0f_v", v_scr[:, :], [("l0f_vscr", None)])
    C.tap("l0f_xint", xint[:, :], [("l0f_xintw", None), ("l0f_xints", None)])
    C.tap("l0f_vf", vf_scr[0:128, :], [("l0f_vfscr", None)])
    for n in range(NT):
        tsl = (PAD + n * TTW, PAD + (n + 1) * TTW)
        S.add("sync", lambda e, n=n: e.dma_start(
            out=gap(A_XI, A_XI + 4096).rearrange("p (c f) -> p c f", c=8),
            in_=xint[:, 1 + n * 512:1 + (n + 1) * 512].rearrange("(c p) f -> p c f", p=128)),
            reads=[("l0f_xintw", (0, 100)), ("l0f_xints", (0, 8))],
            writes=[grg(A_XI, A_XI + 4096)], tag=f"fxl{n}")
        for mo in range(DC):
            S.add("vector", lambda e, mo=mo, n=n: e.tensor_tensor(
                rb[:, mo * 512:(mo + 1) * 512],
                C.xres[mo][:, PAD + n * TTW:PAD + (n + 1) * TTW],
                gap(A_XI + mo * 512, A_XI + (mo + 1) * 512), ALU.add),
                reads=[(f"xres{mo}", tsl), grg(A_XI + mo * 512, A_XI + (mo + 1) * 512)],
                writes=[(K_RB, (mo * 512, (mo + 1) * 512))], tag=f"fres{n}_{mo}")
        emit_postln_tile(C, n, rb, K_RB)



EMITTERS = {
    "l0_attn": emit_l0attn,
    "l0_ffn": emit_l0ffn,
    "l1_attn": emit_l1attn,
    "l1_ffn": emit_l1ffn,
    "l2_attn": emit_l2attn,
    "l2_ffn": emit_l2ffn,
    "l3_attn": emit_l3attn,
    "l3_ffn": emit_l3ffn,
}

BLOCK_ORDER = ["l0_attn", "l0_ffn", "l1_attn", "l1_ffn",
               "l2_attn", "l2_ffn", "l3_attn", "l3_ffn"]
FUSED = True


GA_COLS_DEFAULT = 68096


def build_program(blocks, first, last, debug=False, arena=False):
    nc = bass.Bass("TRN2", target_bir_lowering=False)
    stack = contextlib.ExitStack()
    with stack:
        C = Ctx(nc, stack)
        C.debug = debug
        with contextlib.ExitStack() as semstack:
            sems = {e: semstack.enter_context(nc.semaphore(f"sem_{e}")) for e in ENGINES}
            blk = semstack.enter_context(nc.Block())
            setup_common(C, first)
            if arena:
                C.GA_COLS = GA_COLS_DEFAULT
                C.GA = C.sbuf("GA", [128, GA_COLS_DEFAULT], BF16)
                C.use_arena = True
                C.S.translate = C._translate
            for b in blocks:
                C.arena_reset()
                EMITTERS[b](C)
            finish_common(C, last)
            C.S.emit(blk, sems)
    fv = np.zeros((128, 64), dtype=np.float32)
    fv[:, :len(C.fvals)] = np.asarray(C.fvals, dtype=np.float32)[None, :]
    return nc, C.inputs, fv


def kernel(**inputs):
    n_cores = 8
    x_host = np.asarray(inputs["L0_attn_hidden_states"], dtype=np.float32)
    xs = [np.ascontiguousarray(x_host[c].T) for c in range(n_cores)]
    prepped = {}

    def get_prep(name):
        if name not in prepped:
            prepped[name] = np.ascontiguousarray(PREP[name](inputs))
        return prepped[name]

    blocks = [b for b in BLOCK_ORDER if b in EMITTERS]
    if FUSED:
        nc, innames, fv = build_program(blocks, True, True, arena=True)
        in_maps = []
        for c in range(n_cores):
            m = {}
            for nm in innames:
                if nm == "x_in":
                    m[nm] = xs[c]
                elif nm == "c_fvals":
                    m[nm] = fv
                else:
                    m[nm] = get_prep(nm)
            in_maps.append(m)
        res = run_bass_kernel_spmd(nc, in_maps, core_ids=list(range(n_cores)))
        outs = [np.asarray(r["x_out"]) for r in res.results]
        return np.stack([o.T for o in outs]).astype(np.float32)
    outs = xs
    for i, b in enumerate(blocks):
        first = (i == 0)
        last = (i == len(blocks) - 1)
        nc, innames, fv = build_program([b], first, last)
        in_maps = []
        for c in range(n_cores):
            m = {}
            for nm in innames:
                if nm == "x_in":
                    m[nm] = outs[c]
                elif nm == "c_fvals":
                    m[nm] = fv
                else:
                    m[nm] = get_prep(nm)
            in_maps.append(m)
        res = run_bass_kernel_spmd(nc, in_maps, core_ids=list(range(n_cores)))
        outs = [np.asarray(r["x_out"]) for r in res.results]
    return np.stack([o.T for o in outs]).astype(np.float32)
